# version 15
# baseline (speedup 1.0000x reference)
import contextlib
import math
import numpy as np
import concourse.bass as bass
import concourse.mybir as mybir
import concourse.bass_utils as _bu
from concourse.bass_utils import run_bass_kernel_spmd

if not getattr(_bu, "_ldw_patched", False):
    _orig_run_command = _bu.run_command

    def _run_command_ldw(cmd, *a, **kw):
        if isinstance(cmd, (list, tuple)):
            cmd = [c.replace("--enable-ldw-opt=false", "--enable-ldw-opt=true")
                   if isinstance(c, str) else c for c in cmd]
        return _orig_run_command(cmd, *a, **kw)

    _bu.run_command = _run_command_ldw
    _bu._ldw_patched = True

F32 = mybir.dt.float32
BF16 = mybir.dt.bfloat16
AF = mybir.ActivationFunctionType
ALU = mybir.AluOpType
AX = mybir.AxisListType

PI = math.pi
LN_THETA = math.log(10000.0)


class CFG:
    def __init__(self, D=4096, L=512, H=32, T=1024):
        self.D, self.L, self.H, self.T = D, L, H, T
        self.S = 4 * T
        self.E, self.R = 8, 16
        self.ER = self.E * self.R
        self.DT = D // 128
        self.LT = L // 128
        self.TC = T // 512
        self.T8 = T // 128
        self.KT = self.S // 128
        self.KC = self.S // 512
        self.G = 4
        self.DOC = 256
        self.NDC = D // self.DOC
        self.SM = 9 + self.ER


def build_program(cfg: CFG):
    D, L, H, T, S = cfg.D, cfg.L, cfg.H, cfg.T, cfg.S
    DT, LT, TC, T8, KT, KC = cfg.DT, cfg.LT, cfg.TC, cfg.T8, cfg.KT, cfg.KC
    ER, DOC, NDC, G, SM = cfg.ER, cfg.DOC, cfg.NDC, cfg.G, cfg.SM
    SQ = 1.0 / math.sqrt(128.0)
    NVU = KT // 4
    PH3 = 1 + KC + NVU

    nc = bass.Bass("TRN2", target_bir_lowering=False, debug=False)

    x_own = nc.dram_tensor("x_own", [T, D], F32, kind="ExternalInput").ap()
    wq = nc.dram_tensor("wq", [H // 2, 128, DT * 256], F32, kind="ExternalInput").ap()
    wc = nc.dram_tensor("wc", [D, L], F32, kind="ExternalInput").ap()
    wk = nc.dram_tensor("wk", [H, 128, LT * 128], F32, kind="ExternalInput").ap()
    wv = nc.dram_tensor("wv", [H, 128, LT * 128], F32, kind="ExternalInput").ap()
    wo = nc.dram_tensor("wo", [NDC, 128, DT * DOC], F32, kind="ExternalInput").ap()
    wbt = nc.dram_tensor("wbt", [NDC, 128, DT * DOC], F32, kind="ExternalInput").ap()
    wsm = nc.dram_tensor("wsm", [128, DT * SM], F32, kind="ExternalInput").ap()
    bflat = nc.dram_tensor("bflat", [ER, D], F32, kind="ExternalInput").ap()
    bbase = nc.dram_tensor("bbase", [1, D], F32, kind="ExternalInput").ap()
    bthr = nc.dram_tensor("bthr", [128, 1], F32, kind="ExternalInput").ap()
    posq = nc.dram_tensor("posq", [128, T], F32, kind="ExternalInput").ap()
    posk = nc.dram_tensor("posk", [128, S], F32, kind="ExternalInput").ap()
    dind = nc.dram_tensor("dind", [128, 1], F32, kind="ExternalInput").ap()
    ident_in = nc.dram_tensor("ident", [128, 128], F32, kind="ExternalInput").ap()
    out = nc.dram_tensor("out", [T, D], F32, kind="ExternalOutput").ap()

    cc_in = nc.dram_tensor("cc_in", [LT, 128, T], BF16)
    cc_out = nc.dram_tensor("cc_out", [G, LT, 128, T], BF16)
    ffn_hbm = nc.dram_tensor("ffn_hbm", [T, D], BF16)

    es = contextlib.ExitStack()

    with es:
        def sb(name, shape, dt, stack=es):
            return stack.enter_context(nc.sbuf_tensor(name, shape, dt))

        def psm(name, shape, dt, stack):
            return stack.enter_context(nc.psum_tensor(name, shape, dt))

        ones_col = sb("ones_col", [1, 128], BF16)
        ones128f = sb("ones128f", [128, 1], F32)
        ones1f = sb("ones1f", [1, 128], F32)
        Tck = sb("Tck", [128, S], BF16)
        Tsk = sb("Tsk", [128, S], BF16)
        Tcq = sb("Tcq", [128, T], BF16)
        Tsq = sb("Tsq", [128, T], BF16)
        QA = sb("QA", [128, H * T], BF16)
        hwT = sb("hwT", [128, T], BF16)
        ident_s = sb("ident_s", [128, 128], BF16)
        bthr_s = sb("bthr_s", [128, 1], F32)

        sx = es.enter_context(nc.semaphore("sx"))
        sw = es.enter_context(nc.semaphore("sw"))
        scc = es.enter_context(nc.semaphore("scc"))
        st = es.enter_context(nc.semaphore("st"))
        stk = es.enter_context(nc.semaphore("stk"))
        sts = es.enter_context(nc.semaphore("sts"))
        stp = es.enter_context(nc.semaphore("stp"))
        se = es.enter_context(nc.semaphore("se"))
        sl = es.enter_context(nc.semaphore("sl"))
        sv = es.enter_context(nc.semaphore("sv"))
        ss = es.enter_context(nc.semaphore("ss"))
        so = es.enter_context(nc.semaphore("so"))
        sm0 = es.enter_context(nc.semaphore("sm0"))
        sf0 = es.enter_context(nc.semaphore("sf0"))
        sf1 = es.enter_context(nc.semaphore("sf1"))
        so1 = es.enter_context(nc.semaphore("so1"))
        svc = es.enter_context(nc.semaphore("svc"))
        sb2 = es.enter_context(nc.semaphore("sb2"))
        svk = es.enter_context(nc.semaphore("svk"))
        VC = [0]

        def vchain(v, ins):
            ins.then_inc(svc, 1)
            VC[0] += 1

        def vjoin(v):
            v.wait_ge(svc, VC[0])

        blk = es.enter_context(nc.Block())

        SX = SW = SCC = ST = SV = SS = 0

        c1 = es.enter_context(contextlib.ExitStack())
        xnT = sb("xnT", [128, DT * T], BF16, c1)

        c0 = c1.enter_context(contextlib.ExitStack())
        th = sb("th", [128, S], F32, c0)
        pq_s = sb("pq_s", [128, T], F32, c0)
        tk0 = sb("tk0", [128, S], F32, c0)
        ck0 = sb("ck0", [128, S], F32, c0)
        invf = sb("invf", [128, 1], F32, c0)

        @blk.sync
        def _(sy):
            sy.dma_start(out=th[:], in_=posk).then_inc(sx, 16)
            sy.dma_start(out=pq_s[:], in_=posq).then_inc(sx, 16)
            sy.dma_start(out=invf[:], in_=dind).then_inc(sx, 16)
            sy.dma_start(out=bthr_s[:], in_=bthr).then_inc(sx, 16)
        SX += 64

        @blk.gpsimd
        def _(gp):
            gp.dma_start(out=ident_s[:], in_=ident_in).then_inc(sw, 16)
            gp.memset(ones_col[:], 1.0)
            gp.memset(ones128f[:], 1.0)
            gp.memset(ones1f[:], 1.0).then_inc(sm0, 1)
        SW += 16

        @blk.scalar
        def _(s):
            s.wait_ge(sx, 64)
            s.activation(invf[:], invf[:], AF.Exp, scale=-LN_THETA / 64.0).then_inc(ss, 1)
        SS += 1

        def emit_wrap(v, pos_sl, tk, ck):
            vchain(v, v.tensor_scalar(tk, pos_sl, 1.0 / (2 * PI), None, ALU.mult))
            for b in [512.0, 256.0, 128.0, 64.0, 32.0, 16.0, 8.0, 4.0, 2.0, 1.0]:
                vjoin(v)
                vchain(v, v.tensor_scalar(ck, tk, b, None, ALU.is_ge))
                vjoin(v)
                vchain(v, v.scalar_tensor_tensor(tk, ck, -b, tk, ALU.mult, ALU.add))
            vjoin(v)
            v.tensor_scalar(tk, tk, 2 * PI, -PI, ALU.mult, ALU.add).then_inc(sv, 1)

        def emit_tables(pos_ap, width, Tc, Ts, fold, SV, SS):
            ss0, sv0 = SS, SV
            tk, ck = tk0[:, :width], ck0[:, :width]

            @blk.vector
            def _(v):
                v.wait_ge(ss, ss0)
                vjoin(v)
                vchain(v, v.tensor_scalar(pos_ap, pos_ap, invf[:, 0:1], None, ALU.mult))
                vjoin(v)
                emit_wrap(v, pos_ap, tk, ck)

            @blk.scalar
            def _(s):
                s.wait_ge(sv, sv0 + 1)
                s.activation(Ts[:, :width], tk, AF.Sin, scale=1.0).then_inc(ss, 1)

            @blk.vector
            def _(v):
                v.wait_ge(ss, ss0 + 1)
                v.tensor_scalar(Ts[0:64, :width], Ts[0:64, :width], fold, None, ALU.mult)
                v.tensor_scalar(Ts[64:128, :width], Ts[64:128, :width], -fold, None, ALU.mult)
                vjoin(v)
                vchain(v, v.tensor_scalar(pos_ap, pos_ap, PI / 2, None, ALU.add))
                vjoin(v)
                emit_wrap(v, pos_ap, tk, ck)

            @blk.scalar
            def _(s):
                s.wait_ge(sv, sv0 + 2)
                s.activation(Tc[:, :width], tk, AF.Sin, scale=1.0).then_inc(ss, 1)

            @blk.vector
            def _(v):
                v.wait_ge(ss, ss0 + 2)
                v.tensor_scalar(Tc[:, :width], Tc[:, :width], -fold, None, ALU.mult).then_inc(sv, 1)
            return sv0 + 3, ss0 + 2

        SV, SS = emit_tables(th[:], S, Tck, Tsk, 1.0, SV, SS)
        SV, SS = emit_tables(pq_s[:], T, Tcq, Tsq, SQ, SV, SS)
        sv_tab, ss_tab = SV, SS

        c0.close()
        c1p = c1.enter_context(contextlib.ExitStack())
        xst = sb("xst", [128, 2 * D], F32, c1p)
        xn_tm = sb("xn_tm", [128, 2 * D], BF16, c1p)
        stats = sb("stats", [128, 10], F32, c1p)
        ps_tp = [psm(f"ps_tp{i}", [128, 512], BF16, c1p) for i in range(2)]

        NQ4 = DT // 4
        sx_p1, ss_p1, sv_p1, st_p1 = SX, SS, SV, ST

        @blk.sync
        def _(sy):
            for i in range(T8):
                if i == 0:
                    sy.wait_ge(ss, ss_tab)
                if i >= 2:
                    sy.wait_ge(ss, ss_p1 + 3 * (i - 1))
                sy.dma_start(out=xst[:, (i % 2) * D:(i % 2 + 1) * D],
                             in_=x_own[i * 128:(i + 1) * 128, :]).then_inc(sx, 16)
        SX += 16 * T8

        PCH = 2 + NQ4

        def _evac_val(k):
            i, q4 = k // NQ4, k % NQ4
            if i + 1 < T8:
                return sv_p1 + 2 * (i + 2) + i * NQ4 + q4 + 1
            return sv_p1 + 2 * T8 + (T8 - 1) * NQ4 + q4 + 1

        def _emit_evac(v, i, q4):
            k = i * NQ4 + q4
            v.wait_ge(st, st_p1 + k + 1)
            dst = xnT[:].rearrange("p (d t) -> p d t", d=DT, t=T)[
                :, q4 * 4:(q4 + 1) * 4, i * 128:(i + 1) * 128]
            v.tensor_copy(dst, ps_tp[k % 2][:].rearrange(
                "p (d c) -> p d c", d=4, c=128)).then_inc(sv, 1)

        @blk.vector
        def _(v):
            for i in range(T8):
                o = (i % 2) * 5
                mean_, msq, var_, rstd, nb = (stats[:, o + j:o + j + 1] for j in range(5))
                xb = xst[:, (i % 2) * D:(i % 2 + 1) * D]
                v.wait_ge(sx, sx_p1 + 16 * min(i + 2, T8))
                if i >= 2:
                    v.wait_ge(ss, ss_p1 + 3 * (i - 1))
                v.wait_ge(ss, ss_p1 + 3 * i + 1)
                vjoin(v)
                vchain(v, v.tensor_scalar(xn_tm[:, (i % 2) * D:(i % 2 + 1) * D], xb,
                                          1.0 / D, None, ALU.mult, ALU.add, accum_out=mean_))
                vjoin(v)
                vchain(v, v.tensor_mul(var_, mean_, mean_))
                vjoin(v)
                vchain(v, v.tensor_sub(var_, msq, var_))
                vjoin(v)
                vchain(v, v.tensor_scalar(var_, var_, 1e-5, None, ALU.add))
                vjoin(v)
                v.reciprocal(var_, var_).then_inc(sv, 1)
                v.wait_ge(ss, ss_p1 + 3 * i + 2)
                v.scalar_tensor_tensor(nb, mean_, -1.0, rstd, ALU.mult, ALU.mult).then_inc(sv, 1)
                if i >= 1:
                    for q4 in range(NQ4):
                        _emit_evac(v, i - 1, q4)
            for q4 in range(NQ4):
                _emit_evac(v, T8 - 1, q4)
        SV += T8 * PCH

        import math as _math
        RSD = 1.0 / _math.sqrt(D)

        @blk.scalar
        def _(s):
            for i in range(T8):
                o = (i % 2) * 5
                msq_, var_, rstd, nb = (stats[:, o + j:o + j + 1] for j in range(1, 5))
                s.wait_ge(sx, sx_p1 + 16 * min(i + 2, T8))
                if i >= 2:
                    s.wait_ge(sv, sv_p1 + 2 * (i - 2) + max(i - 3, 0) * NQ4 + 2)
                    s.wait_ge(st, st_p1 + (i - 1) * NQ4)
                s.activation(xn_tm[:, (i % 2) * D:(i % 2 + 1) * D],
                             xst[:, (i % 2) * D:(i % 2 + 1) * D], AF.Square,
                             scale=RSD, accum_out=msq_).then_inc(ss, 1)
                s.wait_ge(sv, sv_p1 + 2 * i + max(i - 1, 0) * NQ4 + 1)
                s.activation(rstd, var_, AF.Sqrt).then_inc(ss, 1)
                s.wait_ge(sv, sv_p1 + 2 * i + max(i - 1, 0) * NQ4 + 2)
                if i >= 2:
                    s.wait_ge(st, st_p1 + (i - 1) * NQ4)
                s.activation(xn_tm[:, (i % 2) * D:(i % 2 + 1) * D],
                             xst[:, (i % 2) * D:(i % 2 + 1) * D], AF.Identity,
                             bias=nb, scale=rstd).then_inc(ss, 1)
        SS += 3 * T8

        @blk.tensor
        def _(t):
            t.wait_ge(sw, 16)
            t.wait_ge(sm0, 1)
            for i in range(T8):
                t.wait_ge(ss, ss_p1 + 3 * i + 3)
                xb = xn_tm[:, (i % 2) * D:(i % 2 + 1) * D]
                for q4 in range(NQ4):
                    k = i * NQ4 + q4
                    pst = ps_tp[k % 2]
                    if k >= 2:
                        t.wait_ge(sv, _evac_val(k - 2))
                    ins = None
                    for j in range(4):
                        dt = q4 * 4 + j
                        ins = t.transpose(pst[:, j * 128:(j + 1) * 128],
                                          xb[:, dt * 128:(dt + 1) * 128], ident_s[:])
                    ins.then_inc(st, 1)
        ST += T8 * NQ4
        sv_xnT = SV

        c1p.close()

        c1b = c1.enter_context(contextlib.ExitStack())
        wc_sb = sb("wc_sb", [128, 2 * L], BF16, c1b)
        kv_stage = sb("kv_stage", [128, LT * T], BF16, c1b)
        ps_kv = [psm(f"ps_kv{i}", [128, 512], F32, c1b) for i in range(LT * TC)]

        sw_kv, st_kv, ss_kv = SW, ST, SS

        @blk.gpsimd
        def _(gp):
            gp.wait_ge(sv, sv_xnT)
            for dt in range(DT):
                if dt >= 2:
                    gp.wait_ge(st, st_kv + (dt - 1) * LT * TC)
                gp.dma_start(out=wc_sb[:, (dt % 2) * L:(dt % 2) * L + L],
                             in_=wc[dt * 128:(dt + 1) * 128, :]).then_inc(sw, 16)
        SW += 16 * DT

        @blk.tensor
        def _(t):
            t.wait_ge(sv, sv_xnT)
            for dt in range(DT):
                t.wait_ge(sw, sw_kv + 16 * min(dt + 2, DT))
                for lt in range(LT):
                    for tc in range(TC):
                        t.matmul(ps_kv[lt * TC + tc][:],
                                 wc_sb[:, (dt % 2) * L + lt * 128:(dt % 2) * L + (lt + 1) * 128],
                                 xnT[:, dt * T + tc * 512:dt * T + (tc + 1) * 512],
                                 start=(dt == 0), stop=(dt == DT - 1)).then_inc(st, 1)
        ST += DT * LT * TC

        @blk.scalar
        def _(s):
            s.wait_ge(st, ST)
            for lt in range(LT):
                for tc in range(TC):
                    s.copy(kv_stage[:, lt * T + tc * 512:lt * T + (tc + 1) * 512],
                           ps_kv[lt * TC + tc][:]).then_inc(ss, 1)
        SS += LT * TC

        sw_cc = SW

        @blk.gpsimd
        def _(gp):
            gp.wait_ge(ss, SS)
            for lt in range(LT):
                gp.dma_start(out=cc_in.ap()[lt], in_=kv_stage[:, lt * T:(lt + 1) * T]).then_inc(sw, 16)
            gp.wait_ge(sw, sw_cc + 16 * LT)
            gp.collective_compute(
                "AllGather", ALU.bypass,
                replica_groups=[[0, 1, 2, 3], [4, 5, 6, 7]],
                ins=[cc_in.ap().opt()],
                outs=[cc_out.ap().opt()],
            ).then_inc(scc, 1)
        SW += 16 * LT
        SCC = 1

        c1b.close()

        c2a = c1.enter_context(contextlib.ExitStack())
        wq_sb = sb("wq_sb", [128, 2 * DT * 256], BF16, c2a)
        wsm_sb_fwd = [sb("wsm_sb", [128, DT * SM], BF16, c2a)]
        ropeA = sb("ropeA", [128, 2 * 512], F32, c2a)
        ropeB = sb("ropeB", [128, 2 * 512], F32, c2a)
        ps_q = [psm(f"ps_q{i}", [128, 512], F32, c2a) for i in range(4)]

        NHP = H // 2
        NU = H * TC
        sw_q, st_q, ss_q, sv_q = SW, ST, SS, SV

        @blk.gpsimd
        def _(gp):
            gp.wait_ge(st, st_q)
            gp.wait_ge(ss, ss_q)
            gp.dma_start(out=wsm_sb_fwd[0][:], in_=wsm).then_inc(sw, 16)
            for hp in range(NHP):
                if hp >= 2:
                    gp.wait_ge(st, st_q + (hp - 1) * 2 * TC)
                gp.dma_start(
                    out=wq_sb[:, (hp % 2) * DT * 256:(hp % 2 + 1) * DT * 256],
                    in_=wq[hp],
                ).then_inc(sw, 16)
        SW += 16 * (NHP + 1)

        @blk.tensor
        def _(t):
            t.wait_ge(ss, ss_q)
            for hp in range(NHP):
                t.wait_ge(sw, sw_q + 16 + 16 * min(hp + 2, NHP))
                for h2 in range(2):
                    for tc in range(TC):
                        u = (hp * 2 + h2) * TC + tc
                        pq = ps_q[u % 4]
                        if u >= 4:
                            t.wait_ge(sv, sv_q + (u - 4) + 1)
                        ins = None
                        for dt in range(DT):
                            base = (hp % 2) * DT * 256 + dt * 256 + h2 * 128
                            ins = t.matmul(pq[:], wq_sb[:, base:base + 128],
                                           xnT[:, dt * T + tc * 512:dt * T + (tc + 1) * 512],
                                           start=(dt == 0), stop=(dt == DT - 1))
                        ins.then_inc(st, 1)
        ST += NU

        @blk.vector
        def _(v):
            v.wait_ge(sv, sv_q)
            for u in range(NU):
                h, tc = u // TC, u % TC
                pq = ps_q[u % 4]
                ra = ropeA[:, (u % 2) * 512:(u % 2 + 1) * 512]
                rb = ropeB[:, (u % 2) * 512:(u % 2 + 1) * 512]
                v.wait_ge(st, st_q + u + 1)
                if u >= 2:
                    v.wait_ge(sv, sv_q + u - 1)
                vjoin(v)
                vchain(v, v.tensor_mul(ra, pq[:], Tcq[:, tc * 512:(tc + 1) * 512]))
                vchain(v, v.tensor_mul(rb[0:64, :], pq[64:128, :], Tsq[0:64, tc * 512:(tc + 1) * 512]))
                vchain(v, v.tensor_mul(rb[64:128, :], pq[0:64, :], Tsq[64:128, tc * 512:(tc + 1) * 512]))
                vjoin(v)
                v.tensor_add(QA[:, h * T + tc * 512:h * T + (tc + 1) * 512], ra, rb).then_inc(sv, 1)
        SV += NU

        wsm_sb = wsm_sb_fwd[0]
        fsc = sb("fsc", [128, 32], F32, c2a)
        hw_tm = sb("hw_tm", [128, 128], BF16, c2a)
        ps_sm = psm("ps_sm", [128, SM], F32, c2a)
        ps_hwt = psm("ps_hwt", [128, 128], BF16, c2a)

        sw_sm, st_sm, ss_sm, sv_sm = SW, ST, SS, SV

        @blk.tensor
        def _(t):
            t.wait_ge(sw, sw_sm)
            for i in range(T8):
                if i >= 1:
                    t.wait_ge(sv, sv_sm + i)
                ins = None
                for dt in range(DT):
                    ins = t.matmul(ps_sm[:], xnT[:, dt * T + i * 128:dt * T + i * 128 + 128],
                                   wsm_sb[:, dt * SM:(dt + 1) * SM],
                                   start=(dt == 0), stop=(dt == DT - 1))
                ins.then_inc(st, 1)
                t.wait_ge(sv, sv_sm + i + 1)
                t.transpose(ps_hwt[:], hw_tm[:], ident_s[:]).then_inc(st, 1)
        ST += 2 * T8

        @blk.scalar
        def _(s):
            for i in range(T8):
                s.wait_ge(st, st_sm + 2 * i + 1)
                s.activation(fsc[:, 0:8], ps_sm[:, 0:8], AF.Exp, scale=1.0).then_inc(ss, 1)
                s.activation(fsc[:, 8:9], ps_sm[:, 8:9], AF.Sigmoid,
                             bias=bthr_s[:, 0:1], scale=1.0).then_inc(ss, 1)
                s.wait_ge(st, st_sm + 2 * i + 2)
                s.copy(hwT[:, i * 128:(i + 1) * 128], ps_hwt[:]).then_inc(ss, 1)
        SS += 3 * T8

        @blk.vector
        def _(v):
            for i in range(T8):
                v.wait_ge(ss, ss_sm + 3 * i + 2)
                if i >= 1:
                    v.wait_ge(st, st_sm + 2 * i)
                gsum, thr, iz, winv = fsc[:, 9:10], fsc[:, 10:11], fsc[:, 11:12], fsc[:, 12:13]
                gate = fsc[:, 16:24]
                ops = [
                    lambda: v.reduce_sum(gsum, fsc[:, 0:8], axis=AX.X),
                    lambda: v.reciprocal(gsum, gsum),
                    lambda: v.tensor_scalar(gate, fsc[:, 0:8], gsum[:, 0:1], None, ALU.mult),
                    lambda: v.tensor_scalar(thr, fsc[:, 8:9], 0.125, None, ALU.mult),
                    lambda: v.tensor_scalar(gate, gate, thr[:, 0:1], None, ALU.subtract),
                    lambda: v.tensor_relu(gate, gate),
                    lambda: v.reduce_sum(gsum, gate, axis=AX.X),
                    lambda: v.tensor_scalar(iz, gsum, 0.0, None, ALU.is_le),
                    lambda: v.tensor_add(gsum, gsum, iz),
                    lambda: v.reciprocal(winv, gsum),
                    lambda: v.tensor_scalar(gate, gate, 2.0, None, ALU.mult),
                    lambda: v.tensor_scalar(gate, gate, winv[:, 0:1], None, ALU.mult),
                ]
                for op in ops:
                    vjoin(v)
                    vchain(v, op())
                vjoin(v)
                ins = None
                for e in range(8):
                    ins = v.tensor_scalar(hw_tm[:, e * 16:(e + 1) * 16],
                                          ps_sm[:, 9 + e * 16:9 + (e + 1) * 16],
                                          gate[:, e:e + 1], None, ALU.mult)
                ins.then_inc(sv, 1)
        SV += T8

        c2a.close()

        c2b = c1.enter_context(contextlib.ExitStack())
        wbt_sb = sb("wbt_sb", [128, 2 * DT * DOC], BF16, c2b)
        bflat_sb = sb("bflat_sb", [128, D], BF16, c2b)
        bbase_s = sb("bbase_s", [1, D], BF16, c2b)
        fstage = sb("fstage", [128, 2 * DOC], BF16, c2b)
        ps_b = [psm(f"ps_b{i}", [128, DOC], F32, c2b) for i in range(2)]

        sw_b, st_b, ss_b, sx_b, sv_b = SW, ST, SS, SX, SV

        @blk.gpsimd
        def _(gp):
            gp.wait_ge(st, st_b)
            gp.wait_ge(sv, sv_b)
            gp.wait_ge(ss, ss_b)
            gp.dma_start(out=bflat_sb[:], in_=bflat).then_inc(sw, 16)
            gp.dma_start(out=bbase_s[:], in_=bbase).then_inc(sw, 16)
            for dc in range(NDC):
                if dc >= 2:
                    gp.wait_ge(st, st_b + (dc - 1) * T8)
                gp.dma_start(
                    out=wbt_sb[:, (dc % 2) * DT * DOC:(dc % 2 + 1) * DT * DOC],
                    in_=wbt[dc],
                ).then_inc(sw, 16)
        SW += 16 * (2 + NDC)

        @blk.tensor
        def _(t):
            t.wait_ge(ss, ss_b)
            for dc in range(NDC):
                t.wait_ge(sw, sw_b + 32 + 16 * min(dc + 2, NDC))
                for i in range(T8):
                    k = dc * T8 + i
                    pb = ps_b[k % 2]
                    if k >= 2:
                        t.wait_ge(ss, ss_b + k - 1)
                    for dt in range(DT):
                        o = (dc % 2) * DT * DOC + dt * DOC
                        t.matmul(pb[:], xnT[:, dt * T + i * 128:dt * T + i * 128 + 128],
                                 wbt_sb[:, o:o + DOC],
                                 start=(dt == 0), stop=False)
                    t.matmul(pb[:], hwT[:, i * 128:(i + 1) * 128],
                             bflat_sb[:, dc * DOC:(dc + 1) * DOC], start=False, stop=False)
                    t.matmul(pb[:], ones_col[:], bbase_s[0:1, dc * DOC:(dc + 1) * DOC],
                             start=False, stop=True).then_inc(st, 1)
        ST += NDC * T8

        @blk.scalar
        def _(s):
            for k in range(NDC * T8):
                s.wait_ge(st, st_b + k + 1)
                if k >= 2:
                    s.wait_ge([sf0, sf1][k % 2], 16 * (k // 2))
                s.copy(fstage[:, (k % 2) * DOC:(k % 2 + 1) * DOC], ps_b[k % 2][:]).then_inc(ss, 1)
        SS += NDC * T8

        @blk.sync
        def _(sy):
            for dc in range(NDC):
                for i in range(T8):
                    k = dc * T8 + i
                    sy.wait_ge(ss, ss_b + k + 1)
                    sy.dma_start(out=ffn_hbm.ap()[i * 128:(i + 1) * 128, dc * DOC:(dc + 1) * DOC],
                                 in_=fstage[:, (k % 2) * DOC:(k % 2 + 1) * DOC]).then_inc([sf0, sf1][k % 2], 16)
        NT2B = NDC * T8

        c2b.close()
        c1.close()

        c3 = es.enter_context(contextlib.ExitStack())
        kvlT = sb("kvlT", [128, LT * S], BF16, c3)
        wkv_sb = sb("wkv_sb", [128, 2 * 2 * LT * 128], BF16, c3)
        KT_sb = sb("KT_sb", [128, 2 * S], BF16, c3)
        V_sb = sb("V_sb", [128, 2 * S], BF16, c3)
        expS = sb("expS", [128, 4 * 512], BF16, c3)
        l_part = sb("l_part", [128, 1024], F32, c3)
        kvlbar = sb("kvlbar", [128, LT], BF16, c3)
        kvlbar_f = sb("kvlbar_f", [128, LT], F32, c3)
        kbar_f = sb("kbar_f", [128, 1], F32, c3)
        sums_s = sb("sums_s", [1, 512], F32, c3)
        krA = sb("krA", [128, 512], F32, c3)
        krB = sb("krB", [128, 512], F32, c3)
        rbc = sb("rbc", [128, 2 * 512], F32, c3)
        ps_k = [psm("ps_k0", [128, 512], F32, c3)]
        ps_v = psm("ps_v", [128, 512], F32, c3)
        ps_sp = [psm(f"ps_sp{i}", [128, 1024], F32, c3) for i in range(2)]
        ps_pv0 = psm("ps_pv0", [128, 512], F32, c3)
        ps_m = psm("ps_m", [128, 512], F32, c3)

        sx_kvl, sv_p3, ss_p3, sw_p3 = SX, SV, SS, SW
        st_p2 = ST

        @blk.sync
        def _(sy):
            sy.wait_ge(scc, SCC)
            sy.wait_ge(st, st_p2)
            sy.wait_ge(sv, sv_p3)
            sy.wait_ge(ss, ss_p3)
            for r in range(G):
                for lt in range(LT):
                    sy.dma_start(out=kvlT[:, lt * S + r * T:lt * S + (r + 1) * T],
                                 in_=cc_out.ap()[r, lt]).then_inc(sx, 16)
        SX += 16 * G * LT
        sx_kv_done = SX

        @blk.vector
        def _(v):
            v.wait_ge(sx, sx_kv_done)
            v.wait_ge(sf0, 16 * ((NT2B + 1) // 2))
            v.wait_ge(sf1, 16 * (NT2B // 2))
            for lt in range(LT):
                vchain(v, v.reduce_sum(kvlbar_f[:, lt:lt + 1], kvlT[:, lt * S:(lt + 1) * S], axis=AX.X))
            vjoin(v)
            vchain(v, v.tensor_scalar(kvlbar_f[:], kvlbar_f[:], 1.0 / S, None, ALU.mult))
            vjoin(v)
            v.tensor_copy(kvlbar[:], kvlbar_f[:]).then_inc(sv, 1)
        SV += 1
        sv_kvb = SV

        @blk.gpsimd
        def _(gp):
            gp.wait_ge(st, st_p2)
            gp.wait_ge(ss, ss_p3)
            gp.wait_ge(sf0, 16 * ((NT2B + 1) // 2))
            gp.wait_ge(sf1, 16 * (NT2B // 2))
            for h in range(H):
                if h >= 2:
                    gp.wait_ge(stk, (h - 1) * PH3)
                base = (h % 2) * 2 * LT * 128
                gp.dma_start(out=wkv_sb[:, base:base + LT * 128], in_=wk[h]).then_inc(sw, 16)
                gp.dma_start(out=wkv_sb[:, base + LT * 128:base + 2 * LT * 128],
                             in_=wv[h]).then_inc(sw, 16)
        SW += 32 * H

        VPH = KC + TC
        SPH = 1 + NVU

        sched = [("kbar", 0)]
        for h in range(H + 1):
            build = []
            if h < H:
                build = [("K", h, kc) for kc in range(KC)] + \
                        [("V", h, vq) for vq in range(NVU)]
            attn = []
            if h >= 1:
                for qc in range(TC):
                    for kt in range(KT):
                        attn.append(("S", h - 1, qc, kt))
                        if kt >= 2:
                            attn.append(("PV", h - 1, qc, kt - 2))
                    attn.append(("PV", h - 1, qc, KT - 2))
                    attn.append(("PV", h - 1, qc, KT - 1))
                    attn.append(("tail", h - 1, qc))
            out_u, bi = [], 0
            for n_, a_ in enumerate(attn):
                out_u.append(a_)
                if n_ % 4 == 3 and bi < len(build):
                    out_u.append(build[bi]); bi += 1
            out_u.extend(build[bi:])
            if not attn:
                out_u = list(build)
            if h + 1 < H:
                out_u.append(("kbar", h + 1))
            sched.extend(out_u)

        @blk.tensor
        def _(t):
            t.wait_ge(sv, sv_kvb)
            t.wait_ge(ss, ss_p3)
            for unit in sched:
                kind = unit[0]
                if kind == "kbar":
                    h = unit[1]
                    base = (h % 2) * 2 * LT * 128
                    t.wait_ge(sw, sw_p3 + 32 * min(h + 2, H))
                    if h >= 1:
                        t.wait_ge(ss, ss_p3 + (h - 1) * SPH + 1)
                    if h >= 2:
                        t.wait_ge(sl, (h - 1) * TC * (KT // 2 + 1))
                    ins = None
                    for lt in range(LT):
                        ins = t.matmul(ps_m[:, 0:1], wkv_sb[:, base + lt * 128:base + (lt + 1) * 128],
                                       kvlbar[:, lt:lt + 1], start=(lt == 0), stop=(lt == LT - 1))
                    ins.then_inc(stk, 1)
                elif kind == "K":
                    h, kc = unit[1], unit[2]
                    base = (h % 2) * 2 * LT * 128
                    g = h * KC + kc
                    pk = ps_k[0]
                    if g >= 1:
                        t.wait_ge(svk, g)
                    ins = None
                    for lt in range(LT):
                        ins = t.matmul(pk[:], wkv_sb[:, base + lt * 128:base + (lt + 1) * 128],
                                       kvlT[:, lt * S + kc * 512:lt * S + (kc + 1) * 512],
                                       start=(lt == 0), stop=(lt == LT - 1))
                    ins.then_inc(stk, 1)
                elif kind == "V":
                    h, vq = unit[1], unit[2]
                    base = (h % 2) * 2 * LT * 128
                    gv = h * NVU + vq
                    if gv >= 1:
                        t.wait_ge(ss, ss_p3 + ((gv - 1) // NVU) * SPH + 1 + ((gv - 1) % NVU) + 1)
                    ins = None
                    for j in range(4):
                        tt = vq * 4 + j
                        for lt in range(LT):
                            ins = t.matmul(ps_v[:, j * 128:(j + 1) * 128],
                                           kvlT[:, lt * S + tt * 128:lt * S + (tt + 1) * 128],
                                           wkv_sb[:, base + LT * 128 + lt * 128:base + LT * 128 + (lt + 1) * 128],
                                           start=(lt == 0), stop=(lt == LT - 1))
                    ins.then_inc(stk, 1)
                elif kind == "S":
                    h, qc, kt = unit[1], unit[2], unit[3]
                    hb = h % 2
                    u = h * TC + qc
                    pss = ps_sp[(kt // 2) % 2][:, (kt % 2) * 512:(kt % 2 + 1) * 512]
                    if kt >= 4 and kt % 2 == 0:
                        t.wait_ge(se, u * (KT // 2) + kt // 2 - 1)
                    if kt % 4 == 0:
                        t.wait_ge(svk, h * KC + (kt // 4) + 1)
                    ins = t.matmul(pss, KT_sb[:, hb * S + kt * 128:hb * S + (kt + 1) * 128],
                                   QA[:, h * T + qc * 512:h * T + (qc + 1) * 512],
                                   start=True, stop=True)
                    if kt % 2 == 1:
                        ins.then_inc(sts, 1)
                elif kind == "PV":
                    h, qc, kt = unit[1], unit[2], unit[3]
                    hb = h % 2
                    u = h * TC + qc
                    if kt == 0 and u >= 1:
                        t.wait_ge(sv, sv_p3 + 1 + u)
                    if kt % 2 == 0:
                        t.wait_ge(se, u * (KT // 2) + kt // 2 + 1)
                    if kt % 4 == 0:
                        t.wait_ge(ss, ss_p3 + h * SPH + 1 + (kt // 4) + 1)
                    ins = t.matmul(ps_pv0[:], V_sb[:, hb * S + kt * 128:hb * S + (kt + 1) * 128],
                                   expS[:, (kt % 4) * 512:(kt % 4 + 1) * 512],
                                   start=(kt == 0), stop=(kt == KT - 1))
                    if kt % 2 == 1:
                        ins.then_inc(stp, 1)
                else:
                    h, qc = unit[1], unit[2]
                    u = h * TC + qc
                    t.wait_ge(sl, u * (KT // 2 + 1) + KT // 2)
                    if h + 1 <= H - 1:
                        t.wait_ge(ss, ss_p3 + (h + 1) * SPH + 1)
                    if u >= 1:
                        t.wait_ge(sl, u * (KT // 2 + 1))
                    t.matmul(ps_m[0:1, :], ones128f[:], l_part[:, 0:512], start=True, stop=False)
                    t.matmul(ps_m[0:1, :], ones128f[:], l_part[:, 512:1024], start=False, stop=True).then_inc(stp, 1)
                    t.wait_ge(sb2, u + 1)
                    t.matmul(ps_m[:, 0:512], ones1f[:], sums_s[:], start=True, stop=True).then_inc(stp, 1)

        @blk.scalar
        def _(s):
            s.wait_ge(sf0, 16 * ((NT2B + 1) // 2))
            s.wait_ge(sf1, 16 * (NT2B // 2))
            for unit in sched:
                kind = unit[0]
                if kind == "kbar":
                    h = unit[1]
                    s.wait_ge(stk, h * PH3 + 1)
                    if h >= 1:
                        s.wait_ge(svk, h * KC)
                    s.copy(kbar_f[:], ps_m[:, 0:1]).then_inc(ss, 1)
                elif kind == "V":
                    h, vq = unit[1], unit[2]
                    hb = h % 2
                    s.wait_ge(stk, h * PH3 + 1 + KC + vq + 1)
                    if h >= 2:
                        s.wait_ge(stp, (h - 1) * TC * (KT // 2 + 2))
                    s.copy(V_sb[:, hb * S + vq * 512:hb * S + (vq + 1) * 512], ps_v[:]).then_inc(ss, 1)
                elif kind == "S":
                    h, qc, kt = unit[1], unit[2], unit[3]
                    if kt % 2 == 0:
                        continue
                    u = h * TC + qc
                    j = kt // 2
                    s.wait_ge(sts, u * (KT // 2) + j + 1)
                    if j >= 2:
                        s.wait_ge(stp, u * (KT // 2 + 2) + j - 1)
                        s.wait_ge(sl, u * (KT // 2 + 1) + j - 1)
                    s.activation(expS[:, ((j % 2) * 1024):((j % 2) * 1024 + 1024)],
                                 ps_sp[(j % 2)][:], AF.Exp, scale=1.0).then_inc(se, 1)
                elif kind == "tail":
                    h, qc = unit[1], unit[2]
                    u = h * TC + qc
                    s.wait_ge(stp, u * (KT // 2 + 2) + KT // 2 + 1)
                    if u >= 1:
                        s.wait_ge(stp, (u - 1) * (KT // 2 + 2) + KT // 2 + 2)
                    s.copy(sums_s[:], ps_m[0:1, :]).then_inc(sb2, 1)

        @blk.vector
        def _(v):
            v.wait_ge(sf0, 16 * ((NT2B + 1) // 2))
            v.wait_ge(sf1, 16 * (NT2B // 2))
            for unit in sched:
                kind = unit[0]
                if kind == "K":
                    h, kc = unit[1], unit[2]
                    hb = h % 2
                    g = h * KC + kc
                    pk = ps_k[0]
                    v.wait_ge(stk, h * PH3 + 1 + kc + 1)
                    if g > 0:
                        v.wait_ge(svk, g)
                    if kc == 0:
                        v.wait_ge(ss, ss_p3 + h * SPH + 1)
                        if h >= 2:
                            v.wait_ge(sts, (h - 1) * TC * (KT // 2))
                    vjoin(v)
                    vchain(v, v.tensor_mul(krA[:], pk[:], Tck[:, kc * 512:(kc + 1) * 512]))
                    vchain(v, v.tensor_mul(krB[0:64, :], pk[64:128, :], Tsk[0:64, kc * 512:(kc + 1) * 512]))
                    vchain(v, v.tensor_mul(krB[64:128, :], pk[0:64, :], Tsk[64:128, kc * 512:(kc + 1) * 512]))
                    vjoin(v)
                    v.scalar_tensor_tensor(KT_sb[:, hb * S + kc * 512:hb * S + (kc + 1) * 512],
                                           krA[:], kbar_f[:, 0:1], krB[:],
                                           ALU.subtract, ALU.add).then_inc(svk, 1)
                elif kind == "S":
                    h, qc, kt = unit[1], unit[2], unit[3]
                    if kt % 2 == 0:
                        continue
                    u = h * TC + qc
                    j = kt // 2
                    v.wait_ge(se, u * (KT // 2) + j + 1)
                    exs = expS[:, (j % 2) * 1024:(j % 2) * 1024 + 1024]
                    if j == 0:
                        if u >= 1:
                            v.wait_ge(stp, (u - 1) * (KT // 2 + 2) + KT // 2 + 1)
                        v.tensor_scalar(l_part[:], exs, 1.0, None, ALU.mult).then_inc(sl, 1)
                    else:
                        v.wait_ge(sl, u * (KT // 2 + 1) + j)
                        v.tensor_add(l_part[:], l_part[:], exs).then_inc(sl, 1)
                elif kind == "tail":
                    h, qc = unit[1], unit[2]
                    u = h * TC + qc
                    v.wait_ge(stp, u * (KT // 2 + 2) + KT // 2 + 2)
                    if u >= 2:
                        v.wait_ge(sv, sv_p3 + 1 + (u - 2) + 1)
                    v.reciprocal(rbc[:, (u % 2) * 512:(u % 2 + 1) * 512],
                                 ps_m[:, 0:512]).then_inc(sl, 1)
                    v.wait_ge(sl, u * (KT // 2 + 1) + KT // 2 + 1)
                    v.tensor_mul(QA[:, h * T + qc * 512:h * T + (qc + 1) * 512],
                                 ps_pv0[:], rbc[:, (u % 2) * 512:(u % 2 + 1) * 512]).then_inc(sv, 1)
        SV += H * TC
        sv_attn = SV

        c3.close()

        c4 = es.enter_context(contextlib.ExitStack())
        wo_sb = sb("wo_sb", [128, 2 * DT * DOC], BF16, c4)
        xres = sb("xres", [128, 2 * DOC], F32, c4)
        fres = sb("fres", [128, 2 * DOC], BF16, c4)
        ostage = sb("ostage", [128, 2 * DOC], F32, c4)
        ps_o = [psm(f"ps_o{i}", [128, DOC], F32, c4) for i in range(2)]

        sv_p4, sw_p4, st_p4, sx_p4 = SV, SW, ST, SX

        @blk.gpsimd
        def _(gp):
            gp.wait_ge(sv, sv_attn)
            gp.wait_ge(svk, H * KC)
            for dc in range(NDC):
                if dc >= 2:
                    gp.wait_ge(st, st_p4 + (dc - 1) * T8)
                gp.dma_start(
                    out=wo_sb[:, (dc % 2) * DT * DOC:(dc % 2 + 1) * DT * DOC],
                    in_=wo[dc],
                ).then_inc(sw, 16)
        SW += 16 * NDC

        @blk.sync
        def _(sy):
            sy.wait_ge(sv, sv_attn)
            for dc in range(NDC):
                for i in range(T8):
                    k = dc * T8 + i
                    if k >= 2:
                        sy.wait_ge(sv, sv_p4 + k - 1)
                    sy.dma_start(out=xres[:, (k % 2) * DOC:(k % 2 + 1) * DOC],
                                 in_=x_own[i * 128:(i + 1) * 128, dc * DOC:(dc + 1) * DOC]).then_inc(sx, 16)
                    sy.dma_start(out=fres[:, (k % 2) * DOC:(k % 2 + 1) * DOC],
                                 in_=ffn_hbm.ap()[i * 128:(i + 1) * 128, dc * DOC:(dc + 1) * DOC]).then_inc(sx, 16)
        SX += 32 * NDC * T8

        @blk.tensor
        def _(t):
            t.wait_ge(sv, sv_attn)
            for dc in range(NDC):
                t.wait_ge(sw, sw_p4 + 16 * min(dc + 2, NDC))
                for i in range(T8):
                    k = dc * T8 + i
                    po = ps_o[k % 2]
                    if k >= 2:
                        t.wait_ge(sv, sv_p4 + k - 1)
                    ins = None
                    for h in range(DT):
                        o = (dc % 2) * DT * DOC + h * DOC
                        ins = t.matmul(po[:], QA[:, h * T + i * 128:h * T + i * 128 + 128],
                                       wo_sb[:, o:o + DOC], start=(h == 0), stop=(h == DT - 1))
                    ins.then_inc(st, 1)
        ST += NDC * T8

        @blk.vector
        def _(v):
            for k in range(NDC * T8):
                v.wait_ge(st, st_p4 + k + 1)
                v.wait_ge(sx, sx_p4 + 32 * min(k + 2, NDC * T8))
                if k >= 2:
                    v.wait_ge([so, so1][k % 2], 16 * (k // 2))
                osl = ostage[:, (k % 2) * DOC:(k % 2 + 1) * DOC]
                vjoin(v)
                vchain(v, v.tensor_add(osl, ps_o[k % 2][:], xres[:, (k % 2) * DOC:(k % 2 + 1) * DOC]))
                vjoin(v)
                v.tensor_add(osl, osl, fres[:, (k % 2) * DOC:(k % 2 + 1) * DOC]).then_inc(sv, 1)
        SV += NDC * T8

        @blk.scalar
        def _(s):
            for dc in range(NDC):
                for i in range(T8):
                    k = dc * T8 + i
                    s.wait_ge(sv, sv_p4 + k + 1)
                    s.dma_start(out=out[i * 128:(i + 1) * 128, dc * DOC:(dc + 1) * DOC],
                                in_=ostage[:, (k % 2) * DOC:(k % 2 + 1) * DOC]).then_inc([so, so1][k % 2], 16)
            s.wait_ge(so, 16 * ((NDC * T8 + 1) // 2))
            s.wait_ge(so1, 16 * ((NDC * T8) // 2))

        c4.close()

    return nc


def _host_layouts(inputs, cfg: CFG):
    D, T, S, ER = cfg.D, cfg.T, cfg.S, cfg.ER
    f32 = np.float32
    W_gate = np.asarray(inputs["L0_ffn_W_gate"])
    W_thr = np.asarray(inputs["L0_ffn_W_thr"])
    A_w = np.asarray(inputs["L0_ffn_A_w"])
    B_w = np.asarray(inputs["L0_ffn_B_w"])
    DT, LT, H, NDC, DOC, SM = cfg.DT, cfg.LT, cfg.H, cfg.NDC, cfg.DOC, cfg.SM

    def tile_din(w, np_, cw):
        return np.ascontiguousarray(
            np.asarray(w, f32).reshape(DT, 128, np_, cw).transpose(2, 1, 0, 3).reshape(np_, 128, DT * cw))

    def tile_lat(w, nh):
        return np.ascontiguousarray(
            np.asarray(w, f32).reshape(LT, 128, nh, 128).transpose(2, 1, 0, 3).reshape(nh, 128, LT * 128))

    wsm_flat = np.concatenate(
        [W_gate.T, W_thr[:, None], np.transpose(A_w, (2, 0, 1)).reshape(D, ER)], axis=1)
    common = {
        "wq": tile_din(inputs["L0_attn_Wq"], H // 2, 256),
        "wc": np.ascontiguousarray(inputs["L0_attn_Wc"], f32),
        "wk": tile_lat(inputs["L0_attn_Wk"], H),
        "wv": tile_lat(inputs["L0_attn_Wv"], H),
        "wo": tile_din(inputs["L0_attn_Wo"], NDC, DOC),
        "wbt": tile_din(np.asarray(inputs["L0_ffn_W_base"]).T, NDC, DOC),
        "wsm": np.ascontiguousarray(
            np.asarray(wsm_flat, f32).reshape(DT, 128, SM).transpose(1, 0, 2).reshape(128, DT * SM)),
        "bflat": np.ascontiguousarray(np.transpose(B_w, (0, 2, 1)).reshape(ER, D), f32),
        "bbase": np.ascontiguousarray(np.asarray(inputs["L0_ffn_b_base"], f32)[None, :]),
        "bthr": np.ascontiguousarray(np.broadcast_to(
            np.asarray(inputs["L0_ffn_b_thr"], f32).reshape(1, 1), (128, 1))),
        "posk": np.ascontiguousarray(np.broadcast_to(np.arange(S, dtype=f32)[None, :], (128, S))),
        "dind": np.ascontiguousarray((np.arange(128, dtype=f32) % 64)[:, None]),
        "ident": np.eye(128, dtype=f32),
    }
    x = np.asarray(inputs["L0_attn_hidden_states"], f32)
    in_maps = []
    for c in range(8):
        b, qq = c // 4, c % 4
        m = dict(common)
        m["x_own"] = np.ascontiguousarray(x[b, qq * T:(qq + 1) * T, :])
        m["posq"] = np.ascontiguousarray(np.broadcast_to(
            np.arange(qq * T, (qq + 1) * T, dtype=f32)[None, :], (128, T)))
        in_maps.append(m)
    return in_maps


def kernel(**inputs):
    cfg = CFG()
    nc = build_program(cfg)
    in_maps = _host_layouts(inputs, cfg)
    res = run_bass_kernel_spmd(nc, in_maps, core_ids=list(range(8)))
    out = np.empty((2, 4 * cfg.T, cfg.D), dtype=np.float32)
    for c in range(8):
        b, qq = c // 4, c % 4
        out[b, qq * cfg.T:(qq + 1) * cfg.T, :] = res.results[c]["out"]
    return out


# version 16
# speedup vs baseline: 1.0267x; 1.0267x over previous
import contextlib
import math
import numpy as np
import concourse.bass as bass
import concourse.mybir as mybir
import concourse.bass_utils as _bu
from concourse.bass_utils import run_bass_kernel_spmd

if not getattr(_bu, "_ldw_patched", False):
    _orig_run_command = _bu.run_command

    def _run_command_ldw(cmd, *a, **kw):
        if isinstance(cmd, (list, tuple)):
            cmd = [c.replace("--enable-ldw-opt=false", "--enable-ldw-opt=true")
                   if isinstance(c, str) else c for c in cmd]
        return _orig_run_command(cmd, *a, **kw)

    _bu.run_command = _run_command_ldw
    _bu._ldw_patched = True

F32 = mybir.dt.float32
BF16 = mybir.dt.bfloat16
AF = mybir.ActivationFunctionType
ALU = mybir.AluOpType
AX = mybir.AxisListType

PI = math.pi
LN_THETA = math.log(10000.0)


class CFG:
    def __init__(self, D=4096, L=512, H=32, T=1024):
        self.D, self.L, self.H, self.T = D, L, H, T
        self.S = 4 * T
        self.E, self.R = 8, 16
        self.ER = self.E * self.R
        self.DT = D // 128
        self.LT = L // 128
        self.TC = T // 512
        self.T8 = T // 128
        self.KT = self.S // 128
        self.KC = self.S // 512
        self.G = 4
        self.DOC = 256
        self.NDC = D // self.DOC
        self.DOC4 = 512
        self.NDC4 = D // self.DOC4
        self.SM = 9 + self.ER


def build_program(cfg: CFG):
    D, L, H, T, S = cfg.D, cfg.L, cfg.H, cfg.T, cfg.S
    DT, LT, TC, T8, KT, KC = cfg.DT, cfg.LT, cfg.TC, cfg.T8, cfg.KT, cfg.KC
    ER, DOC, NDC, G, SM = cfg.ER, cfg.DOC, cfg.NDC, cfg.G, cfg.SM
    DOC4, NDC4 = cfg.DOC4, cfg.NDC4
    SQ = 1.0 / math.sqrt(128.0)
    NVU = KT // 4
    PH3 = 1 + KC + NVU

    nc = bass.Bass("TRN2", target_bir_lowering=False, debug=False)

    x_own = nc.dram_tensor("x_own", [T, D], F32, kind="ExternalInput").ap()
    wq = nc.dram_tensor("wq", [H // 2, 128, DT * 256], F32, kind="ExternalInput").ap()
    wc = nc.dram_tensor("wc", [D, L], F32, kind="ExternalInput").ap()
    wk = nc.dram_tensor("wk", [H, 128, LT * 128], F32, kind="ExternalInput").ap()
    wv = nc.dram_tensor("wv", [H, 128, LT * 128], F32, kind="ExternalInput").ap()
    wo = nc.dram_tensor("wo", [NDC4, 128, DT * DOC4], F32, kind="ExternalInput").ap()
    wbt = nc.dram_tensor("wbt", [NDC, 128, DT * DOC], F32, kind="ExternalInput").ap()
    wsm = nc.dram_tensor("wsm", [128, DT * SM], F32, kind="ExternalInput").ap()
    bflat = nc.dram_tensor("bflat", [ER, D], F32, kind="ExternalInput").ap()
    bbase = nc.dram_tensor("bbase", [1, D], F32, kind="ExternalInput").ap()
    bthr = nc.dram_tensor("bthr", [128, 1], F32, kind="ExternalInput").ap()
    posq = nc.dram_tensor("posq", [128, T], F32, kind="ExternalInput").ap()
    posk = nc.dram_tensor("posk", [128, S], F32, kind="ExternalInput").ap()
    dind = nc.dram_tensor("dind", [128, 1], F32, kind="ExternalInput").ap()
    ident_in = nc.dram_tensor("ident", [128, 128], F32, kind="ExternalInput").ap()
    out = nc.dram_tensor("out", [T, D], F32, kind="ExternalOutput").ap()

    cc_in = nc.dram_tensor("cc_in", [LT, 128, T], BF16)
    cc_out = nc.dram_tensor("cc_out", [G, LT, 128, T], BF16)
    ffn_hbm = nc.dram_tensor("ffn_hbm", [T, D], BF16)

    es = contextlib.ExitStack()

    with es:
        def sb(name, shape, dt, stack=es):
            return stack.enter_context(nc.sbuf_tensor(name, shape, dt))

        def psm(name, shape, dt, stack):
            return stack.enter_context(nc.psum_tensor(name, shape, dt))

        ones_col = sb("ones_col", [1, 128], BF16)
        ones128f = sb("ones128f", [128, 1], F32)
        ones1f = sb("ones1f", [1, 128], F32)
        Tck = sb("Tck", [128, S], BF16)
        Tsk = sb("Tsk", [128, S], BF16)
        Tcq = sb("Tcq", [128, T], BF16)
        Tsq = sb("Tsq", [128, T], BF16)
        QA = sb("QA", [128, H * T], BF16)
        hwT = sb("hwT", [128, T], BF16)
        ident_s = sb("ident_s", [128, 128], BF16)
        bthr_s = sb("bthr_s", [128, 1], F32)

        sx = es.enter_context(nc.semaphore("sx"))
        sw = es.enter_context(nc.semaphore("sw"))
        scc = es.enter_context(nc.semaphore("scc"))
        st = es.enter_context(nc.semaphore("st"))
        stk = es.enter_context(nc.semaphore("stk"))
        sts = es.enter_context(nc.semaphore("sts"))
        stp = es.enter_context(nc.semaphore("stp"))
        se = es.enter_context(nc.semaphore("se"))
        sl = es.enter_context(nc.semaphore("sl"))
        sv = es.enter_context(nc.semaphore("sv"))
        ss = es.enter_context(nc.semaphore("ss"))
        so = es.enter_context(nc.semaphore("so"))
        sm0 = es.enter_context(nc.semaphore("sm0"))
        sf0 = es.enter_context(nc.semaphore("sf0"))
        sf1 = es.enter_context(nc.semaphore("sf1"))
        so1 = es.enter_context(nc.semaphore("so1"))
        svc = es.enter_context(nc.semaphore("svc"))
        sb2 = es.enter_context(nc.semaphore("sb2"))
        svk = es.enter_context(nc.semaphore("svk"))
        VC = [0]

        def vchain(v, ins):
            ins.then_inc(svc, 1)
            VC[0] += 1

        def vjoin(v):
            v.wait_ge(svc, VC[0])

        blk = es.enter_context(nc.Block())

        SX = SW = SCC = ST = SV = SS = 0

        c1 = es.enter_context(contextlib.ExitStack())
        xnT = sb("xnT", [128, DT * T], BF16, c1)

        c0 = c1.enter_context(contextlib.ExitStack())
        th = sb("th", [128, S], F32, c0)
        pq_s = sb("pq_s", [128, T], F32, c0)
        tk0 = sb("tk0", [128, S], F32, c0)
        ck0 = sb("ck0", [128, S], F32, c0)
        invf = sb("invf", [128, 1], F32, c0)

        @blk.sync
        def _(sy):
            sy.dma_start(out=th[:], in_=posk).then_inc(sx, 16)
            sy.dma_start(out=pq_s[:], in_=posq).then_inc(sx, 16)
            sy.dma_start(out=invf[:], in_=dind).then_inc(sx, 16)
            sy.dma_start(out=bthr_s[:], in_=bthr).then_inc(sx, 16)
        SX += 64

        @blk.gpsimd
        def _(gp):
            gp.dma_start(out=ident_s[:], in_=ident_in).then_inc(sw, 16)
            gp.memset(ones_col[:], 1.0)
            gp.memset(ones128f[:], 1.0)
            gp.memset(ones1f[:], 1.0).then_inc(sm0, 1)
        SW += 16

        @blk.scalar
        def _(s):
            s.wait_ge(sx, 64)
            s.activation(invf[:], invf[:], AF.Exp, scale=-LN_THETA / 64.0).then_inc(ss, 1)
        SS += 1

        def emit_wrap(v, pos_sl, tk, ck):
            vchain(v, v.tensor_scalar(tk, pos_sl, 1.0 / (2 * PI), None, ALU.mult))
            for b in [512.0, 256.0, 128.0, 64.0, 32.0, 16.0, 8.0, 4.0, 2.0, 1.0]:
                vjoin(v)
                vchain(v, v.tensor_scalar(ck, tk, b, None, ALU.is_ge))
                vjoin(v)
                vchain(v, v.scalar_tensor_tensor(tk, ck, -b, tk, ALU.mult, ALU.add))
            vjoin(v)
            v.tensor_scalar(tk, tk, 2 * PI, -PI, ALU.mult, ALU.add).then_inc(sv, 1)

        def emit_tables(pos_ap, width, Tc, Ts, fold, SV, SS):
            ss0, sv0 = SS, SV
            tk, ck = tk0[:, :width], ck0[:, :width]

            @blk.vector
            def _(v):
                v.wait_ge(ss, ss0)
                vjoin(v)
                vchain(v, v.tensor_scalar(pos_ap, pos_ap, invf[:, 0:1], None, ALU.mult))
                vjoin(v)
                emit_wrap(v, pos_ap, tk, ck)

            @blk.scalar
            def _(s):
                s.wait_ge(sv, sv0 + 1)
                s.activation(Ts[:, :width], tk, AF.Sin, scale=1.0).then_inc(ss, 1)

            @blk.vector
            def _(v):
                v.wait_ge(ss, ss0 + 1)
                v.tensor_scalar(Ts[0:64, :width], Ts[0:64, :width], fold, None, ALU.mult)
                v.tensor_scalar(Ts[64:128, :width], Ts[64:128, :width], -fold, None, ALU.mult)
                vjoin(v)
                vchain(v, v.tensor_scalar(pos_ap, pos_ap, PI / 2, None, ALU.add))
                vjoin(v)
                emit_wrap(v, pos_ap, tk, ck)

            @blk.scalar
            def _(s):
                s.wait_ge(sv, sv0 + 2)
                s.activation(Tc[:, :width], tk, AF.Sin, scale=1.0).then_inc(ss, 1)

            @blk.vector
            def _(v):
                v.wait_ge(ss, ss0 + 2)
                v.tensor_scalar(Tc[:, :width], Tc[:, :width], -fold, None, ALU.mult).then_inc(sv, 1)
            return sv0 + 3, ss0 + 2

        SV, SS = emit_tables(th[:], S, Tck, Tsk, 1.0, SV, SS)
        SV, SS = emit_tables(pq_s[:], T, Tcq, Tsq, SQ, SV, SS)
        sv_tab, ss_tab = SV, SS

        c0.close()
        c1p = c1.enter_context(contextlib.ExitStack())
        xst = sb("xst", [128, 2 * D], F32, c1p)
        xn_tm = sb("xn_tm", [128, 2 * D], BF16, c1p)
        stats = sb("stats", [128, 10], F32, c1p)
        ps_tp = [psm(f"ps_tp{i}", [128, 512], BF16, c1p) for i in range(2)]

        NQ4 = DT // 4
        sx_p1, ss_p1, sv_p1, st_p1 = SX, SS, SV, ST

        @blk.sync
        def _(sy):
            for i in range(T8):
                if i == 0:
                    sy.wait_ge(ss, ss_tab)
                if i >= 2:
                    sy.wait_ge(ss, ss_p1 + 3 * (i - 1))
                sy.dma_start(out=xst[:, (i % 2) * D:(i % 2 + 1) * D],
                             in_=x_own[i * 128:(i + 1) * 128, :]).then_inc(sx, 16)
        SX += 16 * T8

        PCH = 2 + NQ4

        def _evac_val(k):
            i, q4 = k // NQ4, k % NQ4
            if i + 1 < T8:
                return sv_p1 + 2 * (i + 2) + i * NQ4 + q4 + 1
            return sv_p1 + 2 * T8 + (T8 - 1) * NQ4 + q4 + 1

        def _emit_evac(v, i, q4):
            k = i * NQ4 + q4
            v.wait_ge(st, st_p1 + k + 1)
            dst = xnT[:].rearrange("p (d t) -> p d t", d=DT, t=T)[
                :, q4 * 4:(q4 + 1) * 4, i * 128:(i + 1) * 128]
            v.tensor_copy(dst, ps_tp[k % 2][:].rearrange(
                "p (d c) -> p d c", d=4, c=128)).then_inc(sv, 1)

        @blk.vector
        def _(v):
            for i in range(T8):
                o = (i % 2) * 5
                mean_, msq, var_, rstd, nb = (stats[:, o + j:o + j + 1] for j in range(5))
                xb = xst[:, (i % 2) * D:(i % 2 + 1) * D]
                v.wait_ge(sx, sx_p1 + 16 * min(i + 2, T8))
                if i >= 2:
                    v.wait_ge(ss, ss_p1 + 3 * (i - 1))
                v.wait_ge(ss, ss_p1 + 3 * i + 1)
                vjoin(v)
                vchain(v, v.tensor_scalar(xn_tm[:, (i % 2) * D:(i % 2 + 1) * D], xb,
                                          1.0 / D, None, ALU.mult, ALU.add, accum_out=mean_))
                vjoin(v)
                vchain(v, v.tensor_mul(var_, mean_, mean_))
                vjoin(v)
                vchain(v, v.tensor_sub(var_, msq, var_))
                vjoin(v)
                vchain(v, v.tensor_scalar(var_, var_, 1e-5, None, ALU.add))
                vjoin(v)
                v.reciprocal(var_, var_).then_inc(sv, 1)
                v.wait_ge(ss, ss_p1 + 3 * i + 2)
                v.scalar_tensor_tensor(nb, mean_, -1.0, rstd, ALU.mult, ALU.mult).then_inc(sv, 1)
                if i >= 1:
                    for q4 in range(NQ4):
                        _emit_evac(v, i - 1, q4)
            for q4 in range(NQ4):
                _emit_evac(v, T8 - 1, q4)
        SV += T8 * PCH

        import math as _math
        RSD = 1.0 / _math.sqrt(D)

        @blk.scalar
        def _(s):
            for i in range(T8):
                o = (i % 2) * 5
                msq_, var_, rstd, nb = (stats[:, o + j:o + j + 1] for j in range(1, 5))
                s.wait_ge(sx, sx_p1 + 16 * min(i + 2, T8))
                if i >= 2:
                    s.wait_ge(sv, sv_p1 + 2 * (i - 2) + max(i - 3, 0) * NQ4 + 2)
                    s.wait_ge(st, st_p1 + (i - 1) * NQ4)
                s.activation(xn_tm[:, (i % 2) * D:(i % 2 + 1) * D],
                             xst[:, (i % 2) * D:(i % 2 + 1) * D], AF.Square,
                             scale=RSD, accum_out=msq_).then_inc(ss, 1)
                s.wait_ge(sv, sv_p1 + 2 * i + max(i - 1, 0) * NQ4 + 1)
                s.activation(rstd, var_, AF.Sqrt).then_inc(ss, 1)
                s.wait_ge(sv, sv_p1 + 2 * i + max(i - 1, 0) * NQ4 + 2)
                if i >= 2:
                    s.wait_ge(st, st_p1 + (i - 1) * NQ4)
                s.activation(xn_tm[:, (i % 2) * D:(i % 2 + 1) * D],
                             xst[:, (i % 2) * D:(i % 2 + 1) * D], AF.Identity,
                             bias=nb, scale=rstd).then_inc(ss, 1)
        SS += 3 * T8

        @blk.tensor
        def _(t):
            t.wait_ge(sw, 16)
            t.wait_ge(sm0, 1)
            for i in range(T8):
                t.wait_ge(ss, ss_p1 + 3 * i + 3)
                xb = xn_tm[:, (i % 2) * D:(i % 2 + 1) * D]
                for q4 in range(NQ4):
                    k = i * NQ4 + q4
                    pst = ps_tp[k % 2]
                    if k >= 2:
                        t.wait_ge(sv, _evac_val(k - 2))
                    ins = None
                    for j in range(4):
                        dt = q4 * 4 + j
                        ins = t.transpose(pst[:, j * 128:(j + 1) * 128],
                                          xb[:, dt * 128:(dt + 1) * 128], ident_s[:])
                    ins.then_inc(st, 1)
        ST += T8 * NQ4
        sv_xnT = SV

        c1p.close()

        c1b = c1.enter_context(contextlib.ExitStack())
        wc_sb = sb("wc_sb", [128, 2 * L], BF16, c1b)
        kv_stage = sb("kv_stage", [128, LT * T], BF16, c1b)
        ps_kv = [psm(f"ps_kv{i}", [128, 512], F32, c1b) for i in range(LT * TC)]

        sw_kv, st_kv, ss_kv = SW, ST, SS

        @blk.gpsimd
        def _(gp):
            gp.wait_ge(sv, sv_xnT)
            for dt in range(DT):
                if dt >= 2:
                    gp.wait_ge(st, st_kv + (dt - 1) * LT * TC)
                gp.dma_start(out=wc_sb[:, (dt % 2) * L:(dt % 2) * L + L],
                             in_=wc[dt * 128:(dt + 1) * 128, :]).then_inc(sw, 16)
        SW += 16 * DT

        @blk.tensor
        def _(t):
            t.wait_ge(sv, sv_xnT)
            for dt in range(DT):
                t.wait_ge(sw, sw_kv + 16 * min(dt + 2, DT))
                for lt in range(LT):
                    for tc in range(TC):
                        t.matmul(ps_kv[lt * TC + tc][:],
                                 wc_sb[:, (dt % 2) * L + lt * 128:(dt % 2) * L + (lt + 1) * 128],
                                 xnT[:, dt * T + tc * 512:dt * T + (tc + 1) * 512],
                                 start=(dt == 0), stop=(dt == DT - 1)).then_inc(st, 1)
        ST += DT * LT * TC

        @blk.scalar
        def _(s):
            s.wait_ge(st, ST)
            for lt in range(LT):
                for tc in range(TC):
                    s.copy(kv_stage[:, lt * T + tc * 512:lt * T + (tc + 1) * 512],
                           ps_kv[lt * TC + tc][:]).then_inc(ss, 1)
        SS += LT * TC

        sw_cc = SW

        @blk.gpsimd
        def _(gp):
            gp.wait_ge(ss, SS)
            for lt in range(LT):
                gp.dma_start(out=cc_in.ap()[lt], in_=kv_stage[:, lt * T:(lt + 1) * T]).then_inc(sw, 16)
            gp.wait_ge(sw, sw_cc + 16 * LT)
            gp.collective_compute(
                "AllGather", ALU.bypass,
                replica_groups=[[0, 1, 2, 3], [4, 5, 6, 7]],
                ins=[cc_in.ap().opt()],
                outs=[cc_out.ap().opt()],
            ).then_inc(scc, 1)
        SW += 16 * LT
        SCC = 1

        c1b.close()

        c2a = c1.enter_context(contextlib.ExitStack())
        wq_sb = sb("wq_sb", [128, 2 * DT * 256], BF16, c2a)
        wsm_sb_fwd = [sb("wsm_sb", [128, DT * SM], BF16, c2a)]
        ropeA = sb("ropeA", [128, 2 * 512], F32, c2a)
        ropeB = sb("ropeB", [128, 2 * 512], F32, c2a)
        ps_q = [psm(f"ps_q{i}", [128, 512], F32, c2a) for i in range(4)]

        NHP = H // 2
        NU = H * TC
        sw_q, st_q, ss_q, sv_q = SW, ST, SS, SV

        @blk.gpsimd
        def _(gp):
            gp.wait_ge(st, st_q)
            gp.wait_ge(ss, ss_q)
            gp.dma_start(out=wsm_sb_fwd[0][:], in_=wsm).then_inc(sw, 16)
            for hp in range(NHP):
                if hp >= 2:
                    gp.wait_ge(st, st_q + (hp - 1) * 2 * TC)
                gp.dma_start(
                    out=wq_sb[:, (hp % 2) * DT * 256:(hp % 2 + 1) * DT * 256],
                    in_=wq[hp],
                ).then_inc(sw, 16)
        SW += 16 * (NHP + 1)

        @blk.tensor
        def _(t):
            t.wait_ge(ss, ss_q)
            for hp in range(NHP):
                t.wait_ge(sw, sw_q + 16 + 16 * min(hp + 2, NHP))
                for h2 in range(2):
                    for tc in range(TC):
                        u = (hp * 2 + h2) * TC + tc
                        pq = ps_q[u % 4]
                        if u >= 4:
                            t.wait_ge(sv, sv_q + (u - 4) + 1)
                        ins = None
                        for dt in range(DT):
                            base = (hp % 2) * DT * 256 + dt * 256 + h2 * 128
                            ins = t.matmul(pq[:], wq_sb[:, base:base + 128],
                                           xnT[:, dt * T + tc * 512:dt * T + (tc + 1) * 512],
                                           start=(dt == 0), stop=(dt == DT - 1))
                        ins.then_inc(st, 1)
        ST += NU

        @blk.vector
        def _(v):
            v.wait_ge(sv, sv_q)
            for u in range(NU):
                h, tc = u // TC, u % TC
                pq = ps_q[u % 4]
                ra = ropeA[:, (u % 2) * 512:(u % 2 + 1) * 512]
                rb = ropeB[:, (u % 2) * 512:(u % 2 + 1) * 512]
                v.wait_ge(st, st_q + u + 1)
                if u >= 2:
                    v.wait_ge(sv, sv_q + u - 1)
                vjoin(v)
                vchain(v, v.tensor_mul(ra, pq[:], Tcq[:, tc * 512:(tc + 1) * 512]))
                vchain(v, v.tensor_mul(rb[0:64, :], pq[64:128, :], Tsq[0:64, tc * 512:(tc + 1) * 512]))
                vchain(v, v.tensor_mul(rb[64:128, :], pq[0:64, :], Tsq[64:128, tc * 512:(tc + 1) * 512]))
                vjoin(v)
                v.tensor_add(QA[:, h * T + tc * 512:h * T + (tc + 1) * 512], ra, rb).then_inc(sv, 1)
        SV += NU

        wsm_sb = wsm_sb_fwd[0]
        fsc = sb("fsc", [128, 32], F32, c2a)
        hw_tm = sb("hw_tm", [128, 128], BF16, c2a)
        ps_sm = psm("ps_sm", [128, SM], F32, c2a)
        ps_hwt = psm("ps_hwt", [128, 128], BF16, c2a)

        sw_sm, st_sm, ss_sm, sv_sm = SW, ST, SS, SV

        @blk.tensor
        def _(t):
            t.wait_ge(sw, sw_sm)
            for i in range(T8):
                if i >= 1:
                    t.wait_ge(sv, sv_sm + i)
                ins = None
                for dt in range(DT):
                    ins = t.matmul(ps_sm[:], xnT[:, dt * T + i * 128:dt * T + i * 128 + 128],
                                   wsm_sb[:, dt * SM:(dt + 1) * SM],
                                   start=(dt == 0), stop=(dt == DT - 1))
                ins.then_inc(st, 1)
                t.wait_ge(sv, sv_sm + i + 1)
                t.transpose(ps_hwt[:], hw_tm[:], ident_s[:]).then_inc(st, 1)
        ST += 2 * T8

        @blk.scalar
        def _(s):
            for i in range(T8):
                s.wait_ge(st, st_sm + 2 * i + 1)
                s.activation(fsc[:, 0:8], ps_sm[:, 0:8], AF.Exp, scale=1.0).then_inc(ss, 1)
                s.activation(fsc[:, 8:9], ps_sm[:, 8:9], AF.Sigmoid,
                             bias=bthr_s[:, 0:1], scale=1.0).then_inc(ss, 1)
                s.wait_ge(st, st_sm + 2 * i + 2)
                s.copy(hwT[:, i * 128:(i + 1) * 128], ps_hwt[:]).then_inc(ss, 1)
        SS += 3 * T8

        @blk.vector
        def _(v):
            for i in range(T8):
                v.wait_ge(ss, ss_sm + 3 * i + 2)
                if i >= 1:
                    v.wait_ge(st, st_sm + 2 * i)
                gsum, thr, iz, winv = fsc[:, 9:10], fsc[:, 10:11], fsc[:, 11:12], fsc[:, 12:13]
                gate = fsc[:, 16:24]
                ops = [
                    lambda: v.reduce_sum(gsum, fsc[:, 0:8], axis=AX.X),
                    lambda: v.reciprocal(gsum, gsum),
                    lambda: v.tensor_scalar(gate, fsc[:, 0:8], gsum[:, 0:1], None, ALU.mult),
                    lambda: v.tensor_scalar(thr, fsc[:, 8:9], 0.125, None, ALU.mult),
                    lambda: v.tensor_scalar(gate, gate, thr[:, 0:1], None, ALU.subtract),
                    lambda: v.tensor_relu(gate, gate),
                    lambda: v.reduce_sum(gsum, gate, axis=AX.X),
                    lambda: v.tensor_scalar(iz, gsum, 0.0, None, ALU.is_le),
                    lambda: v.tensor_add(gsum, gsum, iz),
                    lambda: v.reciprocal(winv, gsum),
                    lambda: v.tensor_scalar(gate, gate, 2.0, None, ALU.mult),
                    lambda: v.tensor_scalar(gate, gate, winv[:, 0:1], None, ALU.mult),
                ]
                for op in ops:
                    vjoin(v)
                    vchain(v, op())
                vjoin(v)
                ins = None
                for e in range(8):
                    ins = v.tensor_scalar(hw_tm[:, e * 16:(e + 1) * 16],
                                          ps_sm[:, 9 + e * 16:9 + (e + 1) * 16],
                                          gate[:, e:e + 1], None, ALU.mult)
                ins.then_inc(sv, 1)
        SV += T8

        c2a.close()

        c2b = c1.enter_context(contextlib.ExitStack())
        wbt_sb = sb("wbt_sb", [128, 2 * DT * DOC], BF16, c2b)
        bflat_sb = sb("bflat_sb", [128, D], BF16, c2b)
        bbase_s = sb("bbase_s", [1, D], BF16, c2b)
        fstage = sb("fstage", [128, 2 * DOC], BF16, c2b)
        ps_b = [psm(f"ps_b{i}", [128, DOC], F32, c2b) for i in range(2)]

        sw_b, st_b, ss_b, sx_b, sv_b = SW, ST, SS, SX, SV

        @blk.gpsimd
        def _(gp):
            gp.wait_ge(st, st_b)
            gp.wait_ge(sv, sv_b)
            gp.wait_ge(ss, ss_b)
            gp.dma_start(out=bflat_sb[:], in_=bflat).then_inc(sw, 16)
            gp.dma_start(out=bbase_s[:], in_=bbase).then_inc(sw, 16)
            for dc in range(NDC):
                if dc >= 2:
                    gp.wait_ge(st, st_b + (dc - 1) * T8)
                gp.dma_start(
                    out=wbt_sb[:, (dc % 2) * DT * DOC:(dc % 2 + 1) * DT * DOC],
                    in_=wbt[dc],
                ).then_inc(sw, 16)
        SW += 16 * (2 + NDC)

        @blk.tensor
        def _(t):
            t.wait_ge(ss, ss_b)
            for dc in range(NDC):
                t.wait_ge(sw, sw_b + 32 + 16 * min(dc + 2, NDC))
                for i in range(T8):
                    k = dc * T8 + i
                    pb = ps_b[k % 2]
                    if k >= 2:
                        t.wait_ge(ss, ss_b + k - 1)
                    for dt in range(DT):
                        o = (dc % 2) * DT * DOC + dt * DOC
                        t.matmul(pb[:], xnT[:, dt * T + i * 128:dt * T + i * 128 + 128],
                                 wbt_sb[:, o:o + DOC],
                                 start=(dt == 0), stop=False)
                    t.matmul(pb[:], hwT[:, i * 128:(i + 1) * 128],
                             bflat_sb[:, dc * DOC:(dc + 1) * DOC], start=False, stop=False)
                    t.matmul(pb[:], ones_col[:], bbase_s[0:1, dc * DOC:(dc + 1) * DOC],
                             start=False, stop=True).then_inc(st, 1)
        ST += NDC * T8

        @blk.scalar
        def _(s):
            for k in range(NDC * T8):
                s.wait_ge(st, st_b + k + 1)
                if k >= 2:
                    s.wait_ge([sf0, sf1][k % 2], 16 * (k // 2))
                s.copy(fstage[:, (k % 2) * DOC:(k % 2 + 1) * DOC], ps_b[k % 2][:]).then_inc(ss, 1)
        SS += NDC * T8

        @blk.sync
        def _(sy):
            for dc in range(NDC):
                for i in range(T8):
                    k = dc * T8 + i
                    sy.wait_ge(ss, ss_b + k + 1)
                    sy.dma_start(out=ffn_hbm.ap()[i * 128:(i + 1) * 128, dc * DOC:(dc + 1) * DOC],
                                 in_=fstage[:, (k % 2) * DOC:(k % 2 + 1) * DOC]).then_inc([sf0, sf1][k % 2], 16)
        NT2B = NDC * T8

        c2b.close()
        c1.close()

        c3 = es.enter_context(contextlib.ExitStack())
        kvlT = sb("kvlT", [128, LT * S], BF16, c3)
        wkv_sb = sb("wkv_sb", [128, 2 * 2 * LT * 128], BF16, c3)
        KT_sb = sb("KT_sb", [128, 2 * S], BF16, c3)
        V_sb = sb("V_sb", [128, 2 * S], BF16, c3)
        expS = sb("expS", [128, 4 * 512], BF16, c3)
        l_part = sb("l_part", [128, 1024], F32, c3)
        kvlbar = sb("kvlbar", [128, LT], BF16, c3)
        kvlbar_f = sb("kvlbar_f", [128, LT], F32, c3)
        kbar_f = sb("kbar_f", [128, 1], F32, c3)
        sums_s = sb("sums_s", [1, 512], F32, c3)
        krA = sb("krA", [128, 512], F32, c3)
        krB = sb("krB", [128, 512], F32, c3)
        rbc = sb("rbc", [128, 2 * 512], F32, c3)
        ps_k = [psm("ps_k0", [128, 512], F32, c3)]
        ps_v = psm("ps_v", [128, 512], F32, c3)
        ps_sp = [psm(f"ps_sp{i}", [128, 1024], F32, c3) for i in range(2)]
        ps_pv0 = psm("ps_pv0", [128, 512], F32, c3)
        ps_m = psm("ps_m", [128, 512], F32, c3)

        sx_kvl, sv_p3, ss_p3, sw_p3 = SX, SV, SS, SW
        st_p2 = ST

        @blk.sync
        def _(sy):
            sy.wait_ge(scc, SCC)
            sy.wait_ge(st, st_p2)
            sy.wait_ge(sv, sv_p3)
            sy.wait_ge(ss, ss_p3)
            for r in range(G):
                for lt in range(LT):
                    sy.dma_start(out=kvlT[:, lt * S + r * T:lt * S + (r + 1) * T],
                                 in_=cc_out.ap()[r, lt]).then_inc(sx, 16)
        SX += 16 * G * LT
        sx_kv_done = SX

        @blk.vector
        def _(v):
            v.wait_ge(sx, sx_kv_done)
            v.wait_ge(sf0, 16 * ((NT2B + 1) // 2))
            v.wait_ge(sf1, 16 * (NT2B // 2))
            for lt in range(LT):
                vchain(v, v.reduce_sum(kvlbar_f[:, lt:lt + 1], kvlT[:, lt * S:(lt + 1) * S], axis=AX.X))
            vjoin(v)
            vchain(v, v.tensor_scalar(kvlbar_f[:], kvlbar_f[:], 1.0 / S, None, ALU.mult))
            vjoin(v)
            v.tensor_copy(kvlbar[:], kvlbar_f[:]).then_inc(sv, 1)
        SV += 1
        sv_kvb = SV

        @blk.gpsimd
        def _(gp):
            gp.wait_ge(st, st_p2)
            gp.wait_ge(ss, ss_p3)
            gp.wait_ge(sf0, 16 * ((NT2B + 1) // 2))
            gp.wait_ge(sf1, 16 * (NT2B // 2))
            for h in range(H):
                if h >= 2:
                    gp.wait_ge(stk, (h - 1) * PH3)
                base = (h % 2) * 2 * LT * 128
                gp.dma_start(out=wkv_sb[:, base:base + LT * 128], in_=wk[h]).then_inc(sw, 16)
                gp.dma_start(out=wkv_sb[:, base + LT * 128:base + 2 * LT * 128],
                             in_=wv[h]).then_inc(sw, 16)
        SW += 32 * H

        VPH = KC + TC
        SPH = 1 + NVU

        sched = [("kbar", 0)]
        for h in range(H + 1):
            build = []
            if h < H:
                build = [("K", h, kc) for kc in range(KC)] + \
                        [("V", h, vq) for vq in range(NVU)]
            attn = []
            if h >= 1:
                for qc in range(TC):
                    for kt in range(KT):
                        attn.append(("S", h - 1, qc, kt))
                        if kt >= 2:
                            attn.append(("PV", h - 1, qc, kt - 2))
                    attn.append(("PV", h - 1, qc, KT - 2))
                    attn.append(("PV", h - 1, qc, KT - 1))
                    attn.append(("tail", h - 1, qc))
            out_u, bi = [], 0
            for n_, a_ in enumerate(attn):
                out_u.append(a_)
                if n_ % 4 == 3 and bi < len(build):
                    out_u.append(build[bi]); bi += 1
            out_u.extend(build[bi:])
            if not attn:
                out_u = list(build)
            if h + 1 < H:
                out_u.append(("kbar", h + 1))
            sched.extend(out_u)

        @blk.tensor
        def _(t):
            t.wait_ge(sv, sv_kvb)
            t.wait_ge(ss, ss_p3)
            for unit in sched:
                kind = unit[0]
                if kind == "kbar":
                    h = unit[1]
                    base = (h % 2) * 2 * LT * 128
                    t.wait_ge(sw, sw_p3 + 32 * min(h + 2, H))
                    if h >= 1:
                        t.wait_ge(ss, ss_p3 + (h - 1) * SPH + 1)
                    if h >= 2:
                        t.wait_ge(sl, (h - 1) * TC * (KT // 2 + 1))
                    ins = None
                    for lt in range(LT):
                        ins = t.matmul(ps_m[:, 0:1], wkv_sb[:, base + lt * 128:base + (lt + 1) * 128],
                                       kvlbar[:, lt:lt + 1], start=(lt == 0), stop=(lt == LT - 1))
                    ins.then_inc(stk, 1)
                elif kind == "K":
                    h, kc = unit[1], unit[2]
                    base = (h % 2) * 2 * LT * 128
                    g = h * KC + kc
                    pk = ps_k[0]
                    if g >= 1:
                        t.wait_ge(svk, g)
                    ins = None
                    for lt in range(LT):
                        ins = t.matmul(pk[:], wkv_sb[:, base + lt * 128:base + (lt + 1) * 128],
                                       kvlT[:, lt * S + kc * 512:lt * S + (kc + 1) * 512],
                                       start=(lt == 0), stop=(lt == LT - 1))
                    ins.then_inc(stk, 1)
                elif kind == "V":
                    h, vq = unit[1], unit[2]
                    base = (h % 2) * 2 * LT * 128
                    gv = h * NVU + vq
                    if gv >= 1:
                        t.wait_ge(ss, ss_p3 + ((gv - 1) // NVU) * SPH + 1 + ((gv - 1) % NVU) + 1)
                    ins = None
                    for j in range(4):
                        tt = vq * 4 + j
                        for lt in range(LT):
                            ins = t.matmul(ps_v[:, j * 128:(j + 1) * 128],
                                           kvlT[:, lt * S + tt * 128:lt * S + (tt + 1) * 128],
                                           wkv_sb[:, base + LT * 128 + lt * 128:base + LT * 128 + (lt + 1) * 128],
                                           start=(lt == 0), stop=(lt == LT - 1))
                    ins.then_inc(stk, 1)
                elif kind == "S":
                    h, qc, kt = unit[1], unit[2], unit[3]
                    hb = h % 2
                    u = h * TC + qc
                    pss = ps_sp[(kt // 2) % 2][:, (kt % 2) * 512:(kt % 2 + 1) * 512]
                    if kt >= 4 and kt % 2 == 0:
                        t.wait_ge(se, u * (KT // 2) + kt // 2 - 1)
                    if kt % 4 == 0:
                        t.wait_ge(svk, h * KC + (kt // 4) + 1)
                    ins = t.matmul(pss, KT_sb[:, hb * S + kt * 128:hb * S + (kt + 1) * 128],
                                   QA[:, h * T + qc * 512:h * T + (qc + 1) * 512],
                                   start=True, stop=True)
                    if kt % 2 == 1:
                        ins.then_inc(sts, 1)
                elif kind == "PV":
                    h, qc, kt = unit[1], unit[2], unit[3]
                    hb = h % 2
                    u = h * TC + qc
                    if kt == 0 and u >= 1:
                        t.wait_ge(sv, sv_p3 + 1 + u)
                    if kt % 2 == 0:
                        t.wait_ge(se, u * (KT // 2) + kt // 2 + 1)
                    if kt % 4 == 0:
                        t.wait_ge(ss, ss_p3 + h * SPH + 1 + (kt // 4) + 1)
                    ins = t.matmul(ps_pv0[:], V_sb[:, hb * S + kt * 128:hb * S + (kt + 1) * 128],
                                   expS[:, (kt % 4) * 512:(kt % 4 + 1) * 512],
                                   start=(kt == 0), stop=(kt == KT - 1))
                    if kt % 2 == 1:
                        ins.then_inc(stp, 1)
                else:
                    h, qc = unit[1], unit[2]
                    u = h * TC + qc
                    t.wait_ge(sl, u * (KT // 2 + 1) + KT // 2)
                    if h + 1 <= H - 1:
                        t.wait_ge(ss, ss_p3 + (h + 1) * SPH + 1)
                    if u >= 1:
                        t.wait_ge(sl, u * (KT // 2 + 1))
                    t.matmul(ps_m[0:1, :], ones128f[:], l_part[:, 0:512], start=True, stop=False)
                    t.matmul(ps_m[0:1, :], ones128f[:], l_part[:, 512:1024], start=False, stop=True).then_inc(stp, 1)
                    t.wait_ge(sb2, u + 1)
                    t.matmul(ps_m[:, 0:512], ones1f[:], sums_s[:], start=True, stop=True).then_inc(stp, 1)

        @blk.scalar
        def _(s):
            s.wait_ge(sf0, 16 * ((NT2B + 1) // 2))
            s.wait_ge(sf1, 16 * (NT2B // 2))
            for unit in sched:
                kind = unit[0]
                if kind == "kbar":
                    h = unit[1]
                    s.wait_ge(stk, h * PH3 + 1)
                    if h >= 1:
                        s.wait_ge(svk, h * KC)
                    s.copy(kbar_f[:], ps_m[:, 0:1]).then_inc(ss, 1)
                elif kind == "V":
                    h, vq = unit[1], unit[2]
                    hb = h % 2
                    s.wait_ge(stk, h * PH3 + 1 + KC + vq + 1)
                    if h >= 2:
                        s.wait_ge(stp, (h - 1) * TC * (KT // 2 + 2))
                    s.copy(V_sb[:, hb * S + vq * 512:hb * S + (vq + 1) * 512], ps_v[:]).then_inc(ss, 1)
                elif kind == "S":
                    h, qc, kt = unit[1], unit[2], unit[3]
                    if kt % 2 == 0:
                        continue
                    u = h * TC + qc
                    j = kt // 2
                    s.wait_ge(sts, u * (KT // 2) + j + 1)
                    if j >= 2:
                        s.wait_ge(stp, u * (KT // 2 + 2) + j - 1)
                        s.wait_ge(sl, u * (KT // 2 + 1) + j - 1)
                    s.activation(expS[:, ((j % 2) * 1024):((j % 2) * 1024 + 1024)],
                                 ps_sp[(j % 2)][:], AF.Exp, scale=1.0).then_inc(se, 1)
                elif kind == "tail":
                    h, qc = unit[1], unit[2]
                    u = h * TC + qc
                    s.wait_ge(stp, u * (KT // 2 + 2) + KT // 2 + 1)
                    if u >= 1:
                        s.wait_ge(stp, (u - 1) * (KT // 2 + 2) + KT // 2 + 2)
                    s.copy(sums_s[:], ps_m[0:1, :]).then_inc(sb2, 1)

        @blk.vector
        def _(v):
            v.wait_ge(sf0, 16 * ((NT2B + 1) // 2))
            v.wait_ge(sf1, 16 * (NT2B // 2))
            for unit in sched:
                kind = unit[0]
                if kind == "K":
                    h, kc = unit[1], unit[2]
                    hb = h % 2
                    g = h * KC + kc
                    pk = ps_k[0]
                    v.wait_ge(stk, h * PH3 + 1 + kc + 1)
                    if g > 0:
                        v.wait_ge(svk, g)
                    if kc == 0:
                        v.wait_ge(ss, ss_p3 + h * SPH + 1)
                        if h >= 2:
                            v.wait_ge(sts, (h - 1) * TC * (KT // 2))
                    vjoin(v)
                    vchain(v, v.tensor_mul(krA[:], pk[:], Tck[:, kc * 512:(kc + 1) * 512]))
                    vchain(v, v.tensor_mul(krB[0:64, :], pk[64:128, :], Tsk[0:64, kc * 512:(kc + 1) * 512]))
                    vchain(v, v.tensor_mul(krB[64:128, :], pk[0:64, :], Tsk[64:128, kc * 512:(kc + 1) * 512]))
                    vjoin(v)
                    v.scalar_tensor_tensor(KT_sb[:, hb * S + kc * 512:hb * S + (kc + 1) * 512],
                                           krA[:], kbar_f[:, 0:1], krB[:],
                                           ALU.subtract, ALU.add).then_inc(svk, 1)
                elif kind == "S":
                    h, qc, kt = unit[1], unit[2], unit[3]
                    if kt % 2 == 0:
                        continue
                    u = h * TC + qc
                    j = kt // 2
                    v.wait_ge(se, u * (KT // 2) + j + 1)
                    exs = expS[:, (j % 2) * 1024:(j % 2) * 1024 + 1024]
                    if j == 0:
                        if u >= 1:
                            v.wait_ge(stp, (u - 1) * (KT // 2 + 2) + KT // 2 + 1)
                        v.tensor_scalar(l_part[:], exs, 1.0, None, ALU.mult).then_inc(sl, 1)
                    else:
                        v.wait_ge(sl, u * (KT // 2 + 1) + j)
                        v.tensor_add(l_part[:], l_part[:], exs).then_inc(sl, 1)
                elif kind == "tail":
                    h, qc = unit[1], unit[2]
                    u = h * TC + qc
                    v.wait_ge(stp, u * (KT // 2 + 2) + KT // 2 + 2)
                    if u >= 2:
                        v.wait_ge(sv, sv_p3 + 1 + (u - 2) + 1)
                    v.reciprocal(rbc[:, (u % 2) * 512:(u % 2 + 1) * 512],
                                 ps_m[:, 0:512]).then_inc(sl, 1)
                    v.wait_ge(sl, u * (KT // 2 + 1) + KT // 2 + 1)
                    v.tensor_mul(QA[:, h * T + qc * 512:h * T + (qc + 1) * 512],
                                 ps_pv0[:], rbc[:, (u % 2) * 512:(u % 2 + 1) * 512]).then_inc(sv, 1)
        SV += H * TC
        sv_attn = SV

        c3.close()

        c4 = es.enter_context(contextlib.ExitStack())
        wo_sb = sb("wo_sb", [128, 2 * DT * DOC4], BF16, c4)
        xres = sb("xres", [128, 2 * DOC4], F32, c4)
        fres = sb("fres", [128, 2 * DOC4], BF16, c4)
        ostage = sb("ostage", [128, 2 * DOC4], F32, c4)
        ps_o = [psm(f"ps_o{i}", [128, DOC4], F32, c4) for i in range(2)]

        sv_p4, sw_p4, st_p4, sx_p4 = SV, SW, ST, SX

        @blk.gpsimd
        def _(gp):
            gp.wait_ge(sv, sv_attn)
            gp.wait_ge(svk, H * KC)
            for dc in range(NDC4):
                if dc >= 2:
                    gp.wait_ge(st, st_p4 + (dc - 1) * T8)
                gp.dma_start(
                    out=wo_sb[:, (dc % 2) * DT * DOC4:(dc % 2 + 1) * DT * DOC4],
                    in_=wo[dc],
                ).then_inc(sw, 16)
        SW += 16 * NDC

        @blk.sync
        def _(sy):
            sy.wait_ge(sv, sv_attn)
            for dc in range(NDC4):
                for i in range(T8):
                    k = dc * T8 + i
                    if k >= 2:
                        sy.wait_ge(sv, sv_p4 + k - 1)
                    sy.dma_start(out=xres[:, (k % 2) * DOC4:(k % 2 + 1) * DOC4],
                                 in_=x_own[i * 128:(i + 1) * 128, dc * DOC4:(dc + 1) * DOC4]).then_inc(sx, 16)
                    sy.dma_start(out=fres[:, (k % 2) * DOC4:(k % 2 + 1) * DOC4],
                                 in_=ffn_hbm.ap()[i * 128:(i + 1) * 128, dc * DOC4:(dc + 1) * DOC4]).then_inc(sx, 16)
        SX += 32 * NDC4 * T8

        @blk.tensor
        def _(t):
            t.wait_ge(sv, sv_attn)
            for dc in range(NDC4):
                t.wait_ge(sw, sw_p4 + 16 * min(dc + 2, NDC4))
                for i in range(T8):
                    k = dc * T8 + i
                    po = ps_o[k % 2]
                    if k >= 2:
                        t.wait_ge(sv, sv_p4 + k - 1)
                    ins = None
                    for h in range(DT):
                        o = (dc % 2) * DT * DOC4 + h * DOC4
                        ins = t.matmul(po[:], QA[:, h * T + i * 128:h * T + i * 128 + 128],
                                       wo_sb[:, o:o + DOC4], start=(h == 0), stop=(h == DT - 1))
                    ins.then_inc(st, 1)
        ST += NDC4 * T8

        @blk.vector
        def _(v):
            for k in range(NDC4 * T8):
                v.wait_ge(st, st_p4 + k + 1)
                v.wait_ge(sx, sx_p4 + 32 * min(k + 2, NDC4 * T8))
                if k >= 2:
                    v.wait_ge([so, so1][k % 2], 16 * (k // 2))
                osl = ostage[:, (k % 2) * DOC4:(k % 2 + 1) * DOC4]
                vjoin(v)
                vchain(v, v.tensor_add(osl, ps_o[k % 2][:], xres[:, (k % 2) * DOC4:(k % 2 + 1) * DOC4]))
                vjoin(v)
                v.tensor_add(osl, osl, fres[:, (k % 2) * DOC4:(k % 2 + 1) * DOC4]).then_inc(sv, 1)
        SV += NDC4 * T8

        @blk.scalar
        def _(s):
            for dc in range(NDC4):
                for i in range(T8):
                    k = dc * T8 + i
                    s.wait_ge(sv, sv_p4 + k + 1)
                    s.dma_start(out=out[i * 128:(i + 1) * 128, dc * DOC4:(dc + 1) * DOC4],
                                in_=ostage[:, (k % 2) * DOC4:(k % 2 + 1) * DOC4]).then_inc([so, so1][k % 2], 16)
            s.wait_ge(so, 16 * ((NDC4 * T8 + 1) // 2))
            s.wait_ge(so1, 16 * ((NDC4 * T8) // 2))

        c4.close()

    return nc


def _host_layouts(inputs, cfg: CFG):
    D, T, S, ER = cfg.D, cfg.T, cfg.S, cfg.ER
    f32 = np.float32
    W_gate = np.asarray(inputs["L0_ffn_W_gate"])
    W_thr = np.asarray(inputs["L0_ffn_W_thr"])
    A_w = np.asarray(inputs["L0_ffn_A_w"])
    B_w = np.asarray(inputs["L0_ffn_B_w"])
    DT, LT, H, NDC, DOC, SM = cfg.DT, cfg.LT, cfg.H, cfg.NDC, cfg.DOC, cfg.SM

    def tile_din(w, np_, cw):
        return np.ascontiguousarray(
            np.asarray(w, f32).reshape(DT, 128, np_, cw).transpose(2, 1, 0, 3).reshape(np_, 128, DT * cw))

    def tile_lat(w, nh):
        return np.ascontiguousarray(
            np.asarray(w, f32).reshape(LT, 128, nh, 128).transpose(2, 1, 0, 3).reshape(nh, 128, LT * 128))

    wsm_flat = np.concatenate(
        [W_gate.T, W_thr[:, None], np.transpose(A_w, (2, 0, 1)).reshape(D, ER)], axis=1)
    common = {
        "wq": tile_din(inputs["L0_attn_Wq"], H // 2, 256),
        "wc": np.ascontiguousarray(inputs["L0_attn_Wc"], f32),
        "wk": tile_lat(inputs["L0_attn_Wk"], H),
        "wv": tile_lat(inputs["L0_attn_Wv"], H),
        "wo": tile_din(inputs["L0_attn_Wo"], cfg.NDC4, cfg.DOC4),
        "wbt": tile_din(np.asarray(inputs["L0_ffn_W_base"]).T, NDC, DOC),
        "wsm": np.ascontiguousarray(
            np.asarray(wsm_flat, f32).reshape(DT, 128, SM).transpose(1, 0, 2).reshape(128, DT * SM)),
        "bflat": np.ascontiguousarray(np.transpose(B_w, (0, 2, 1)).reshape(ER, D), f32),
        "bbase": np.ascontiguousarray(np.asarray(inputs["L0_ffn_b_base"], f32)[None, :]),
        "bthr": np.ascontiguousarray(np.broadcast_to(
            np.asarray(inputs["L0_ffn_b_thr"], f32).reshape(1, 1), (128, 1))),
        "posk": np.ascontiguousarray(np.broadcast_to(np.arange(S, dtype=f32)[None, :], (128, S))),
        "dind": np.ascontiguousarray((np.arange(128, dtype=f32) % 64)[:, None]),
        "ident": np.eye(128, dtype=f32),
    }
    x = np.asarray(inputs["L0_attn_hidden_states"], f32)
    in_maps = []
    for c in range(8):
        b, qq = c // 4, c % 4
        m = dict(common)
        m["x_own"] = np.ascontiguousarray(x[b, qq * T:(qq + 1) * T, :])
        m["posq"] = np.ascontiguousarray(np.broadcast_to(
            np.arange(qq * T, (qq + 1) * T, dtype=f32)[None, :], (128, T)))
        in_maps.append(m)
    return in_maps


def kernel(**inputs):
    cfg = CFG()
    nc = build_program(cfg)
    in_maps = _host_layouts(inputs, cfg)
    res = run_bass_kernel_spmd(nc, in_maps, core_ids=list(range(8)))
    out = np.empty((2, 4 * cfg.T, cfg.D), dtype=np.float32)
    for c in range(8):
        b, qq = c // 4, c % 4
        out[b, qq * cfg.T:(qq + 1) * cfg.T, :] = res.results[c]["out"]
    return out


# version 17
# speedup vs baseline: 1.1479x; 1.1181x over previous
import contextlib
import math
import numpy as np
import concourse.bass as bass
import concourse.mybir as mybir
import concourse.bass_utils as _bu
from concourse.bass_utils import run_bass_kernel_spmd

if not getattr(_bu, "_ldw_patched", False):
    _orig_run_command = _bu.run_command

    def _run_command_ldw(cmd, *a, **kw):
        if isinstance(cmd, (list, tuple)):
            cmd = [c.replace("--enable-ldw-opt=false", "--enable-ldw-opt=true")
                   if isinstance(c, str) else c for c in cmd]
        return _orig_run_command(cmd, *a, **kw)

    _bu.run_command = _run_command_ldw
    _bu._ldw_patched = True

F32 = mybir.dt.float32
BF16 = mybir.dt.bfloat16
AF = mybir.ActivationFunctionType
ALU = mybir.AluOpType
AX = mybir.AxisListType

PI = math.pi
LN_THETA = math.log(10000.0)


class CFG:
    def __init__(self, D=4096, L=512, H=32, T=1024):
        self.D, self.L, self.H, self.T = D, L, H, T
        self.S = 4 * T
        self.E, self.R = 8, 16
        self.ER = self.E * self.R
        self.DT = D // 128
        self.LT = L // 128
        self.TC = T // 512
        self.T8 = T // 128
        self.KT = self.S // 128
        self.KC = self.S // 512
        self.G = 4
        self.DOC = 256
        self.NDC = D // self.DOC
        self.SM = 9 + self.ER


def build_program(cfg: CFG):
    D, L, H, T, S = cfg.D, cfg.L, cfg.H, cfg.T, cfg.S
    DT, LT, TC, T8, KT, KC = cfg.DT, cfg.LT, cfg.TC, cfg.T8, cfg.KT, cfg.KC
    ER, DOC, NDC, G, SM = cfg.ER, cfg.DOC, cfg.NDC, cfg.G, cfg.SM
    SQ = 1.0 / math.sqrt(128.0)
    NVU = KT // 4
    PH3 = 1 + KC + NVU

    nc = bass.Bass("TRN2", target_bir_lowering=False, debug=False)

    x_own = nc.dram_tensor("x_own", [T, D], F32, kind="ExternalInput").ap()
    wq = nc.dram_tensor("wq", [H // 2, 128, DT * 256], F32, kind="ExternalInput").ap()
    wc = nc.dram_tensor("wc", [D, L], F32, kind="ExternalInput").ap()
    wk = nc.dram_tensor("wk", [H, 128, LT * 128], F32, kind="ExternalInput").ap()
    wv = nc.dram_tensor("wv", [H, 128, LT * 128], F32, kind="ExternalInput").ap()
    wo = nc.dram_tensor("wo", [NDC, 128, DT * DOC], F32, kind="ExternalInput").ap()
    wbt = nc.dram_tensor("wbt", [NDC, 128, DT * DOC], F32, kind="ExternalInput").ap()
    wsm = nc.dram_tensor("wsm", [128, DT * SM], F32, kind="ExternalInput").ap()
    bflat = nc.dram_tensor("bflat", [ER, D], F32, kind="ExternalInput").ap()
    bbase = nc.dram_tensor("bbase", [1, D], F32, kind="ExternalInput").ap()
    bthr = nc.dram_tensor("bthr", [128, 1], F32, kind="ExternalInput").ap()
    posq = nc.dram_tensor("posq", [128, T], F32, kind="ExternalInput").ap()
    posk = nc.dram_tensor("posk", [128, S], F32, kind="ExternalInput").ap()
    dind = nc.dram_tensor("dind", [128, 1], F32, kind="ExternalInput").ap()
    ident_in = nc.dram_tensor("ident", [128, 128], F32, kind="ExternalInput").ap()
    out = nc.dram_tensor("out", [T, D], F32, kind="ExternalOutput").ap()

    cc_in = nc.dram_tensor("cc_in", [LT, 128, T], BF16)
    cc_out = nc.dram_tensor("cc_out", [G, LT, 128, T], BF16)
    ffn_hbm = nc.dram_tensor("ffn_hbm", [T, D], BF16)

    es = contextlib.ExitStack()

    with es:
        def sb(name, shape, dt, stack=es):
            return stack.enter_context(nc.sbuf_tensor(name, shape, dt))

        def psm(name, shape, dt, stack):
            return stack.enter_context(nc.psum_tensor(name, shape, dt))

        ones_col = sb("ones_col", [1, 128], BF16)
        ones128f = sb("ones128f", [128, 1], F32)
        ones1f = sb("ones1f", [1, 128], F32)
        Tck = sb("Tck", [128, S], BF16)
        Tsk = sb("Tsk", [128, S], BF16)
        Tcq = sb("Tcq", [128, T], BF16)
        Tsq = sb("Tsq", [128, T], BF16)
        QA = sb("QA", [128, H * T], BF16)
        hwT = sb("hwT", [128, T], BF16)
        ident_s = sb("ident_s", [128, 128], BF16)
        bthr_s = sb("bthr_s", [128, 1], F32)

        sx = es.enter_context(nc.semaphore("sx"))
        sw = es.enter_context(nc.semaphore("sw"))
        scc = es.enter_context(nc.semaphore("scc"))
        st = es.enter_context(nc.semaphore("st"))
        stk = es.enter_context(nc.semaphore("stk"))
        sts = es.enter_context(nc.semaphore("sts"))
        stp = es.enter_context(nc.semaphore("stp"))
        se = es.enter_context(nc.semaphore("se"))
        sl = es.enter_context(nc.semaphore("sl"))
        sv = es.enter_context(nc.semaphore("sv"))
        ss = es.enter_context(nc.semaphore("ss"))
        so = es.enter_context(nc.semaphore("so"))
        sm0 = es.enter_context(nc.semaphore("sm0"))
        sf0 = es.enter_context(nc.semaphore("sf0"))
        sf1 = es.enter_context(nc.semaphore("sf1"))
        so1 = es.enter_context(nc.semaphore("so1"))
        svc = es.enter_context(nc.semaphore("svc"))
        sb2 = es.enter_context(nc.semaphore("sb2"))
        svk = es.enter_context(nc.semaphore("svk"))
        VC = [0]

        def vchain(v, ins):
            ins.then_inc(svc, 1)
            VC[0] += 1

        def vjoin(v):
            v.wait_ge(svc, VC[0])

        blk = es.enter_context(nc.Block())

        SX = SW = SCC = ST = SV = SS = 0

        c1 = es.enter_context(contextlib.ExitStack())
        xnT = sb("xnT", [128, DT * T], BF16, c1)

        c0 = c1.enter_context(contextlib.ExitStack())
        th = sb("th", [128, S], F32, c0)
        pq_s = sb("pq_s", [128, T], F32, c0)
        tk0 = sb("tk0", [128, S], F32, c0)
        ck0 = sb("ck0", [128, S], F32, c0)
        invf = sb("invf", [128, 1], F32, c0)

        @blk.sync
        def _(sy):
            sy.dma_start(out=th[:], in_=posk).then_inc(sx, 16)
            sy.dma_start(out=pq_s[:], in_=posq).then_inc(sx, 16)
            sy.dma_start(out=invf[:], in_=dind).then_inc(sx, 16)
            sy.dma_start(out=bthr_s[:], in_=bthr).then_inc(sx, 16)
        SX += 64

        @blk.gpsimd
        def _(gp):
            gp.dma_start(out=ident_s[:], in_=ident_in).then_inc(sw, 16)
            gp.memset(ones_col[:], 1.0)
            gp.memset(ones128f[:], 1.0)
            gp.memset(ones1f[:], 1.0).then_inc(sm0, 1)
        SW += 16

        @blk.scalar
        def _(s):
            s.wait_ge(sx, 64)
            s.activation(invf[:], invf[:], AF.Exp, scale=-LN_THETA / 64.0).then_inc(ss, 1)
        SS += 1

        def emit_wrap(v, pos_sl, tk, ck):
            vchain(v, v.tensor_scalar(tk, pos_sl, 1.0 / (2 * PI), None, ALU.mult))
            for b in [512.0, 256.0, 128.0, 64.0, 32.0, 16.0, 8.0, 4.0, 2.0, 1.0]:
                vjoin(v)
                vchain(v, v.tensor_scalar(ck, tk, b, None, ALU.is_ge))
                vjoin(v)
                vchain(v, v.scalar_tensor_tensor(tk, ck, -b, tk, ALU.mult, ALU.add))
            vjoin(v)
            v.tensor_scalar(tk, tk, 2 * PI, -PI, ALU.mult, ALU.add).then_inc(sv, 1)

        def emit_tables(pos_ap, width, Tc, Ts, fold, SV, SS):
            ss0, sv0 = SS, SV
            tk, ck = tk0[:, :width], ck0[:, :width]

            @blk.vector
            def _(v):
                v.wait_ge(ss, ss0)
                vjoin(v)
                vchain(v, v.tensor_scalar(pos_ap, pos_ap, invf[:, 0:1], None, ALU.mult))
                vjoin(v)
                emit_wrap(v, pos_ap, tk, ck)

            @blk.scalar
            def _(s):
                s.wait_ge(sv, sv0 + 1)
                s.activation(Ts[:, :width], tk, AF.Sin, scale=1.0).then_inc(ss, 1)

            @blk.vector
            def _(v):
                v.wait_ge(ss, ss0 + 1)
                v.tensor_scalar(Ts[0:64, :width], Ts[0:64, :width], fold, None, ALU.mult)
                v.tensor_scalar(Ts[64:128, :width], Ts[64:128, :width], -fold, None, ALU.mult)
                vjoin(v)
                vchain(v, v.tensor_scalar(pos_ap, pos_ap, PI / 2, None, ALU.add))
                vjoin(v)
                emit_wrap(v, pos_ap, tk, ck)

            @blk.scalar
            def _(s):
                s.wait_ge(sv, sv0 + 2)
                s.activation(Tc[:, :width], tk, AF.Sin, scale=1.0).then_inc(ss, 1)

            @blk.vector
            def _(v):
                v.wait_ge(ss, ss0 + 2)
                v.tensor_scalar(Tc[:, :width], Tc[:, :width], -fold, None, ALU.mult).then_inc(sv, 1)
            return sv0 + 3, ss0 + 2

        SV, SS = emit_tables(th[:], S, Tck, Tsk, 1.0, SV, SS)
        SV, SS = emit_tables(pq_s[:], T, Tcq, Tsq, SQ, SV, SS)
        sv_tab, ss_tab = SV, SS

        c0.close()
        c1p = c1.enter_context(contextlib.ExitStack())
        xst = sb("xst", [128, 2 * D], F32, c1p)
        xn_tm = sb("xn_tm", [128, 2 * D], BF16, c1p)
        stats = sb("stats", [128, 10], F32, c1p)
        ps_tp = [psm(f"ps_tp{i}", [128, 512], BF16, c1p) for i in range(2)]

        NQ4 = DT // 4
        sx_p1, ss_p1, sv_p1, st_p1 = SX, SS, SV, ST

        @blk.sync
        def _(sy):
            for i in range(T8):
                if i == 0:
                    sy.wait_ge(ss, ss_tab)
                if i >= 2:
                    sy.wait_ge(ss, ss_p1 + 3 * (i - 1))
                sy.dma_start(out=xst[:, (i % 2) * D:(i % 2 + 1) * D],
                             in_=x_own[i * 128:(i + 1) * 128, :]).then_inc(sx, 16)
        SX += 16 * T8

        PCH = 2 + NQ4

        def _evac_val(k):
            i, q4 = k // NQ4, k % NQ4
            if i + 1 < T8:
                return sv_p1 + 2 * (i + 2) + i * NQ4 + q4 + 1
            return sv_p1 + 2 * T8 + (T8 - 1) * NQ4 + q4 + 1

        def _emit_evac(v, i, q4):
            k = i * NQ4 + q4
            v.wait_ge(st, st_p1 + k + 1)
            dst = xnT[:].rearrange("p (d t) -> p d t", d=DT, t=T)[
                :, q4 * 4:(q4 + 1) * 4, i * 128:(i + 1) * 128]
            v.tensor_copy(dst, ps_tp[k % 2][:].rearrange(
                "p (d c) -> p d c", d=4, c=128)).then_inc(sv, 1)

        @blk.vector
        def _(v):
            for i in range(T8):
                o = (i % 2) * 5
                mean_, msq, var_, rstd, nb = (stats[:, o + j:o + j + 1] for j in range(5))
                xb = xst[:, (i % 2) * D:(i % 2 + 1) * D]
                v.wait_ge(sx, sx_p1 + 16 * min(i + 2, T8))
                if i >= 2:
                    v.wait_ge(ss, ss_p1 + 3 * (i - 1))
                v.wait_ge(ss, ss_p1 + 3 * i + 1)
                vjoin(v)
                vchain(v, v.tensor_scalar(xn_tm[:, (i % 2) * D:(i % 2 + 1) * D], xb,
                                          1.0 / D, None, ALU.mult, ALU.add, accum_out=mean_))
                vjoin(v)
                vchain(v, v.tensor_mul(var_, mean_, mean_))
                vjoin(v)
                vchain(v, v.tensor_sub(var_, msq, var_))
                vjoin(v)
                vchain(v, v.tensor_scalar(var_, var_, 1e-5, None, ALU.add))
                vjoin(v)
                v.reciprocal(var_, var_).then_inc(sv, 1)
                v.wait_ge(ss, ss_p1 + 3 * i + 2)
                v.scalar_tensor_tensor(nb, mean_, -1.0, rstd, ALU.mult, ALU.mult).then_inc(sv, 1)
                if i >= 1:
                    for q4 in range(NQ4):
                        _emit_evac(v, i - 1, q4)
            for q4 in range(NQ4):
                _emit_evac(v, T8 - 1, q4)
        SV += T8 * PCH

        import math as _math
        RSD = 1.0 / _math.sqrt(D)

        @blk.scalar
        def _(s):
            for i in range(T8):
                o = (i % 2) * 5
                msq_, var_, rstd, nb = (stats[:, o + j:o + j + 1] for j in range(1, 5))
                s.wait_ge(sx, sx_p1 + 16 * min(i + 2, T8))
                if i >= 2:
                    s.wait_ge(sv, sv_p1 + 2 * (i - 2) + max(i - 3, 0) * NQ4 + 2)
                    s.wait_ge(st, st_p1 + (i - 1) * NQ4)
                s.activation(xn_tm[:, (i % 2) * D:(i % 2 + 1) * D],
                             xst[:, (i % 2) * D:(i % 2 + 1) * D], AF.Square,
                             scale=RSD, accum_out=msq_).then_inc(ss, 1)
                s.wait_ge(sv, sv_p1 + 2 * i + max(i - 1, 0) * NQ4 + 1)
                s.activation(rstd, var_, AF.Sqrt).then_inc(ss, 1)
                s.wait_ge(sv, sv_p1 + 2 * i + max(i - 1, 0) * NQ4 + 2)
                if i >= 2:
                    s.wait_ge(st, st_p1 + (i - 1) * NQ4)
                s.activation(xn_tm[:, (i % 2) * D:(i % 2 + 1) * D],
                             xst[:, (i % 2) * D:(i % 2 + 1) * D], AF.Identity,
                             bias=nb, scale=rstd).then_inc(ss, 1)
        SS += 3 * T8

        @blk.tensor
        def _(t):
            t.wait_ge(sw, 16)
            t.wait_ge(sm0, 1)
            for i in range(T8):
                t.wait_ge(ss, ss_p1 + 3 * i + 3)
                xb = xn_tm[:, (i % 2) * D:(i % 2 + 1) * D]
                for q4 in range(NQ4):
                    k = i * NQ4 + q4
                    pst = ps_tp[k % 2]
                    if k >= 2:
                        t.wait_ge(sv, _evac_val(k - 2))
                    ins = None
                    for j in range(4):
                        dt = q4 * 4 + j
                        ins = t.transpose(pst[:, j * 128:(j + 1) * 128],
                                          xb[:, dt * 128:(dt + 1) * 128], ident_s[:])
                    ins.then_inc(st, 1)
        ST += T8 * NQ4
        sv_xnT = SV

        c1p.close()

        c1b = c1.enter_context(contextlib.ExitStack())
        wc_sb = sb("wc_sb", [128, 2 * L], BF16, c1b)
        kv_stage = sb("kv_stage", [128, LT * T], BF16, c1b)
        ps_kv = [psm(f"ps_kv{i}", [128, 512], F32, c1b) for i in range(LT * TC)]

        sw_kv, st_kv, ss_kv = SW, ST, SS

        @blk.gpsimd
        def _(gp):
            gp.wait_ge(sv, sv_xnT)
            for dt in range(DT):
                if dt >= 2:
                    gp.wait_ge(st, st_kv + (dt - 1) * LT * TC)
                gp.dma_start(out=wc_sb[:, (dt % 2) * L:(dt % 2) * L + L],
                             in_=wc[dt * 128:(dt + 1) * 128, :]).then_inc(sw, 16)
        SW += 16 * DT

        @blk.tensor
        def _(t):
            t.wait_ge(sv, sv_xnT)
            for dt in range(DT):
                t.wait_ge(sw, sw_kv + 16 * min(dt + 2, DT))
                for lt in range(LT):
                    for tc in range(TC):
                        t.matmul(ps_kv[lt * TC + tc][:],
                                 wc_sb[:, (dt % 2) * L + lt * 128:(dt % 2) * L + (lt + 1) * 128],
                                 xnT[:, dt * T + tc * 512:dt * T + (tc + 1) * 512],
                                 start=(dt == 0), stop=(dt == DT - 1)).then_inc(st, 1)
        ST += DT * LT * TC

        @blk.scalar
        def _(s):
            s.wait_ge(st, ST)
            for lt in range(LT):
                for tc in range(TC):
                    s.copy(kv_stage[:, lt * T + tc * 512:lt * T + (tc + 1) * 512],
                           ps_kv[lt * TC + tc][:]).then_inc(ss, 1)
        SS += LT * TC

        sw_cc = SW

        @blk.gpsimd
        def _(gp):
            gp.wait_ge(ss, SS)
            for lt in range(LT):
                gp.dma_start(out=cc_in.ap()[lt], in_=kv_stage[:, lt * T:(lt + 1) * T]).then_inc(sw, 16)
            gp.wait_ge(sw, sw_cc + 16 * LT)
            gp.collective_compute(
                "AllGather", ALU.bypass,
                replica_groups=[[0, 1, 2, 3], [4, 5, 6, 7]],
                ins=[cc_in.ap().opt()],
                outs=[cc_out.ap().opt()],
            ).then_inc(scc, 1)
        SW += 16 * LT
        SCC = 1

        c1b.close()

        c2a = c1.enter_context(contextlib.ExitStack())
        wq_sb = sb("wq_sb", [128, 2 * DT * 256], BF16, c2a)
        wsm_sb_fwd = [sb("wsm_sb", [128, DT * SM], BF16, c2a)]
        ropeA = sb("ropeA", [128, 2 * 512], F32, c2a)
        ropeB = sb("ropeB", [128, 2 * 512], F32, c2a)
        ps_q = [psm(f"ps_q{i}", [128, 512], F32, c2a) for i in range(4)]

        NHP = H // 2
        NU = H * TC
        sw_q, st_q, ss_q, sv_q = SW, ST, SS, SV

        @blk.gpsimd
        def _(gp):
            gp.wait_ge(st, st_q)
            gp.wait_ge(ss, ss_q)
            gp.dma_start(out=wsm_sb_fwd[0][:], in_=wsm).then_inc(sw, 16)
            for hp in range(NHP):
                if hp >= 2:
                    gp.wait_ge(st, st_q + (hp - 1) * 2 * TC)
                gp.dma_start(
                    out=wq_sb[:, (hp % 2) * DT * 256:(hp % 2 + 1) * DT * 256],
                    in_=wq[hp],
                ).then_inc(sw, 16)
        SW += 16 * (NHP + 1)

        @blk.tensor
        def _(t):
            t.wait_ge(ss, ss_q)
            for hp in range(NHP):
                t.wait_ge(sw, sw_q + 16 + 16 * min(hp + 2, NHP))
                for h2 in range(2):
                    for tc in range(TC):
                        u = (hp * 2 + h2) * TC + tc
                        pq = ps_q[u % 4]
                        if u >= 4:
                            t.wait_ge(sv, sv_q + (u - 4) + 1)
                        ins = None
                        for dt in range(DT):
                            base = (hp % 2) * DT * 256 + dt * 256 + h2 * 128
                            ins = t.matmul(pq[:], wq_sb[:, base:base + 128],
                                           xnT[:, dt * T + tc * 512:dt * T + (tc + 1) * 512],
                                           start=(dt == 0), stop=(dt == DT - 1))
                        ins.then_inc(st, 1)
        ST += NU

        @blk.vector
        def _(v):
            v.wait_ge(sv, sv_q)
            for u in range(NU):
                h, tc = u // TC, u % TC
                pq = ps_q[u % 4]
                ra = ropeA[:, (u % 2) * 512:(u % 2 + 1) * 512]
                rb = ropeB[:, (u % 2) * 512:(u % 2 + 1) * 512]
                v.wait_ge(st, st_q + u + 1)
                if u >= 2:
                    v.wait_ge(sv, sv_q + u - 1)
                vjoin(v)
                vchain(v, v.tensor_mul(ra, pq[:], Tcq[:, tc * 512:(tc + 1) * 512]))
                vchain(v, v.tensor_mul(rb[0:64, :], pq[64:128, :], Tsq[0:64, tc * 512:(tc + 1) * 512]))
                vchain(v, v.tensor_mul(rb[64:128, :], pq[0:64, :], Tsq[64:128, tc * 512:(tc + 1) * 512]))
                vjoin(v)
                v.tensor_add(QA[:, h * T + tc * 512:h * T + (tc + 1) * 512], ra, rb).then_inc(sv, 1)
        SV += NU

        wsm_sb = wsm_sb_fwd[0]
        fsc = sb("fsc", [128, 32], F32, c2a)
        hw_tm = sb("hw_tm", [128, 128], BF16, c2a)
        ps_sm = psm("ps_sm", [128, SM], F32, c2a)
        ps_hwt = psm("ps_hwt", [128, 128], BF16, c2a)

        sw_sm, st_sm, ss_sm, sv_sm = SW, ST, SS, SV

        @blk.tensor
        def _(t):
            t.wait_ge(sw, sw_sm)
            for i in range(T8):
                if i >= 1:
                    t.wait_ge(sv, sv_sm + i)
                ins = None
                for dt in range(DT):
                    ins = t.matmul(ps_sm[:], xnT[:, dt * T + i * 128:dt * T + i * 128 + 128],
                                   wsm_sb[:, dt * SM:(dt + 1) * SM],
                                   start=(dt == 0), stop=(dt == DT - 1))
                ins.then_inc(st, 1)
                t.wait_ge(sv, sv_sm + i + 1)
                t.transpose(ps_hwt[:], hw_tm[:], ident_s[:]).then_inc(st, 1)
        ST += 2 * T8

        @blk.scalar
        def _(s):
            for i in range(T8):
                s.wait_ge(st, st_sm + 2 * i + 1)
                s.activation(fsc[:, 0:8], ps_sm[:, 0:8], AF.Exp, scale=1.0).then_inc(ss, 1)
                s.activation(fsc[:, 8:9], ps_sm[:, 8:9], AF.Sigmoid,
                             bias=bthr_s[:, 0:1], scale=1.0).then_inc(ss, 1)
                s.wait_ge(st, st_sm + 2 * i + 2)
                s.copy(hwT[:, i * 128:(i + 1) * 128], ps_hwt[:]).then_inc(ss, 1)
        SS += 3 * T8

        @blk.vector
        def _(v):
            for i in range(T8):
                v.wait_ge(ss, ss_sm + 3 * i + 2)
                if i >= 1:
                    v.wait_ge(st, st_sm + 2 * i)
                gsum, thr, iz, winv = fsc[:, 9:10], fsc[:, 10:11], fsc[:, 11:12], fsc[:, 12:13]
                gate = fsc[:, 16:24]
                ops = [
                    lambda: v.reduce_sum(gsum, fsc[:, 0:8], axis=AX.X),
                    lambda: v.reciprocal(gsum, gsum),
                    lambda: v.tensor_scalar(gate, fsc[:, 0:8], gsum[:, 0:1], None, ALU.mult),
                    lambda: v.tensor_scalar(thr, fsc[:, 8:9], 0.125, None, ALU.mult),
                    lambda: v.tensor_scalar(gate, gate, thr[:, 0:1], None, ALU.subtract),
                    lambda: v.tensor_relu(gate, gate),
                    lambda: v.reduce_sum(gsum, gate, axis=AX.X),
                    lambda: v.tensor_scalar(iz, gsum, 0.0, None, ALU.is_le),
                    lambda: v.tensor_add(gsum, gsum, iz),
                    lambda: v.reciprocal(winv, gsum),
                    lambda: v.tensor_scalar(gate, gate, 2.0, None, ALU.mult),
                    lambda: v.tensor_scalar(gate, gate, winv[:, 0:1], None, ALU.mult),
                ]
                for op in ops:
                    vjoin(v)
                    vchain(v, op())
                vjoin(v)
                ins = None
                for e in range(8):
                    ins = v.tensor_scalar(hw_tm[:, e * 16:(e + 1) * 16],
                                          ps_sm[:, 9 + e * 16:9 + (e + 1) * 16],
                                          gate[:, e:e + 1], None, ALU.mult)
                ins.then_inc(sv, 1)
        SV += T8

        c2a.close()

        c2b = c1.enter_context(contextlib.ExitStack())
        wbt_sb = sb("wbt_sb", [128, 2 * DT * DOC], BF16, c2b)
        bflat_sb = sb("bflat_sb", [128, D], BF16, c2b)
        bbase_s = sb("bbase_s", [1, D], BF16, c2b)
        fstage = sb("fstage", [128, 2 * DOC], BF16, c2b)
        ps_b = [psm(f"ps_b{i}", [128, DOC], F32, c2b) for i in range(2)]

        sw_b, st_b, ss_b, sx_b, sv_b = SW, ST, SS, SX, SV

        @blk.gpsimd
        def _(gp):
            gp.wait_ge(st, st_b)
            gp.wait_ge(sv, sv_b)
            gp.wait_ge(ss, ss_b)
            gp.dma_start(out=bflat_sb[:], in_=bflat).then_inc(sw, 16)
            gp.dma_start(out=bbase_s[:], in_=bbase).then_inc(sw, 16)
            for dc in range(NDC):
                if dc >= 2:
                    gp.wait_ge(st, st_b + (dc - 1) * T8)
                gp.dma_start(
                    out=wbt_sb[:, (dc % 2) * DT * DOC:(dc % 2 + 1) * DT * DOC],
                    in_=wbt[dc],
                ).then_inc(sw, 16)
        SW += 16 * (2 + NDC)

        @blk.tensor
        def _(t):
            t.wait_ge(ss, ss_b)
            for dc in range(NDC):
                t.wait_ge(sw, sw_b + 32 + 16 * min(dc + 2, NDC))
                for i in range(T8):
                    k = dc * T8 + i
                    pb = ps_b[k % 2]
                    if k >= 2:
                        t.wait_ge(ss, ss_b + k - 1)
                    for dt in range(DT):
                        o = (dc % 2) * DT * DOC + dt * DOC
                        t.matmul(pb[:], xnT[:, dt * T + i * 128:dt * T + i * 128 + 128],
                                 wbt_sb[:, o:o + DOC],
                                 start=(dt == 0), stop=False)
                    t.matmul(pb[:], hwT[:, i * 128:(i + 1) * 128],
                             bflat_sb[:, dc * DOC:(dc + 1) * DOC], start=False, stop=False)
                    t.matmul(pb[:], ones_col[:], bbase_s[0:1, dc * DOC:(dc + 1) * DOC],
                             start=False, stop=True).then_inc(st, 1)
        ST += NDC * T8

        @blk.scalar
        def _(s):
            for k in range(NDC * T8):
                s.wait_ge(st, st_b + k + 1)
                if k >= 2:
                    s.wait_ge([sf0, sf1][k % 2], 16 * (k // 2))
                s.copy(fstage[:, (k % 2) * DOC:(k % 2 + 1) * DOC], ps_b[k % 2][:]).then_inc(ss, 1)
        SS += NDC * T8

        @blk.sync
        def _(sy):
            for dc in range(NDC):
                for i in range(T8):
                    k = dc * T8 + i
                    sy.wait_ge(ss, ss_b + k + 1)
                    sy.dma_start(out=ffn_hbm.ap()[i * 128:(i + 1) * 128, dc * DOC:(dc + 1) * DOC],
                                 in_=fstage[:, (k % 2) * DOC:(k % 2 + 1) * DOC]).then_inc([sf0, sf1][k % 2], 16)
        NT2B = NDC * T8

        c2b.close()
        c1.close()

        c3 = es.enter_context(contextlib.ExitStack())
        kvlT = sb("kvlT", [128, LT * S], BF16, c3)
        wkv_sb = sb("wkv_sb", [128, 2 * 2 * LT * 128], BF16, c3)
        KT_sb = sb("KT_sb", [128, 2 * S], BF16, c3)
        V_sb = sb("V_sb", [128, 2 * S], BF16, c3)
        expS = sb("expS", [128, 4 * 512], BF16, c3)
        l_part = sb("l_part", [128, 1024], F32, c3)
        kvlbar = sb("kvlbar", [128, LT], BF16, c3)
        kvlbar_f = sb("kvlbar_f", [128, LT], F32, c3)
        kbar_f = sb("kbar_f", [128, 1], F32, c3)
        sums_s = sb("sums_s", [1, 512], F32, c3)
        krA = sb("krA", [128, 512], F32, c3)
        krB = sb("krB", [128, 512], F32, c3)
        rbc = sb("rbc", [128, 2 * 512], F32, c3)
        ps_k = [psm("ps_k0", [128, 512], F32, c3)]
        ps_v = psm("ps_v", [128, 512], F32, c3)
        ps_sp = [psm(f"ps_sp{i}", [128, 1024], F32, c3) for i in range(2)]
        ps_pv0 = psm("ps_pv0", [128, 512], F32, c3)
        ps_m = psm("ps_m", [128, 512], F32, c3)

        sx_kvl, sv_p3, ss_p3, sw_p3 = SX, SV, SS, SW
        st_p2 = ST

        @blk.sync
        def _(sy):
            sy.wait_ge(scc, SCC)
            sy.wait_ge(st, st_p2)
            sy.wait_ge(sv, sv_p3)
            sy.wait_ge(ss, ss_p3)
            for r in range(G):
                for lt in range(LT):
                    sy.dma_start(out=kvlT[:, lt * S + r * T:lt * S + (r + 1) * T],
                                 in_=cc_out.ap()[r, lt]).then_inc(sx, 16)
        SX += 16 * G * LT
        sx_kv_done = SX

        @blk.vector
        def _(v):
            v.wait_ge(sx, sx_kv_done)
            v.wait_ge(sf0, 16 * ((NT2B + 1) // 2))
            v.wait_ge(sf1, 16 * (NT2B // 2))
            for lt in range(LT):
                vchain(v, v.reduce_sum(kvlbar_f[:, lt:lt + 1], kvlT[:, lt * S:(lt + 1) * S], axis=AX.X))
            vjoin(v)
            vchain(v, v.tensor_scalar(kvlbar_f[:], kvlbar_f[:], 1.0 / S, None, ALU.mult))
            vjoin(v)
            v.tensor_copy(kvlbar[:], kvlbar_f[:]).then_inc(sv, 1)
        SV += 1
        sv_kvb = SV

        @blk.gpsimd
        def _(gp):
            gp.wait_ge(st, st_p2)
            gp.wait_ge(ss, ss_p3)
            gp.wait_ge(sf0, 16 * ((NT2B + 1) // 2))
            gp.wait_ge(sf1, 16 * (NT2B // 2))
            for h in range(H):
                if h >= 2:
                    gp.wait_ge(stk, (h - 1) * PH3)
                base = (h % 2) * 2 * LT * 128
                gp.dma_start(out=wkv_sb[:, base:base + LT * 128], in_=wk[h]).then_inc(sw, 16)
                gp.dma_start(out=wkv_sb[:, base + LT * 128:base + 2 * LT * 128],
                             in_=wv[h]).then_inc(sw, 16)
        SW += 32 * H

        VPH = KC + TC
        SPH = 1 + NVU

        sched = [("kbar", 0)]
        for h in range(H + 1):
            build = []
            if h < H:
                build = [("K", h, kc) for kc in range(KC)] + \
                        [("V", h, vq) for vq in range(NVU)]
            attn = []
            if h >= 1:
                for qc in range(TC):
                    for kt in range(KT):
                        attn.append(("S", h - 1, qc, kt))
                        if kt >= 4:
                            attn.append(("PV", h - 1, qc, kt - 4))
                    for kt in range(KT - 4, KT):
                        attn.append(("PV", h - 1, qc, kt))
                    attn.append(("tail", h - 1, qc))
            out_u, bi = [], 0
            for n_, a_ in enumerate(attn):
                out_u.append(a_)
                if n_ % 4 == 3 and bi < len(build):
                    out_u.append(build[bi]); bi += 1
            out_u.extend(build[bi:])
            if not attn:
                out_u = list(build)
            if h + 1 < H:
                out_u.append(("kbar", h + 1))
            sched.extend(out_u)

        @blk.tensor
        def _(t):
            t.wait_ge(sv, sv_kvb)
            t.wait_ge(ss, ss_p3)
            for unit in sched:
                kind = unit[0]
                if kind == "kbar":
                    h = unit[1]
                    base = (h % 2) * 2 * LT * 128
                    t.wait_ge(sw, sw_p3 + 32 * min(h + 2, H))
                    if h >= 1:
                        t.wait_ge(ss, ss_p3 + (h - 1) * SPH + 1)
                    if h >= 2:
                        t.wait_ge(sl, (h - 1) * TC * (KT // 2 + 1))
                    ins = None
                    for lt in range(LT):
                        ins = t.matmul(ps_m[:, 0:1], wkv_sb[:, base + lt * 128:base + (lt + 1) * 128],
                                       kvlbar[:, lt:lt + 1], start=(lt == 0), stop=(lt == LT - 1))
                    ins.then_inc(stk, 1)
                elif kind == "K":
                    h, kc = unit[1], unit[2]
                    base = (h % 2) * 2 * LT * 128
                    g = h * KC + kc
                    pk = ps_k[0]
                    if g >= 1:
                        t.wait_ge(svk, g)
                    ins = None
                    for lt in range(LT):
                        ins = t.matmul(pk[:], wkv_sb[:, base + lt * 128:base + (lt + 1) * 128],
                                       kvlT[:, lt * S + kc * 512:lt * S + (kc + 1) * 512],
                                       start=(lt == 0), stop=(lt == LT - 1))
                    ins.then_inc(stk, 1)
                elif kind == "V":
                    h, vq = unit[1], unit[2]
                    base = (h % 2) * 2 * LT * 128
                    gv = h * NVU + vq
                    if gv >= 1:
                        t.wait_ge(ss, ss_p3 + ((gv - 1) // NVU) * SPH + 1 + ((gv - 1) % NVU) + 1)
                    ins = None
                    for j in range(4):
                        tt = vq * 4 + j
                        for lt in range(LT):
                            ins = t.matmul(ps_v[:, j * 128:(j + 1) * 128],
                                           kvlT[:, lt * S + tt * 128:lt * S + (tt + 1) * 128],
                                           wkv_sb[:, base + LT * 128 + lt * 128:base + LT * 128 + (lt + 1) * 128],
                                           start=(lt == 0), stop=(lt == LT - 1))
                    ins.then_inc(stk, 1)
                elif kind == "S":
                    h, qc, kt = unit[1], unit[2], unit[3]
                    hb = h % 2
                    u = h * TC + qc
                    pss = ps_sp[(kt // 2) % 2][:, (kt % 2) * 512:(kt % 2 + 1) * 512]
                    if kt >= 4 and kt % 2 == 0:
                        t.wait_ge(se, u * (KT // 2) + kt // 2 - 1)
                    if kt % 4 == 0:
                        t.wait_ge(svk, h * KC + (kt // 4) + 1)
                    ins = t.matmul(pss, KT_sb[:, hb * S + kt * 128:hb * S + (kt + 1) * 128],
                                   QA[:, h * T + qc * 512:h * T + (qc + 1) * 512],
                                   start=True, stop=True)
                    if kt % 2 == 1:
                        ins.then_inc(sts, 1)
                elif kind == "PV":
                    h, qc, kt = unit[1], unit[2], unit[3]
                    hb = h % 2
                    u = h * TC + qc
                    if kt == 0 and u >= 1:
                        t.wait_ge(sv, sv_p3 + 1 + u)
                    if kt % 2 == 0:
                        t.wait_ge(se, u * (KT // 2) + kt // 2 + 1)
                    if kt % 4 == 0:
                        t.wait_ge(ss, ss_p3 + h * SPH + 1 + (kt // 4) + 1)
                    ins = t.matmul(ps_pv0[:], V_sb[:, hb * S + kt * 128:hb * S + (kt + 1) * 128],
                                   expS[:, (kt % 4) * 512:(kt % 4 + 1) * 512],
                                   start=(kt == 0), stop=(kt == KT - 1))
                    if kt % 2 == 1:
                        ins.then_inc(stp, 1)
                else:
                    h, qc = unit[1], unit[2]
                    u = h * TC + qc
                    t.wait_ge(sl, u * (KT // 2 + 1) + KT // 2)
                    if h + 1 <= H - 1:
                        t.wait_ge(ss, ss_p3 + (h + 1) * SPH + 1)
                    if u >= 1:
                        t.wait_ge(sl, u * (KT // 2 + 1))
                    t.matmul(ps_m[0:1, :], ones128f[:], l_part[:, 0:512], start=True, stop=False)
                    t.matmul(ps_m[0:1, :], ones128f[:], l_part[:, 512:1024], start=False, stop=True).then_inc(stp, 1)
                    t.wait_ge(sb2, u + 1)
                    t.matmul(ps_m[:, 0:512], ones1f[:], sums_s[:], start=True, stop=True).then_inc(stp, 1)

        @blk.scalar
        def _(s):
            s.wait_ge(sf0, 16 * ((NT2B + 1) // 2))
            s.wait_ge(sf1, 16 * (NT2B // 2))
            for unit in sched:
                kind = unit[0]
                if kind == "kbar":
                    h = unit[1]
                    s.wait_ge(stk, h * PH3 + 1)
                    if h >= 1:
                        s.wait_ge(svk, h * KC)
                    s.copy(kbar_f[:], ps_m[:, 0:1]).then_inc(ss, 1)
                elif kind == "V":
                    h, vq = unit[1], unit[2]
                    hb = h % 2
                    s.wait_ge(stk, h * PH3 + 1 + KC + vq + 1)
                    if h >= 2:
                        s.wait_ge(stp, (h - 1) * TC * (KT // 2 + 2))
                    s.copy(V_sb[:, hb * S + vq * 512:hb * S + (vq + 1) * 512], ps_v[:]).then_inc(ss, 1)
                elif kind == "S":
                    h, qc, kt = unit[1], unit[2], unit[3]
                    if kt % 2 == 0:
                        continue
                    u = h * TC + qc
                    j = kt // 2
                    s.wait_ge(sts, u * (KT // 2) + j + 1)
                    if j >= 2:
                        s.wait_ge(stp, u * (KT // 2 + 2) + j - 1)
                        s.wait_ge(sl, u * (KT // 2 + 1) + j - 1)
                    s.activation(expS[:, ((j % 2) * 1024):((j % 2) * 1024 + 1024)],
                                 ps_sp[(j % 2)][:], AF.Exp, scale=1.0).then_inc(se, 1)
                elif kind == "tail":
                    h, qc = unit[1], unit[2]
                    u = h * TC + qc
                    s.wait_ge(stp, u * (KT // 2 + 2) + KT // 2 + 1)
                    if u >= 1:
                        s.wait_ge(stp, (u - 1) * (KT // 2 + 2) + KT // 2 + 2)
                    s.copy(sums_s[:], ps_m[0:1, :]).then_inc(sb2, 1)

        @blk.vector
        def _(v):
            v.wait_ge(sf0, 16 * ((NT2B + 1) // 2))
            v.wait_ge(sf1, 16 * (NT2B // 2))
            for unit in sched:
                kind = unit[0]
                if kind == "K":
                    h, kc = unit[1], unit[2]
                    hb = h % 2
                    g = h * KC + kc
                    pk = ps_k[0]
                    v.wait_ge(stk, h * PH3 + 1 + kc + 1)
                    if g > 0:
                        v.wait_ge(svk, g)
                    if kc == 0:
                        v.wait_ge(ss, ss_p3 + h * SPH + 1)
                        if h >= 2:
                            v.wait_ge(sts, (h - 1) * TC * (KT // 2))
                    vjoin(v)
                    vchain(v, v.tensor_mul(krA[:], pk[:], Tck[:, kc * 512:(kc + 1) * 512]))
                    vchain(v, v.tensor_mul(krB[0:64, :], pk[64:128, :], Tsk[0:64, kc * 512:(kc + 1) * 512]))
                    vchain(v, v.tensor_mul(krB[64:128, :], pk[0:64, :], Tsk[64:128, kc * 512:(kc + 1) * 512]))
                    vjoin(v)
                    v.scalar_tensor_tensor(KT_sb[:, hb * S + kc * 512:hb * S + (kc + 1) * 512],
                                           krA[:], kbar_f[:, 0:1], krB[:],
                                           ALU.subtract, ALU.add).then_inc(svk, 1)
                elif kind == "S":
                    h, qc, kt = unit[1], unit[2], unit[3]
                    if kt % 2 == 0:
                        continue
                    u = h * TC + qc
                    j = kt // 2
                    v.wait_ge(se, u * (KT // 2) + j + 1)
                    exs = expS[:, (j % 2) * 1024:(j % 2) * 1024 + 1024]
                    if j == 0:
                        if u >= 1:
                            v.wait_ge(stp, (u - 1) * (KT // 2 + 2) + KT // 2 + 1)
                        v.tensor_scalar(l_part[:], exs, 1.0, None, ALU.mult).then_inc(sl, 1)
                    else:
                        v.wait_ge(sl, u * (KT // 2 + 1) + j)
                        v.tensor_add(l_part[:], l_part[:], exs).then_inc(sl, 1)
                elif kind == "tail":
                    h, qc = unit[1], unit[2]
                    u = h * TC + qc
                    v.wait_ge(stp, u * (KT // 2 + 2) + KT // 2 + 2)
                    if u >= 2:
                        v.wait_ge(sv, sv_p3 + 1 + (u - 2) + 1)
                    v.reciprocal(rbc[:, (u % 2) * 512:(u % 2 + 1) * 512],
                                 ps_m[:, 0:512]).then_inc(sl, 1)
                    v.wait_ge(sl, u * (KT // 2 + 1) + KT // 2 + 1)
                    v.tensor_mul(QA[:, h * T + qc * 512:h * T + (qc + 1) * 512],
                                 ps_pv0[:], rbc[:, (u % 2) * 512:(u % 2 + 1) * 512]).then_inc(sv, 1)
        SV += H * TC
        sv_attn = SV

        c3.close()

        c4 = es.enter_context(contextlib.ExitStack())
        wo_sb = sb("wo_sb", [128, 2 * DT * DOC], BF16, c4)
        xres = sb("xres", [128, 2 * DOC], F32, c4)
        fres = sb("fres", [128, 2 * DOC], BF16, c4)
        ostage = sb("ostage", [128, 2 * DOC], F32, c4)
        ps_o = [psm(f"ps_o{i}", [128, DOC], F32, c4) for i in range(2)]

        sv_p4, sw_p4, st_p4, sx_p4 = SV, SW, ST, SX

        @blk.gpsimd
        def _(gp):
            gp.wait_ge(sv, sv_attn)
            gp.wait_ge(svk, H * KC)
            for dc in range(NDC):
                if dc >= 2:
                    gp.wait_ge(st, st_p4 + (dc - 1) * T8)
                gp.dma_start(
                    out=wo_sb[:, (dc % 2) * DT * DOC:(dc % 2 + 1) * DT * DOC],
                    in_=wo[dc],
                ).then_inc(sw, 16)
        SW += 16 * NDC

        @blk.sync
        def _(sy):
            sy.wait_ge(sv, sv_attn)
            for dc in range(NDC):
                for i in range(T8):
                    k = dc * T8 + i
                    if k >= 2:
                        sy.wait_ge(sv, sv_p4 + k - 1)
                    sy.dma_start(out=xres[:, (k % 2) * DOC:(k % 2 + 1) * DOC],
                                 in_=x_own[i * 128:(i + 1) * 128, dc * DOC:(dc + 1) * DOC]).then_inc(sx, 16)
                    sy.dma_start(out=fres[:, (k % 2) * DOC:(k % 2 + 1) * DOC],
                                 in_=ffn_hbm.ap()[i * 128:(i + 1) * 128, dc * DOC:(dc + 1) * DOC]).then_inc(sx, 16)
        SX += 32 * NDC * T8

        @blk.tensor
        def _(t):
            t.wait_ge(sv, sv_attn)
            for dc in range(NDC):
                t.wait_ge(sw, sw_p4 + 16 * min(dc + 2, NDC))
                for i in range(T8):
                    k = dc * T8 + i
                    po = ps_o[k % 2]
                    if k >= 2:
                        t.wait_ge(sv, sv_p4 + k - 1)
                    ins = None
                    for h in range(DT):
                        o = (dc % 2) * DT * DOC + h * DOC
                        ins = t.matmul(po[:], QA[:, h * T + i * 128:h * T + i * 128 + 128],
                                       wo_sb[:, o:o + DOC], start=(h == 0), stop=(h == DT - 1))
                    ins.then_inc(st, 1)
        ST += NDC * T8

        @blk.vector
        def _(v):
            for k in range(NDC * T8):
                v.wait_ge(st, st_p4 + k + 1)
                v.wait_ge(sx, sx_p4 + 32 * min(k + 2, NDC * T8))
                if k >= 2:
                    v.wait_ge([so, so1][k % 2], 16 * (k // 2))
                osl = ostage[:, (k % 2) * DOC:(k % 2 + 1) * DOC]
                vjoin(v)
                vchain(v, v.tensor_add(osl, ps_o[k % 2][:], xres[:, (k % 2) * DOC:(k % 2 + 1) * DOC]))
                vjoin(v)
                v.tensor_add(osl, osl, fres[:, (k % 2) * DOC:(k % 2 + 1) * DOC]).then_inc(sv, 1)
        SV += NDC * T8

        @blk.scalar
        def _(s):
            for dc in range(NDC):
                for i in range(T8):
                    k = dc * T8 + i
                    s.wait_ge(sv, sv_p4 + k + 1)
                    s.dma_start(out=out[i * 128:(i + 1) * 128, dc * DOC:(dc + 1) * DOC],
                                in_=ostage[:, (k % 2) * DOC:(k % 2 + 1) * DOC]).then_inc([so, so1][k % 2], 16)
            s.wait_ge(so, 16 * ((NDC * T8 + 1) // 2))
            s.wait_ge(so1, 16 * ((NDC * T8) // 2))

        c4.close()

    return nc


def _host_layouts(inputs, cfg: CFG):
    D, T, S, ER = cfg.D, cfg.T, cfg.S, cfg.ER
    f32 = np.float32
    W_gate = np.asarray(inputs["L0_ffn_W_gate"])
    W_thr = np.asarray(inputs["L0_ffn_W_thr"])
    A_w = np.asarray(inputs["L0_ffn_A_w"])
    B_w = np.asarray(inputs["L0_ffn_B_w"])
    DT, LT, H, NDC, DOC, SM = cfg.DT, cfg.LT, cfg.H, cfg.NDC, cfg.DOC, cfg.SM

    def tile_din(w, np_, cw):
        return np.ascontiguousarray(
            np.asarray(w, f32).reshape(DT, 128, np_, cw).transpose(2, 1, 0, 3).reshape(np_, 128, DT * cw))

    def tile_lat(w, nh):
        return np.ascontiguousarray(
            np.asarray(w, f32).reshape(LT, 128, nh, 128).transpose(2, 1, 0, 3).reshape(nh, 128, LT * 128))

    wsm_flat = np.concatenate(
        [W_gate.T, W_thr[:, None], np.transpose(A_w, (2, 0, 1)).reshape(D, ER)], axis=1)
    common = {
        "wq": tile_din(inputs["L0_attn_Wq"], H // 2, 256),
        "wc": np.ascontiguousarray(inputs["L0_attn_Wc"], f32),
        "wk": tile_lat(inputs["L0_attn_Wk"], H),
        "wv": tile_lat(inputs["L0_attn_Wv"], H),
        "wo": tile_din(inputs["L0_attn_Wo"], NDC, DOC),
        "wbt": tile_din(np.asarray(inputs["L0_ffn_W_base"]).T, NDC, DOC),
        "wsm": np.ascontiguousarray(
            np.asarray(wsm_flat, f32).reshape(DT, 128, SM).transpose(1, 0, 2).reshape(128, DT * SM)),
        "bflat": np.ascontiguousarray(np.transpose(B_w, (0, 2, 1)).reshape(ER, D), f32),
        "bbase": np.ascontiguousarray(np.asarray(inputs["L0_ffn_b_base"], f32)[None, :]),
        "bthr": np.ascontiguousarray(np.broadcast_to(
            np.asarray(inputs["L0_ffn_b_thr"], f32).reshape(1, 1), (128, 1))),
        "posk": np.ascontiguousarray(np.broadcast_to(np.arange(S, dtype=f32)[None, :], (128, S))),
        "dind": np.ascontiguousarray((np.arange(128, dtype=f32) % 64)[:, None]),
        "ident": np.eye(128, dtype=f32),
    }
    x = np.asarray(inputs["L0_attn_hidden_states"], f32)
    in_maps = []
    for c in range(8):
        b, qq = c // 4, c % 4
        m = dict(common)
        m["x_own"] = np.ascontiguousarray(x[b, qq * T:(qq + 1) * T, :])
        m["posq"] = np.ascontiguousarray(np.broadcast_to(
            np.arange(qq * T, (qq + 1) * T, dtype=f32)[None, :], (128, T)))
        in_maps.append(m)
    return in_maps


def kernel(**inputs):
    cfg = CFG()
    nc = build_program(cfg)
    in_maps = _host_layouts(inputs, cfg)
    res = run_bass_kernel_spmd(nc, in_maps, core_ids=list(range(8)))
    out = np.empty((2, 4 * cfg.T, cfg.D), dtype=np.float32)
    for c in range(8):
        b, qq = c // 4, c % 4
        out[b, qq * cfg.T:(qq + 1) * cfg.T, :] = res.results[c]["out"]
    return out


# version 19
# speedup vs baseline: 1.1506x; 1.0024x over previous
import contextlib
import math
import numpy as np
import concourse.bass as bass
import concourse.mybir as mybir
import concourse.bass_utils as _bu
from concourse.bass_utils import run_bass_kernel_spmd

if not getattr(_bu, "_ldw_patched", False):
    _orig_run_command = _bu.run_command

    def _run_command_ldw(cmd, *a, **kw):
        if isinstance(cmd, (list, tuple)):
            cmd = [c.replace("--enable-ldw-opt=false", "--enable-ldw-opt=true")
                   if isinstance(c, str) else c for c in cmd]
        return _orig_run_command(cmd, *a, **kw)

    _bu.run_command = _run_command_ldw
    _bu._ldw_patched = True

F32 = mybir.dt.float32
BF16 = mybir.dt.bfloat16
AF = mybir.ActivationFunctionType
ALU = mybir.AluOpType
AX = mybir.AxisListType

PI = math.pi
LN_THETA = math.log(10000.0)


class CFG:
    def __init__(self, D=4096, L=512, H=32, T=1024):
        self.D, self.L, self.H, self.T = D, L, H, T
        self.S = 4 * T
        self.E, self.R = 8, 16
        self.ER = self.E * self.R
        self.DT = D // 128
        self.LT = L // 128
        self.TC = T // 512
        self.T8 = T // 128
        self.KT = self.S // 128
        self.KC = self.S // 512
        self.G = 4
        self.DOC = 256
        self.NDC = D // self.DOC
        self.SM = 9 + self.ER


def build_program(cfg: CFG):
    D, L, H, T, S = cfg.D, cfg.L, cfg.H, cfg.T, cfg.S
    DT, LT, TC, T8, KT, KC = cfg.DT, cfg.LT, cfg.TC, cfg.T8, cfg.KT, cfg.KC
    ER, DOC, NDC, G, SM = cfg.ER, cfg.DOC, cfg.NDC, cfg.G, cfg.SM
    SQ = 1.0 / math.sqrt(128.0)
    NVU = KT // 4
    PH3 = 1 + KC + NVU

    nc = bass.Bass("TRN2", target_bir_lowering=False, debug=False)

    x_own = nc.dram_tensor("x_own", [T, D], F32, kind="ExternalInput").ap()
    wq = nc.dram_tensor("wq", [H // 2, 128, DT * 256], F32, kind="ExternalInput").ap()
    wc = nc.dram_tensor("wc", [D, L], F32, kind="ExternalInput").ap()
    wk = nc.dram_tensor("wk", [H, 128, LT * 128], F32, kind="ExternalInput").ap()
    wv = nc.dram_tensor("wv", [H, 128, LT * 128], F32, kind="ExternalInput").ap()
    wo = nc.dram_tensor("wo", [NDC, 128, DT * DOC], F32, kind="ExternalInput").ap()
    wbt = nc.dram_tensor("wbt", [NDC, 128, DT * DOC], F32, kind="ExternalInput").ap()
    wsm = nc.dram_tensor("wsm", [128, DT * SM], F32, kind="ExternalInput").ap()
    bflat = nc.dram_tensor("bflat", [ER, D], F32, kind="ExternalInput").ap()
    bbase = nc.dram_tensor("bbase", [1, D], F32, kind="ExternalInput").ap()
    bthr = nc.dram_tensor("bthr", [128, 1], F32, kind="ExternalInput").ap()
    posq = nc.dram_tensor("posq", [128, T], F32, kind="ExternalInput").ap()
    posk = nc.dram_tensor("posk", [128, S], F32, kind="ExternalInput").ap()
    dind = nc.dram_tensor("dind", [128, 1], F32, kind="ExternalInput").ap()
    ident_in = nc.dram_tensor("ident", [128, 128], F32, kind="ExternalInput").ap()
    out = nc.dram_tensor("out", [T, D], F32, kind="ExternalOutput").ap()

    cc_in = nc.dram_tensor("cc_in", [LT, 128, T], BF16)
    cc_out = nc.dram_tensor("cc_out", [G, LT, 128, T], BF16)
    ffn_hbm = nc.dram_tensor("ffn_hbm", [T, D], BF16)

    es = contextlib.ExitStack()

    with es:
        def sb(name, shape, dt, stack=es):
            return stack.enter_context(nc.sbuf_tensor(name, shape, dt))

        def psm(name, shape, dt, stack):
            return stack.enter_context(nc.psum_tensor(name, shape, dt))

        ones_col = sb("ones_col", [1, 128], BF16)
        ones128f = sb("ones128f", [128, 1], F32)
        ones1f = sb("ones1f", [1, 128], F32)
        Tck = sb("Tck", [128, S], BF16)
        Tsk = sb("Tsk", [128, S], BF16)
        Tcq = sb("Tcq", [128, T], BF16)
        Tsq = sb("Tsq", [128, T], BF16)
        QA = sb("QA", [128, H * T], BF16)
        hwT = sb("hwT", [128, T], BF16)
        ident_s = sb("ident_s", [128, 128], BF16)
        bthr_s = sb("bthr_s", [128, 1], F32)

        sx = es.enter_context(nc.semaphore("sx"))
        sw = es.enter_context(nc.semaphore("sw"))
        scc = es.enter_context(nc.semaphore("scc"))
        st = es.enter_context(nc.semaphore("st"))
        stk = es.enter_context(nc.semaphore("stk"))
        sts = es.enter_context(nc.semaphore("sts"))
        stp = es.enter_context(nc.semaphore("stp"))
        se = es.enter_context(nc.semaphore("se"))
        sl = es.enter_context(nc.semaphore("sl"))
        sv = es.enter_context(nc.semaphore("sv"))
        ss = es.enter_context(nc.semaphore("ss"))
        so = es.enter_context(nc.semaphore("so"))
        sm0 = es.enter_context(nc.semaphore("sm0"))
        sf0 = es.enter_context(nc.semaphore("sf0"))
        sf1 = es.enter_context(nc.semaphore("sf1"))
        so1 = es.enter_context(nc.semaphore("so1"))
        svc = es.enter_context(nc.semaphore("svc"))
        sb2 = es.enter_context(nc.semaphore("sb2"))
        svk = es.enter_context(nc.semaphore("svk"))
        VC = [0]

        def vchain(v, ins):
            ins.then_inc(svc, 1)
            VC[0] += 1

        def vjoin(v):
            v.wait_ge(svc, VC[0])

        blk = es.enter_context(nc.Block())

        SX = SW = SCC = ST = SV = SS = 0

        c1 = es.enter_context(contextlib.ExitStack())
        xnT = sb("xnT", [128, DT * T], BF16, c1)

        c0 = c1.enter_context(contextlib.ExitStack())
        th = sb("th", [128, S], F32, c0)
        pq_s = sb("pq_s", [128, T], F32, c0)
        tk0 = sb("tk0", [128, S], F32, c0)
        ck0 = sb("ck0", [128, S], F32, c0)
        invf = sb("invf", [128, 1], F32, c0)

        @blk.sync
        def _(sy):
            sy.dma_start(out=th[:], in_=posk).then_inc(sx, 16)
            sy.dma_start(out=pq_s[:], in_=posq).then_inc(sx, 16)
            sy.dma_start(out=invf[:], in_=dind).then_inc(sx, 16)
            sy.dma_start(out=bthr_s[:], in_=bthr).then_inc(sx, 16)
        SX += 64

        @blk.gpsimd
        def _(gp):
            gp.dma_start(out=ident_s[:], in_=ident_in).then_inc(sw, 16)
            gp.memset(ones_col[:], 1.0)
            gp.memset(ones128f[:], 1.0)
            gp.memset(ones1f[:], 1.0).then_inc(sm0, 1)
        SW += 16

        @blk.scalar
        def _(s):
            s.wait_ge(sx, 64)
            s.activation(invf[:], invf[:], AF.Exp, scale=-LN_THETA / 64.0).then_inc(ss, 1)
        SS += 1

        def emit_wrap(v, pos_sl, tk, ck):
            vchain(v, v.tensor_scalar(tk, pos_sl, 1.0 / (2 * PI), None, ALU.mult))
            for b in [512.0, 256.0, 128.0, 64.0, 32.0, 16.0, 8.0, 4.0, 2.0, 1.0]:
                vjoin(v)
                vchain(v, v.tensor_scalar(ck, tk, b, None, ALU.is_ge))
                vjoin(v)
                vchain(v, v.scalar_tensor_tensor(tk, ck, -b, tk, ALU.mult, ALU.add))
            vjoin(v)
            v.tensor_scalar(tk, tk, 2 * PI, -PI, ALU.mult, ALU.add).then_inc(sv, 1)

        def emit_tables(pos_ap, width, Tc, Ts, fold, SV, SS):
            ss0, sv0 = SS, SV
            tk, ck = tk0[:, :width], ck0[:, :width]

            @blk.vector
            def _(v):
                v.wait_ge(ss, ss0)
                vjoin(v)
                vchain(v, v.tensor_scalar(pos_ap, pos_ap, invf[:, 0:1], None, ALU.mult))
                vjoin(v)
                emit_wrap(v, pos_ap, tk, ck)

            @blk.scalar
            def _(s):
                s.wait_ge(sv, sv0 + 1)
                s.activation(Ts[:, :width], tk, AF.Sin, scale=1.0).then_inc(ss, 1)

            @blk.vector
            def _(v):
                v.wait_ge(ss, ss0 + 1)
                v.tensor_scalar(Ts[0:64, :width], Ts[0:64, :width], fold, None, ALU.mult)
                v.tensor_scalar(Ts[64:128, :width], Ts[64:128, :width], -fold, None, ALU.mult)
                vjoin(v)
                vchain(v, v.tensor_scalar(pos_ap, pos_ap, PI / 2, None, ALU.add))
                vjoin(v)
                emit_wrap(v, pos_ap, tk, ck)

            @blk.scalar
            def _(s):
                s.wait_ge(sv, sv0 + 2)
                s.activation(Tc[:, :width], tk, AF.Sin, scale=1.0).then_inc(ss, 1)

            @blk.vector
            def _(v):
                v.wait_ge(ss, ss0 + 2)
                v.tensor_scalar(Tc[:, :width], Tc[:, :width], -fold, None, ALU.mult).then_inc(sv, 1)
            return sv0 + 3, ss0 + 2

        SV, SS = emit_tables(th[:], S, Tck, Tsk, 1.0, SV, SS)
        SV, SS = emit_tables(pq_s[:], T, Tcq, Tsq, SQ, SV, SS)
        sv_tab, ss_tab = SV, SS

        c0.close()
        c1p = c1.enter_context(contextlib.ExitStack())
        xst = sb("xst", [128, 2 * D], F32, c1p)
        xn_tm = sb("xn_tm", [128, 2 * D], BF16, c1p)
        stats = sb("stats", [128, 10], F32, c1p)
        ps_tp = [psm(f"ps_tp{i}", [128, 512], BF16, c1p) for i in range(2)]

        NQ4 = DT // 4
        sx_p1, ss_p1, sv_p1, st_p1 = SX, SS, SV, ST

        @blk.sync
        def _(sy):
            for i in range(T8):
                if i == 0:
                    sy.wait_ge(ss, ss_tab)
                if i >= 2:
                    sy.wait_ge(ss, ss_p1 + 3 * (i - 1))
                sy.dma_start(out=xst[:, (i % 2) * D:(i % 2 + 1) * D],
                             in_=x_own[i * 128:(i + 1) * 128, :]).then_inc(sx, 16)
        SX += 16 * T8

        PCH = 2 + NQ4

        def _evac_val(k):
            i, q4 = k // NQ4, k % NQ4
            if i + 1 < T8:
                return sv_p1 + 2 * (i + 2) + i * NQ4 + q4 + 1
            return sv_p1 + 2 * T8 + (T8 - 1) * NQ4 + q4 + 1

        def _emit_evac(v, i, q4):
            k = i * NQ4 + q4
            v.wait_ge(st, st_p1 + k + 1)
            dst = xnT[:].rearrange("p (d t) -> p d t", d=DT, t=T)[
                :, q4 * 4:(q4 + 1) * 4, i * 128:(i + 1) * 128]
            v.tensor_copy(dst, ps_tp[k % 2][:].rearrange(
                "p (d c) -> p d c", d=4, c=128)).then_inc(sv, 1)

        @blk.vector
        def _(v):
            for i in range(T8):
                o = (i % 2) * 5
                mean_, msq, var_, rstd, nb = (stats[:, o + j:o + j + 1] for j in range(5))
                xb = xst[:, (i % 2) * D:(i % 2 + 1) * D]
                v.wait_ge(sx, sx_p1 + 16 * min(i + 2, T8))
                if i >= 2:
                    v.wait_ge(ss, ss_p1 + 3 * (i - 1))
                v.wait_ge(ss, ss_p1 + 3 * i + 1)
                vjoin(v)
                vchain(v, v.tensor_scalar(xn_tm[:, (i % 2) * D:(i % 2 + 1) * D], xb,
                                          1.0 / D, None, ALU.mult, ALU.add, accum_out=mean_))
                vjoin(v)
                vchain(v, v.tensor_mul(var_, mean_, mean_))
                vjoin(v)
                vchain(v, v.tensor_sub(var_, msq, var_))
                vjoin(v)
                vchain(v, v.tensor_scalar(var_, var_, 1e-5, None, ALU.add))
                vjoin(v)
                v.reciprocal(var_, var_).then_inc(sv, 1)
                v.wait_ge(ss, ss_p1 + 3 * i + 2)
                v.scalar_tensor_tensor(nb, mean_, -1.0, rstd, ALU.mult, ALU.mult).then_inc(sv, 1)
                if i >= 1:
                    for q4 in range(NQ4):
                        _emit_evac(v, i - 1, q4)
            for q4 in range(NQ4):
                _emit_evac(v, T8 - 1, q4)
        SV += T8 * PCH

        import math as _math
        RSD = 1.0 / _math.sqrt(D)

        @blk.scalar
        def _(s):
            for i in range(T8):
                o = (i % 2) * 5
                msq_, var_, rstd, nb = (stats[:, o + j:o + j + 1] for j in range(1, 5))
                s.wait_ge(sx, sx_p1 + 16 * min(i + 2, T8))
                if i >= 2:
                    s.wait_ge(sv, sv_p1 + 2 * (i - 2) + max(i - 3, 0) * NQ4 + 2)
                    s.wait_ge(st, st_p1 + (i - 1) * NQ4)
                s.activation(xn_tm[:, (i % 2) * D:(i % 2 + 1) * D],
                             xst[:, (i % 2) * D:(i % 2 + 1) * D], AF.Square,
                             scale=RSD, accum_out=msq_).then_inc(ss, 1)
                s.wait_ge(sv, sv_p1 + 2 * i + max(i - 1, 0) * NQ4 + 1)
                s.activation(rstd, var_, AF.Sqrt).then_inc(ss, 1)
                s.wait_ge(sv, sv_p1 + 2 * i + max(i - 1, 0) * NQ4 + 2)
                if i >= 2:
                    s.wait_ge(st, st_p1 + (i - 1) * NQ4)
                s.activation(xn_tm[:, (i % 2) * D:(i % 2 + 1) * D],
                             xst[:, (i % 2) * D:(i % 2 + 1) * D], AF.Identity,
                             bias=nb, scale=rstd).then_inc(ss, 1)
        SS += 3 * T8

        @blk.tensor
        def _(t):
            t.wait_ge(sw, 16)
            t.wait_ge(sm0, 1)
            for i in range(T8):
                t.wait_ge(ss, ss_p1 + 3 * i + 3)
                xb = xn_tm[:, (i % 2) * D:(i % 2 + 1) * D]
                for q4 in range(NQ4):
                    k = i * NQ4 + q4
                    pst = ps_tp[k % 2]
                    if k >= 2:
                        t.wait_ge(sv, _evac_val(k - 2))
                    ins = None
                    for j in range(4):
                        dt = q4 * 4 + j
                        ins = t.transpose(pst[:, j * 128:(j + 1) * 128],
                                          xb[:, dt * 128:(dt + 1) * 128], ident_s[:])
                    ins.then_inc(st, 1)
        ST += T8 * NQ4
        sv_xnT = SV

        c1p.close()

        c1b = c1.enter_context(contextlib.ExitStack())
        wc_sb = sb("wc_sb", [128, 2 * L], BF16, c1b)
        kv_stage = sb("kv_stage", [128, LT * T], BF16, c1b)
        ps_kv = [psm(f"ps_kv{i}", [128, 512], F32, c1b) for i in range(LT * TC)]

        sw_kv, st_kv, ss_kv = SW, ST, SS

        @blk.gpsimd
        def _(gp):
            gp.wait_ge(sv, sv_xnT)
            for dt in range(DT):
                if dt >= 2:
                    gp.wait_ge(st, st_kv + (dt - 1) * LT * TC)
                gp.dma_start(out=wc_sb[:, (dt % 2) * L:(dt % 2) * L + L],
                             in_=wc[dt * 128:(dt + 1) * 128, :]).then_inc(sw, 16)
        SW += 16 * DT

        @blk.tensor
        def _(t):
            t.wait_ge(sv, sv_xnT)
            for dt in range(DT):
                t.wait_ge(sw, sw_kv + 16 * min(dt + 2, DT))
                for lt in range(LT):
                    for tc in range(TC):
                        t.matmul(ps_kv[lt * TC + tc][:],
                                 wc_sb[:, (dt % 2) * L + lt * 128:(dt % 2) * L + (lt + 1) * 128],
                                 xnT[:, dt * T + tc * 512:dt * T + (tc + 1) * 512],
                                 start=(dt == 0), stop=(dt == DT - 1)).then_inc(st, 1)
        ST += DT * LT * TC

        @blk.scalar
        def _(s):
            s.wait_ge(st, ST)
            for lt in range(LT):
                for tc in range(TC):
                    s.copy(kv_stage[:, lt * T + tc * 512:lt * T + (tc + 1) * 512],
                           ps_kv[lt * TC + tc][:]).then_inc(ss, 1)
        SS += LT * TC

        sw_cc = SW

        @blk.gpsimd
        def _(gp):
            gp.wait_ge(ss, SS)
            for lt in range(LT):
                gp.dma_start(out=cc_in.ap()[lt], in_=kv_stage[:, lt * T:(lt + 1) * T]).then_inc(sw, 16)
            gp.wait_ge(sw, sw_cc + 16 * LT)
            gp.collective_compute(
                "AllGather", ALU.bypass,
                replica_groups=[[0, 1, 2, 3], [4, 5, 6, 7]],
                ins=[cc_in.ap().opt()],
                outs=[cc_out.ap().opt()],
            ).then_inc(scc, 1)
        SW += 16 * LT
        SCC = 1

        c1b.close()

        c2a = c1.enter_context(contextlib.ExitStack())
        wq_sb = sb("wq_sb", [128, 2 * DT * 256], BF16, c2a)
        wsm_sb_fwd = [sb("wsm_sb", [128, DT * SM], BF16, c2a)]
        ropeA = sb("ropeA", [128, 2 * 512], F32, c2a)
        ropeB = sb("ropeB", [128, 2 * 512], F32, c2a)
        ps_q = [psm(f"ps_q{i}", [128, 512], F32, c2a) for i in range(4)]

        NHP = H // 2
        NU = H * TC
        sw_q, st_q, ss_q, sv_q = SW, ST, SS, SV

        @blk.gpsimd
        def _(gp):
            gp.wait_ge(st, st_q)
            gp.wait_ge(ss, ss_q)
            gp.dma_start(out=wsm_sb_fwd[0][:], in_=wsm).then_inc(sw, 16)
            for hp in range(NHP):
                if hp >= 2:
                    gp.wait_ge(st, st_q + (hp - 1) * 2 * TC)
                gp.dma_start(
                    out=wq_sb[:, (hp % 2) * DT * 256:(hp % 2 + 1) * DT * 256],
                    in_=wq[hp],
                ).then_inc(sw, 16)
        SW += 16 * (NHP + 1)

        @blk.tensor
        def _(t):
            t.wait_ge(ss, ss_q)
            for hp in range(NHP):
                t.wait_ge(sw, sw_q + 16 + 16 * min(hp + 2, NHP))
                for h2 in range(2):
                    for tc in range(TC):
                        u = (hp * 2 + h2) * TC + tc
                        pq = ps_q[u % 4]
                        if u >= 4:
                            t.wait_ge(sv, sv_q + (u - 4) + 1)
                        ins = None
                        for dt in range(DT):
                            base = (hp % 2) * DT * 256 + dt * 256 + h2 * 128
                            ins = t.matmul(pq[:], wq_sb[:, base:base + 128],
                                           xnT[:, dt * T + tc * 512:dt * T + (tc + 1) * 512],
                                           start=(dt == 0), stop=(dt == DT - 1))
                        ins.then_inc(st, 1)
        ST += NU

        @blk.vector
        def _(v):
            v.wait_ge(sv, sv_q)
            for u in range(NU):
                h, tc = u // TC, u % TC
                pq = ps_q[u % 4]
                ra = ropeA[:, (u % 2) * 512:(u % 2 + 1) * 512]
                rb = ropeB[:, (u % 2) * 512:(u % 2 + 1) * 512]
                v.wait_ge(st, st_q + u + 1)
                if u >= 2:
                    v.wait_ge(sv, sv_q + u - 1)
                vjoin(v)
                vchain(v, v.tensor_mul(ra, pq[:], Tcq[:, tc * 512:(tc + 1) * 512]))
                vchain(v, v.tensor_mul(rb[0:64, :], pq[64:128, :], Tsq[0:64, tc * 512:(tc + 1) * 512]))
                vchain(v, v.tensor_mul(rb[64:128, :], pq[0:64, :], Tsq[64:128, tc * 512:(tc + 1) * 512]))
                vjoin(v)
                v.tensor_add(QA[:, h * T + tc * 512:h * T + (tc + 1) * 512], ra, rb).then_inc(sv, 1)
        SV += NU

        wsm_sb = wsm_sb_fwd[0]
        fsc = sb("fsc", [128, 32], F32, c2a)
        hw_tm = sb("hw_tm", [128, 128], BF16, c2a)
        ps_sm = psm("ps_sm", [128, SM], F32, c2a)
        ps_hwt = psm("ps_hwt", [128, 128], BF16, c2a)

        sw_sm, st_sm, ss_sm, sv_sm = SW, ST, SS, SV

        @blk.tensor
        def _(t):
            t.wait_ge(sw, sw_sm)
            for i in range(T8):
                if i >= 1:
                    t.wait_ge(sv, sv_sm + i)
                ins = None
                for dt in range(DT):
                    ins = t.matmul(ps_sm[:], xnT[:, dt * T + i * 128:dt * T + i * 128 + 128],
                                   wsm_sb[:, dt * SM:(dt + 1) * SM],
                                   start=(dt == 0), stop=(dt == DT - 1))
                ins.then_inc(st, 1)
                t.wait_ge(sv, sv_sm + i + 1)
                t.transpose(ps_hwt[:], hw_tm[:], ident_s[:]).then_inc(st, 1)
        ST += 2 * T8

        @blk.scalar
        def _(s):
            for i in range(T8):
                s.wait_ge(st, st_sm + 2 * i + 1)
                s.activation(fsc[:, 0:8], ps_sm[:, 0:8], AF.Exp, scale=1.0).then_inc(ss, 1)
                s.activation(fsc[:, 8:9], ps_sm[:, 8:9], AF.Sigmoid,
                             bias=bthr_s[:, 0:1], scale=1.0).then_inc(ss, 1)
                s.wait_ge(st, st_sm + 2 * i + 2)
                s.copy(hwT[:, i * 128:(i + 1) * 128], ps_hwt[:]).then_inc(ss, 1)
        SS += 3 * T8

        @blk.vector
        def _(v):
            for i in range(T8):
                v.wait_ge(ss, ss_sm + 3 * i + 2)
                if i >= 1:
                    v.wait_ge(st, st_sm + 2 * i)
                gsum, thr, iz, winv = fsc[:, 9:10], fsc[:, 10:11], fsc[:, 11:12], fsc[:, 12:13]
                gate = fsc[:, 16:24]
                ops = [
                    lambda: v.reduce_sum(gsum, fsc[:, 0:8], axis=AX.X),
                    lambda: v.reciprocal(gsum, gsum),
                    lambda: v.tensor_scalar(gate, fsc[:, 0:8], gsum[:, 0:1], None, ALU.mult),
                    lambda: v.tensor_scalar(thr, fsc[:, 8:9], 0.125, None, ALU.mult),
                    lambda: v.tensor_scalar(gate, gate, thr[:, 0:1], None, ALU.subtract),
                    lambda: v.tensor_relu(gate, gate),
                    lambda: v.reduce_sum(gsum, gate, axis=AX.X),
                    lambda: v.tensor_scalar(iz, gsum, 0.0, None, ALU.is_le),
                    lambda: v.tensor_add(gsum, gsum, iz),
                    lambda: v.reciprocal(winv, gsum),
                    lambda: v.tensor_scalar(gate, gate, 2.0, None, ALU.mult),
                    lambda: v.tensor_scalar(gate, gate, winv[:, 0:1], None, ALU.mult),
                ]
                for op in ops:
                    vjoin(v)
                    vchain(v, op())
                vjoin(v)
                ins = None
                for e in range(8):
                    ins = v.tensor_scalar(hw_tm[:, e * 16:(e + 1) * 16],
                                          ps_sm[:, 9 + e * 16:9 + (e + 1) * 16],
                                          gate[:, e:e + 1], None, ALU.mult)
                ins.then_inc(sv, 1)
        SV += T8

        c2a.close()

        c2b = c1.enter_context(contextlib.ExitStack())
        wbt_sb = sb("wbt_sb", [128, 2 * DT * DOC], BF16, c2b)
        bflat_sb = sb("bflat_sb", [128, D], BF16, c2b)
        bbase_s = sb("bbase_s", [1, D], BF16, c2b)
        fstage = sb("fstage", [128, 2 * DOC], BF16, c2b)
        ps_b = [psm(f"ps_b{i}", [128, DOC], F32, c2b) for i in range(2)]

        sw_b, st_b, ss_b, sx_b, sv_b = SW, ST, SS, SX, SV

        @blk.gpsimd
        def _(gp):
            gp.wait_ge(st, st_b)
            gp.wait_ge(sv, sv_b)
            gp.wait_ge(ss, ss_b)
            gp.dma_start(out=bflat_sb[:], in_=bflat).then_inc(sw, 16)
            gp.dma_start(out=bbase_s[:], in_=bbase).then_inc(sw, 16)
            for dc in range(NDC):
                if dc >= 2:
                    gp.wait_ge(st, st_b + (dc - 1) * T8)
                gp.dma_start(
                    out=wbt_sb[:, (dc % 2) * DT * DOC:(dc % 2 + 1) * DT * DOC],
                    in_=wbt[dc],
                ).then_inc(sw, 16)
        SW += 16 * (2 + NDC)

        @blk.tensor
        def _(t):
            t.wait_ge(ss, ss_b)
            for dc in range(NDC):
                t.wait_ge(sw, sw_b + 32 + 16 * min(dc + 2, NDC))
                for i in range(T8):
                    k = dc * T8 + i
                    pb = ps_b[k % 2]
                    if k >= 2:
                        t.wait_ge(ss, ss_b + k - 1)
                    for dt in range(DT):
                        o = (dc % 2) * DT * DOC + dt * DOC
                        t.matmul(pb[:], xnT[:, dt * T + i * 128:dt * T + i * 128 + 128],
                                 wbt_sb[:, o:o + DOC],
                                 start=(dt == 0), stop=False)
                    t.matmul(pb[:], hwT[:, i * 128:(i + 1) * 128],
                             bflat_sb[:, dc * DOC:(dc + 1) * DOC], start=False, stop=False)
                    t.matmul(pb[:], ones_col[:], bbase_s[0:1, dc * DOC:(dc + 1) * DOC],
                             start=False, stop=True).then_inc(st, 1)
        ST += NDC * T8

        @blk.scalar
        def _(s):
            for k in range(NDC * T8):
                s.wait_ge(st, st_b + k + 1)
                if k >= 2:
                    s.wait_ge([sf0, sf1][k % 2], 16 * (k // 2))
                s.copy(fstage[:, (k % 2) * DOC:(k % 2 + 1) * DOC], ps_b[k % 2][:]).then_inc(ss, 1)
        SS += NDC * T8

        @blk.sync
        def _(sy):
            for dc in range(NDC):
                for i in range(T8):
                    k = dc * T8 + i
                    sy.wait_ge(ss, ss_b + k + 1)
                    sy.dma_start(out=ffn_hbm.ap()[i * 128:(i + 1) * 128, dc * DOC:(dc + 1) * DOC],
                                 in_=fstage[:, (k % 2) * DOC:(k % 2 + 1) * DOC]).then_inc([sf0, sf1][k % 2], 16)
        NT2B = NDC * T8

        c2b.close()
        c1.close()

        c3 = es.enter_context(contextlib.ExitStack())
        kvlT = sb("kvlT", [128, LT * S], BF16, c3)
        wkv_sb = sb("wkv_sb", [128, 2 * 2 * LT * 128], BF16, c3)
        KT_sb = sb("KT_sb", [128, 2 * S], BF16, c3)
        V_sb = sb("V_sb", [128, 2 * S], BF16, c3)
        expS = sb("expS", [128, 4 * 512], BF16, c3)
        l_part = sb("l_part", [128, 1024], F32, c3)
        kvlbar = sb("kvlbar", [128, LT], BF16, c3)
        kvlbar_f = sb("kvlbar_f", [128, LT], F32, c3)
        kbar_f = sb("kbar_f", [128, 1], F32, c3)
        sums_s = sb("sums_s", [1, 512], F32, c3)
        krA = sb("krA", [128, 512], F32, c3)
        krB = sb("krB", [128, 512], F32, c3)
        rbc = sb("rbc", [128, 2 * 512], F32, c3)
        ps_k = [psm("ps_k0", [128, 512], F32, c3)]
        ps_v = psm("ps_v", [128, 512], F32, c3)
        ps_sp = [psm(f"ps_sp{i}", [128, 1024], F32, c3) for i in range(2)]
        ps_pv0 = psm("ps_pv0", [128, 512], F32, c3)
        ps_m = psm("ps_m", [128, 512], F32, c3)

        sx_kvl, sv_p3, ss_p3, sw_p3 = SX, SV, SS, SW
        st_p2 = ST

        @blk.sync
        def _(sy):
            sy.wait_ge(scc, SCC)
            sy.wait_ge(st, st_p2)
            sy.wait_ge(sv, sv_p3)
            sy.wait_ge(ss, ss_p3)
            for r in range(G):
                for lt in range(LT):
                    sy.dma_start(out=kvlT[:, lt * S + r * T:lt * S + (r + 1) * T],
                                 in_=cc_out.ap()[r, lt]).then_inc(sx, 16)
        SX += 16 * G * LT
        sx_kv_done = SX

        @blk.vector
        def _(v):
            v.wait_ge(sx, sx_kv_done)
            v.wait_ge(sf0, 16 * ((NT2B + 1) // 2))
            v.wait_ge(sf1, 16 * (NT2B // 2))
            for lt in range(LT):
                vchain(v, v.reduce_sum(kvlbar_f[:, lt:lt + 1], kvlT[:, lt * S:(lt + 1) * S], axis=AX.X))
            vjoin(v)
            vchain(v, v.tensor_scalar(kvlbar_f[:], kvlbar_f[:], 1.0 / S, None, ALU.mult))
            vjoin(v)
            v.tensor_copy(kvlbar[:], kvlbar_f[:]).then_inc(sv, 1)
        SV += 1
        sv_kvb = SV

        @blk.gpsimd
        def _(gp):
            gp.wait_ge(st, st_p2)
            gp.wait_ge(ss, ss_p3)
            gp.wait_ge(sf0, 16 * ((NT2B + 1) // 2))
            gp.wait_ge(sf1, 16 * (NT2B // 2))
            for h in range(H):
                if h >= 2:
                    gp.wait_ge(stk, (h - 1) * PH3)
                base = (h % 2) * 2 * LT * 128
                gp.dma_start(out=wkv_sb[:, base:base + LT * 128], in_=wk[h]).then_inc(sw, 16)
                gp.dma_start(out=wkv_sb[:, base + LT * 128:base + 2 * LT * 128],
                             in_=wv[h]).then_inc(sw, 16)
        SW += 32 * H

        VPH = KC + TC
        SPH = 1 + NVU

        sched = [("kbar", 0)]
        for h in range(H + 1):
            build = []
            if h < H:
                build = [("K", h, kc) for kc in range(KC)] + \
                        [("V", h, vq) for vq in range(NVU)]
            attn = []
            if h >= 1:
                for qc in range(TC):
                    for kt in range(KT):
                        attn.append(("S", h - 1, qc, kt))
                        if kt >= 4:
                            attn.append(("PV", h - 1, qc, kt - 4))
                    for kt in range(KT - 4, KT):
                        attn.append(("PV", h - 1, qc, kt))
                    attn.append(("tail", h - 1, qc))
            out_u, bi = [], 0
            for n_, a_ in enumerate(attn):
                out_u.append(a_)
                if n_ % 6 == 3 and bi < len(build):
                    out_u.append(build[bi]); bi += 1
            out_u.extend(build[bi:])
            if not attn:
                out_u = list(build)
            if h + 1 < H:
                out_u.append(("kbar", h + 1))
            sched.extend(out_u)

        @blk.tensor
        def _(t):
            t.wait_ge(sv, sv_kvb)
            t.wait_ge(ss, ss_p3)
            for unit in sched:
                kind = unit[0]
                if kind == "kbar":
                    h = unit[1]
                    base = (h % 2) * 2 * LT * 128
                    t.wait_ge(sw, sw_p3 + 32 * min(h + 2, H))
                    if h >= 1:
                        t.wait_ge(ss, ss_p3 + (h - 1) * SPH + 1)
                    if h >= 2:
                        t.wait_ge(sl, (h - 1) * TC * (KT // 2 + 1))
                    ins = None
                    for lt in range(LT):
                        ins = t.matmul(ps_m[:, 0:1], wkv_sb[:, base + lt * 128:base + (lt + 1) * 128],
                                       kvlbar[:, lt:lt + 1], start=(lt == 0), stop=(lt == LT - 1))
                    ins.then_inc(stk, 1)
                elif kind == "K":
                    h, kc = unit[1], unit[2]
                    base = (h % 2) * 2 * LT * 128
                    g = h * KC + kc
                    pk = ps_k[0]
                    if g >= 1:
                        t.wait_ge(svk, g)
                    ins = None
                    for lt in range(LT):
                        ins = t.matmul(pk[:], wkv_sb[:, base + lt * 128:base + (lt + 1) * 128],
                                       kvlT[:, lt * S + kc * 512:lt * S + (kc + 1) * 512],
                                       start=(lt == 0), stop=(lt == LT - 1))
                    ins.then_inc(stk, 1)
                elif kind == "V":
                    h, vq = unit[1], unit[2]
                    base = (h % 2) * 2 * LT * 128
                    gv = h * NVU + vq
                    if gv >= 1:
                        t.wait_ge(ss, ss_p3 + ((gv - 1) // NVU) * SPH + 1 + ((gv - 1) % NVU) + 1)
                    ins = None
                    for j in range(4):
                        tt = vq * 4 + j
                        for lt in range(LT):
                            ins = t.matmul(ps_v[:, j * 128:(j + 1) * 128],
                                           kvlT[:, lt * S + tt * 128:lt * S + (tt + 1) * 128],
                                           wkv_sb[:, base + LT * 128 + lt * 128:base + LT * 128 + (lt + 1) * 128],
                                           start=(lt == 0), stop=(lt == LT - 1))
                    ins.then_inc(stk, 1)
                elif kind == "S":
                    h, qc, kt = unit[1], unit[2], unit[3]
                    hb = h % 2
                    u = h * TC + qc
                    pss = ps_sp[(kt // 2) % 2][:, (kt % 2) * 512:(kt % 2 + 1) * 512]
                    if kt >= 4 and kt % 2 == 0:
                        t.wait_ge(se, u * (KT // 2) + kt // 2 - 1)
                    if kt % 4 == 0:
                        t.wait_ge(svk, h * KC + (kt // 4) + 1)
                    ins = t.matmul(pss, KT_sb[:, hb * S + kt * 128:hb * S + (kt + 1) * 128],
                                   QA[:, h * T + qc * 512:h * T + (qc + 1) * 512],
                                   start=True, stop=True)
                    if kt % 2 == 1:
                        ins.then_inc(sts, 1)
                elif kind == "PV":
                    h, qc, kt = unit[1], unit[2], unit[3]
                    hb = h % 2
                    u = h * TC + qc
                    if kt == 0 and u >= 1:
                        t.wait_ge(sv, sv_p3 + 1 + u)
                    if kt % 2 == 0:
                        t.wait_ge(se, u * (KT // 2) + kt // 2 + 1)
                    if kt % 4 == 0:
                        t.wait_ge(ss, ss_p3 + h * SPH + 1 + (kt // 4) + 1)
                    ins = t.matmul(ps_pv0[:], V_sb[:, hb * S + kt * 128:hb * S + (kt + 1) * 128],
                                   expS[:, (kt % 4) * 512:(kt % 4 + 1) * 512],
                                   start=(kt == 0), stop=(kt == KT - 1))
                    if kt % 2 == 1:
                        ins.then_inc(stp, 1)
                else:
                    h, qc = unit[1], unit[2]
                    u = h * TC + qc
                    t.wait_ge(sl, u * (KT // 2 + 1) + KT // 2)
                    if h + 1 <= H - 1:
                        t.wait_ge(ss, ss_p3 + (h + 1) * SPH + 1)
                    if u >= 1:
                        t.wait_ge(sl, u * (KT // 2 + 1))
                    t.matmul(ps_m[0:1, :], ones128f[:], l_part[:, 0:512], start=True, stop=False)
                    t.matmul(ps_m[0:1, :], ones128f[:], l_part[:, 512:1024], start=False, stop=True).then_inc(stp, 1)
                    t.wait_ge(sb2, u + 1)
                    t.matmul(ps_m[:, 0:512], ones1f[:], sums_s[:], start=True, stop=True).then_inc(stp, 1)

        @blk.scalar
        def _(s):
            s.wait_ge(sf0, 16 * ((NT2B + 1) // 2))
            s.wait_ge(sf1, 16 * (NT2B // 2))
            for unit in sched:
                kind = unit[0]
                if kind == "kbar":
                    h = unit[1]
                    s.wait_ge(stk, h * PH3 + 1)
                    if h >= 1:
                        s.wait_ge(svk, h * KC)
                    s.copy(kbar_f[:], ps_m[:, 0:1]).then_inc(ss, 1)
                elif kind == "V":
                    h, vq = unit[1], unit[2]
                    hb = h % 2
                    s.wait_ge(stk, h * PH3 + 1 + KC + vq + 1)
                    if h >= 2:
                        s.wait_ge(stp, (h - 1) * TC * (KT // 2 + 2))
                    s.copy(V_sb[:, hb * S + vq * 512:hb * S + (vq + 1) * 512], ps_v[:]).then_inc(ss, 1)
                elif kind == "S":
                    h, qc, kt = unit[1], unit[2], unit[3]
                    if kt % 2 == 0:
                        continue
                    u = h * TC + qc
                    j = kt // 2
                    s.wait_ge(sts, u * (KT // 2) + j + 1)
                    if j >= 2:
                        s.wait_ge(stp, u * (KT // 2 + 2) + j - 1)
                        s.wait_ge(sl, u * (KT // 2 + 1) + j - 1)
                    s.activation(expS[:, ((j % 2) * 1024):((j % 2) * 1024 + 1024)],
                                 ps_sp[(j % 2)][:], AF.Exp, scale=1.0).then_inc(se, 1)
                elif kind == "tail":
                    h, qc = unit[1], unit[2]
                    u = h * TC + qc
                    s.wait_ge(stp, u * (KT // 2 + 2) + KT // 2 + 1)
                    if u >= 1:
                        s.wait_ge(stp, (u - 1) * (KT // 2 + 2) + KT // 2 + 2)
                    s.copy(sums_s[:], ps_m[0:1, :]).then_inc(sb2, 1)

        @blk.vector
        def _(v):
            v.wait_ge(sf0, 16 * ((NT2B + 1) // 2))
            v.wait_ge(sf1, 16 * (NT2B // 2))
            for unit in sched:
                kind = unit[0]
                if kind == "K":
                    h, kc = unit[1], unit[2]
                    hb = h % 2
                    g = h * KC + kc
                    pk = ps_k[0]
                    v.wait_ge(stk, h * PH3 + 1 + kc + 1)
                    if g > 0:
                        v.wait_ge(svk, g)
                    if kc == 0:
                        v.wait_ge(ss, ss_p3 + h * SPH + 1)
                        if h >= 2:
                            v.wait_ge(sts, (h - 1) * TC * (KT // 2))
                    vjoin(v)
                    vchain(v, v.tensor_mul(krA[:], pk[:], Tck[:, kc * 512:(kc + 1) * 512]))
                    vchain(v, v.tensor_mul(krB[0:64, :], pk[64:128, :], Tsk[0:64, kc * 512:(kc + 1) * 512]))
                    vchain(v, v.tensor_mul(krB[64:128, :], pk[0:64, :], Tsk[64:128, kc * 512:(kc + 1) * 512]))
                    vjoin(v)
                    v.scalar_tensor_tensor(KT_sb[:, hb * S + kc * 512:hb * S + (kc + 1) * 512],
                                           krA[:], kbar_f[:, 0:1], krB[:],
                                           ALU.subtract, ALU.add).then_inc(svk, 1)
                elif kind == "S":
                    h, qc, kt = unit[1], unit[2], unit[3]
                    if kt % 2 == 0:
                        continue
                    u = h * TC + qc
                    j = kt // 2
                    v.wait_ge(se, u * (KT // 2) + j + 1)
                    exs = expS[:, (j % 2) * 1024:(j % 2) * 1024 + 1024]
                    if j == 0:
                        if u >= 1:
                            v.wait_ge(stp, (u - 1) * (KT // 2 + 2) + KT // 2 + 1)
                        v.tensor_scalar(l_part[:], exs, 1.0, None, ALU.mult).then_inc(sl, 1)
                    else:
                        v.wait_ge(sl, u * (KT // 2 + 1) + j)
                        v.tensor_add(l_part[:], l_part[:], exs).then_inc(sl, 1)
                elif kind == "tail":
                    h, qc = unit[1], unit[2]
                    u = h * TC + qc
                    v.wait_ge(stp, u * (KT // 2 + 2) + KT // 2 + 2)
                    if u >= 2:
                        v.wait_ge(sv, sv_p3 + 1 + (u - 2) + 1)
                    v.reciprocal(rbc[:, (u % 2) * 512:(u % 2 + 1) * 512],
                                 ps_m[:, 0:512]).then_inc(sl, 1)
                    v.wait_ge(sl, u * (KT // 2 + 1) + KT // 2 + 1)
                    v.tensor_mul(QA[:, h * T + qc * 512:h * T + (qc + 1) * 512],
                                 ps_pv0[:], rbc[:, (u % 2) * 512:(u % 2 + 1) * 512]).then_inc(sv, 1)
        SV += H * TC
        sv_attn = SV

        c3.close()

        c4 = es.enter_context(contextlib.ExitStack())
        wo_sb = sb("wo_sb", [128, 2 * DT * DOC], BF16, c4)
        xres = sb("xres", [128, 2 * DOC], F32, c4)
        fres = sb("fres", [128, 2 * DOC], BF16, c4)
        ostage = sb("ostage", [128, 2 * DOC], F32, c4)
        ps_o = [psm(f"ps_o{i}", [128, DOC], F32, c4) for i in range(2)]

        sv_p4, sw_p4, st_p4, sx_p4 = SV, SW, ST, SX

        @blk.gpsimd
        def _(gp):
            gp.wait_ge(sv, sv_attn)
            gp.wait_ge(svk, H * KC)
            for dc in range(NDC):
                if dc >= 2:
                    gp.wait_ge(st, st_p4 + (dc - 1) * T8)
                gp.dma_start(
                    out=wo_sb[:, (dc % 2) * DT * DOC:(dc % 2 + 1) * DT * DOC],
                    in_=wo[dc],
                ).then_inc(sw, 16)
        SW += 16 * NDC

        @blk.sync
        def _(sy):
            sy.wait_ge(sv, sv_attn)
            for dc in range(NDC):
                for i in range(T8):
                    k = dc * T8 + i
                    if k >= 2:
                        sy.wait_ge(sv, sv_p4 + k - 1)
                    sy.dma_start(out=xres[:, (k % 2) * DOC:(k % 2 + 1) * DOC],
                                 in_=x_own[i * 128:(i + 1) * 128, dc * DOC:(dc + 1) * DOC]).then_inc(sx, 16)
                    sy.dma_start(out=fres[:, (k % 2) * DOC:(k % 2 + 1) * DOC],
                                 in_=ffn_hbm.ap()[i * 128:(i + 1) * 128, dc * DOC:(dc + 1) * DOC]).then_inc(sx, 16)
        SX += 32 * NDC * T8

        @blk.tensor
        def _(t):
            t.wait_ge(sv, sv_attn)
            for dc in range(NDC):
                t.wait_ge(sw, sw_p4 + 16 * min(dc + 2, NDC))
                for i in range(T8):
                    k = dc * T8 + i
                    po = ps_o[k % 2]
                    if k >= 2:
                        t.wait_ge(sv, sv_p4 + k - 1)
                    ins = None
                    for h in range(DT):
                        o = (dc % 2) * DT * DOC + h * DOC
                        ins = t.matmul(po[:], QA[:, h * T + i * 128:h * T + i * 128 + 128],
                                       wo_sb[:, o:o + DOC], start=(h == 0), stop=(h == DT - 1))
                    ins.then_inc(st, 1)
        ST += NDC * T8

        @blk.vector
        def _(v):
            for k in range(NDC * T8):
                v.wait_ge(st, st_p4 + k + 1)
                v.wait_ge(sx, sx_p4 + 32 * min(k + 2, NDC * T8))
                if k >= 2:
                    v.wait_ge([so, so1][k % 2], 16 * (k // 2))
                osl = ostage[:, (k % 2) * DOC:(k % 2 + 1) * DOC]
                vjoin(v)
                vchain(v, v.tensor_add(osl, ps_o[k % 2][:], xres[:, (k % 2) * DOC:(k % 2 + 1) * DOC]))
                vjoin(v)
                v.tensor_add(osl, osl, fres[:, (k % 2) * DOC:(k % 2 + 1) * DOC]).then_inc(sv, 1)
        SV += NDC * T8

        @blk.scalar
        def _(s):
            for dc in range(NDC):
                for i in range(T8):
                    k = dc * T8 + i
                    s.wait_ge(sv, sv_p4 + k + 1)
                    s.dma_start(out=out[i * 128:(i + 1) * 128, dc * DOC:(dc + 1) * DOC],
                                in_=ostage[:, (k % 2) * DOC:(k % 2 + 1) * DOC]).then_inc([so, so1][k % 2], 16)
            s.wait_ge(so, 16 * ((NDC * T8 + 1) // 2))
            s.wait_ge(so1, 16 * ((NDC * T8) // 2))

        c4.close()

    return nc


def _host_layouts(inputs, cfg: CFG):
    D, T, S, ER = cfg.D, cfg.T, cfg.S, cfg.ER
    f32 = np.float32
    W_gate = np.asarray(inputs["L0_ffn_W_gate"])
    W_thr = np.asarray(inputs["L0_ffn_W_thr"])
    A_w = np.asarray(inputs["L0_ffn_A_w"])
    B_w = np.asarray(inputs["L0_ffn_B_w"])
    DT, LT, H, NDC, DOC, SM = cfg.DT, cfg.LT, cfg.H, cfg.NDC, cfg.DOC, cfg.SM

    def tile_din(w, np_, cw):
        return np.ascontiguousarray(
            np.asarray(w, f32).reshape(DT, 128, np_, cw).transpose(2, 1, 0, 3).reshape(np_, 128, DT * cw))

    def tile_lat(w, nh):
        return np.ascontiguousarray(
            np.asarray(w, f32).reshape(LT, 128, nh, 128).transpose(2, 1, 0, 3).reshape(nh, 128, LT * 128))

    wsm_flat = np.concatenate(
        [W_gate.T, W_thr[:, None], np.transpose(A_w, (2, 0, 1)).reshape(D, ER)], axis=1)
    common = {
        "wq": tile_din(inputs["L0_attn_Wq"], H // 2, 256),
        "wc": np.ascontiguousarray(inputs["L0_attn_Wc"], f32),
        "wk": tile_lat(inputs["L0_attn_Wk"], H),
        "wv": tile_lat(inputs["L0_attn_Wv"], H),
        "wo": tile_din(inputs["L0_attn_Wo"], NDC, DOC),
        "wbt": tile_din(np.asarray(inputs["L0_ffn_W_base"]).T, NDC, DOC),
        "wsm": np.ascontiguousarray(
            np.asarray(wsm_flat, f32).reshape(DT, 128, SM).transpose(1, 0, 2).reshape(128, DT * SM)),
        "bflat": np.ascontiguousarray(np.transpose(B_w, (0, 2, 1)).reshape(ER, D), f32),
        "bbase": np.ascontiguousarray(np.asarray(inputs["L0_ffn_b_base"], f32)[None, :]),
        "bthr": np.ascontiguousarray(np.broadcast_to(
            np.asarray(inputs["L0_ffn_b_thr"], f32).reshape(1, 1), (128, 1))),
        "posk": np.ascontiguousarray(np.broadcast_to(np.arange(S, dtype=f32)[None, :], (128, S))),
        "dind": np.ascontiguousarray((np.arange(128, dtype=f32) % 64)[:, None]),
        "ident": np.eye(128, dtype=f32),
    }
    x = np.asarray(inputs["L0_attn_hidden_states"], f32)
    in_maps = []
    for c in range(8):
        b, qq = c // 4, c % 4
        m = dict(common)
        m["x_own"] = np.ascontiguousarray(x[b, qq * T:(qq + 1) * T, :])
        m["posq"] = np.ascontiguousarray(np.broadcast_to(
            np.arange(qq * T, (qq + 1) * T, dtype=f32)[None, :], (128, T)))
        in_maps.append(m)
    return in_maps


def kernel(**inputs):
    cfg = CFG()
    nc = build_program(cfg)
    in_maps = _host_layouts(inputs, cfg)
    res = run_bass_kernel_spmd(nc, in_maps, core_ids=list(range(8)))
    out = np.empty((2, 4 * cfg.T, cfg.D), dtype=np.float32)
    for c in range(8):
        b, qq = c // 4, c % 4
        out[b, qq * cfg.T:(qq + 1) * cfg.T, :] = res.results[c]["out"]
    return out


# version 20 (final)
# speedup vs baseline: 1.1914x; 1.0355x over previous
import contextlib
import math
import numpy as np
import concourse.bass as bass
import concourse.mybir as mybir
import concourse.bass_utils as _bu
from concourse.bass_utils import run_bass_kernel_spmd

if not getattr(_bu, "_ldw_patched", False):
    _orig_run_command = _bu.run_command

    def _run_command_ldw(cmd, *a, **kw):
        if isinstance(cmd, (list, tuple)):
            cmd = [c.replace("--enable-ldw-opt=false", "--enable-ldw-opt=true")
                   if isinstance(c, str) else c for c in cmd]
        return _orig_run_command(cmd, *a, **kw)

    _bu.run_command = _run_command_ldw
    _bu._ldw_patched = True

F32 = mybir.dt.float32
BF16 = mybir.dt.bfloat16
AF = mybir.ActivationFunctionType
ALU = mybir.AluOpType
AX = mybir.AxisListType

PI = math.pi
LN_THETA = math.log(10000.0)


class CFG:
    def __init__(self, D=4096, L=512, H=32, T=1024):
        self.D, self.L, self.H, self.T = D, L, H, T
        self.S = 4 * T
        self.E, self.R = 8, 16
        self.ER = self.E * self.R
        self.DT = D // 128
        self.LT = L // 128
        self.TC = T // 512
        self.T8 = T // 128
        self.KT = self.S // 128
        self.KC = self.S // 512
        self.G = 4
        self.DOC = 256
        self.NDC = D // self.DOC
        self.SM = 9 + self.ER


def build_program(cfg: CFG):
    D, L, H, T, S = cfg.D, cfg.L, cfg.H, cfg.T, cfg.S
    DT, LT, TC, T8, KT, KC = cfg.DT, cfg.LT, cfg.TC, cfg.T8, cfg.KT, cfg.KC
    ER, DOC, NDC, G, SM = cfg.ER, cfg.DOC, cfg.NDC, cfg.G, cfg.SM
    SQ = 1.0 / math.sqrt(128.0)
    NVU = KT // 4
    PH3 = 1 + KC + NVU

    nc = bass.Bass("TRN2", target_bir_lowering=False, debug=False)

    x_own = nc.dram_tensor("x_own", [T, D], F32, kind="ExternalInput").ap()
    wq = nc.dram_tensor("wq", [H // 2, 128, DT * 256], F32, kind="ExternalInput").ap()
    wc = nc.dram_tensor("wc", [D, L], F32, kind="ExternalInput").ap()
    wk = nc.dram_tensor("wk", [H, 128, LT * 128], F32, kind="ExternalInput").ap()
    wv = nc.dram_tensor("wv", [H, 128, LT * 128], F32, kind="ExternalInput").ap()
    wo = nc.dram_tensor("wo", [NDC, 128, DT * DOC], F32, kind="ExternalInput").ap()
    wbt = nc.dram_tensor("wbt", [NDC, 128, DT * DOC], F32, kind="ExternalInput").ap()
    wsm = nc.dram_tensor("wsm", [128, DT * SM], F32, kind="ExternalInput").ap()
    bflat = nc.dram_tensor("bflat", [ER, D], F32, kind="ExternalInput").ap()
    bbase = nc.dram_tensor("bbase", [1, D], F32, kind="ExternalInput").ap()
    bthr = nc.dram_tensor("bthr", [128, 1], F32, kind="ExternalInput").ap()
    posq = nc.dram_tensor("posq", [128, T], F32, kind="ExternalInput").ap()
    posk = nc.dram_tensor("posk", [128, S], F32, kind="ExternalInput").ap()
    dind = nc.dram_tensor("dind", [128, 1], F32, kind="ExternalInput").ap()
    ident_in = nc.dram_tensor("ident", [128, 128], F32, kind="ExternalInput").ap()
    out = nc.dram_tensor("out", [T, D], F32, kind="ExternalOutput").ap()

    cc_in = nc.dram_tensor("cc_in", [LT, 128, T], BF16)
    cc_out = nc.dram_tensor("cc_out", [G, LT, 128, T], BF16)
    ffn_hbm = nc.dram_tensor("ffn_hbm", [T, D], BF16)

    es = contextlib.ExitStack()

    with es:
        def sb(name, shape, dt, stack=es):
            return stack.enter_context(nc.sbuf_tensor(name, shape, dt))

        def psm(name, shape, dt, stack):
            return stack.enter_context(nc.psum_tensor(name, shape, dt))

        ones_col = sb("ones_col", [1, 128], BF16)
        ones128f = sb("ones128f", [128, 1], F32)
        ones1f = sb("ones1f", [1, 128], F32)
        Tck = sb("Tck", [128, S], BF16)
        Tsk = sb("Tsk", [128, S], BF16)
        Tcq = sb("Tcq", [128, T], BF16)
        Tsq = sb("Tsq", [128, T], BF16)
        QA = sb("QA", [128, H * T], BF16)
        hwT = sb("hwT", [128, T], BF16)
        ident_s = sb("ident_s", [128, 128], BF16)
        bthr_s = sb("bthr_s", [128, 1], F32)

        sx = es.enter_context(nc.semaphore("sx"))
        sw = es.enter_context(nc.semaphore("sw"))
        scc = es.enter_context(nc.semaphore("scc"))
        st = es.enter_context(nc.semaphore("st"))
        stk = es.enter_context(nc.semaphore("stk"))
        sts = es.enter_context(nc.semaphore("sts"))
        stp = es.enter_context(nc.semaphore("stp"))
        se = es.enter_context(nc.semaphore("se"))
        sl = es.enter_context(nc.semaphore("sl"))
        sv = es.enter_context(nc.semaphore("sv"))
        ss = es.enter_context(nc.semaphore("ss"))
        so = es.enter_context(nc.semaphore("so"))
        sm0 = es.enter_context(nc.semaphore("sm0"))
        sf0 = es.enter_context(nc.semaphore("sf0"))
        sf1 = es.enter_context(nc.semaphore("sf1"))
        so1 = es.enter_context(nc.semaphore("so1"))
        svc = es.enter_context(nc.semaphore("svc"))
        sb2 = es.enter_context(nc.semaphore("sb2"))
        svk = es.enter_context(nc.semaphore("svk"))
        VC = [0]

        def vchain(v, ins):
            ins.then_inc(svc, 1)
            VC[0] += 1

        def vjoin(v):
            v.wait_ge(svc, VC[0])

        blk = es.enter_context(nc.Block())

        SX = SW = SCC = ST = SV = SS = 0

        c1 = es.enter_context(contextlib.ExitStack())
        xnT = sb("xnT", [128, DT * T], BF16, c1)

        c0 = c1.enter_context(contextlib.ExitStack())
        th = sb("th", [128, S], F32, c0)
        pq_s = sb("pq_s", [128, T], F32, c0)
        tk0 = sb("tk0", [128, S], F32, c0)
        ck0 = sb("ck0", [128, S], F32, c0)
        invf = sb("invf", [128, 1], F32, c0)

        @blk.sync
        def _(sy):
            sy.dma_start(out=th[:], in_=posk).then_inc(sx, 16)
            sy.dma_start(out=pq_s[:], in_=posq).then_inc(sx, 16)
            sy.dma_start(out=invf[:], in_=dind).then_inc(sx, 16)
            sy.dma_start(out=bthr_s[:], in_=bthr).then_inc(sx, 16)
        SX += 64

        @blk.gpsimd
        def _(gp):
            gp.dma_start(out=ident_s[:], in_=ident_in).then_inc(sw, 16)
            gp.memset(ones_col[:], 1.0)
            gp.memset(ones128f[:], 1.0)
            gp.memset(ones1f[:], 1.0).then_inc(sm0, 1)
        SW += 16

        @blk.scalar
        def _(s):
            s.wait_ge(sx, 64)
            s.activation(invf[:], invf[:], AF.Exp, scale=-LN_THETA / 64.0).then_inc(ss, 1)
        SS += 1

        def emit_wrap(v, pos_sl, tk, ck):
            vchain(v, v.tensor_scalar(tk, pos_sl, 1.0 / (2 * PI), None, ALU.mult))
            for b in [512.0, 256.0, 128.0, 64.0, 32.0, 16.0, 8.0, 4.0, 2.0, 1.0]:
                vjoin(v)
                vchain(v, v.tensor_scalar(ck, tk, b, None, ALU.is_ge))
                vjoin(v)
                vchain(v, v.scalar_tensor_tensor(tk, ck, -b, tk, ALU.mult, ALU.add))
            vjoin(v)
            v.tensor_scalar(tk, tk, 2 * PI, -PI, ALU.mult, ALU.add).then_inc(sv, 1)

        def emit_tables(pos_ap, width, Tc, Ts, fold, SV, SS):
            ss0, sv0 = SS, SV
            tk, ck = tk0[:, :width], ck0[:, :width]

            @blk.vector
            def _(v):
                v.wait_ge(ss, ss0)
                vjoin(v)
                vchain(v, v.tensor_scalar(pos_ap, pos_ap, invf[:, 0:1], None, ALU.mult))
                vjoin(v)
                emit_wrap(v, pos_ap, tk, ck)

            @blk.scalar
            def _(s):
                s.wait_ge(sv, sv0 + 1)
                s.activation(Ts[:, :width], tk, AF.Sin, scale=1.0).then_inc(ss, 1)

            @blk.vector
            def _(v):
                v.wait_ge(ss, ss0 + 1)
                v.tensor_scalar(Ts[0:64, :width], Ts[0:64, :width], fold, None, ALU.mult)
                v.tensor_scalar(Ts[64:128, :width], Ts[64:128, :width], -fold, None, ALU.mult)
                vjoin(v)
                vchain(v, v.tensor_scalar(pos_ap, pos_ap, PI / 2, None, ALU.add))
                vjoin(v)
                emit_wrap(v, pos_ap, tk, ck)

            @blk.scalar
            def _(s):
                s.wait_ge(sv, sv0 + 2)
                s.activation(Tc[:, :width], tk, AF.Sin, scale=1.0).then_inc(ss, 1)

            @blk.vector
            def _(v):
                v.wait_ge(ss, ss0 + 2)
                v.tensor_scalar(Tc[:, :width], Tc[:, :width], -fold, None, ALU.mult).then_inc(sv, 1)
            return sv0 + 3, ss0 + 2

        SV, SS = emit_tables(th[:], S, Tck, Tsk, 1.0, SV, SS)
        SV, SS = emit_tables(pq_s[:], T, Tcq, Tsq, SQ, SV, SS)
        sv_tab, ss_tab = SV, SS

        c0.close()
        c1p = c1.enter_context(contextlib.ExitStack())
        xst = sb("xst", [128, 2 * D], F32, c1p)
        xn_tm = sb("xn_tm", [128, 2 * D], BF16, c1p)
        stats = sb("stats", [128, 10], F32, c1p)
        ps_tp = [psm(f"ps_tp{i}", [128, 512], BF16, c1p) for i in range(2)]

        NQ4 = DT // 4
        sx_p1, ss_p1, sv_p1, st_p1 = SX, SS, SV, ST

        @blk.sync
        def _(sy):
            for i in range(T8):
                if i == 0:
                    sy.wait_ge(ss, ss_tab)
                if i >= 2:
                    sy.wait_ge(ss, ss_p1 + 3 * (i - 1))
                sy.dma_start(out=xst[:, (i % 2) * D:(i % 2 + 1) * D],
                             in_=x_own[i * 128:(i + 1) * 128, :]).then_inc(sx, 16)
        SX += 16 * T8

        PCH = 2 + NQ4

        def _evac_val(k):
            i, q4 = k // NQ4, k % NQ4
            if i + 1 < T8:
                return sv_p1 + 2 * (i + 2) + i * NQ4 + q4 + 1
            return sv_p1 + 2 * T8 + (T8 - 1) * NQ4 + q4 + 1

        def _emit_evac(v, i, q4):
            k = i * NQ4 + q4
            v.wait_ge(st, st_p1 + k + 1)
            dst = xnT[:].rearrange("p (d t) -> p d t", d=DT, t=T)[
                :, q4 * 4:(q4 + 1) * 4, i * 128:(i + 1) * 128]
            v.tensor_copy(dst, ps_tp[k % 2][:].rearrange(
                "p (d c) -> p d c", d=4, c=128)).then_inc(sv, 1)

        @blk.vector
        def _(v):
            for i in range(T8):
                o = (i % 2) * 5
                mean_, msq, var_, rstd, nb = (stats[:, o + j:o + j + 1] for j in range(5))
                xb = xst[:, (i % 2) * D:(i % 2 + 1) * D]
                v.wait_ge(sx, sx_p1 + 16 * min(i + 2, T8))
                if i >= 2:
                    v.wait_ge(ss, ss_p1 + 3 * (i - 1))
                v.wait_ge(ss, ss_p1 + 3 * i + 1)
                vjoin(v)
                vchain(v, v.tensor_scalar(xn_tm[:, (i % 2) * D:(i % 2 + 1) * D], xb,
                                          1.0 / D, None, ALU.mult, ALU.add, accum_out=mean_))
                vjoin(v)
                vchain(v, v.tensor_mul(var_, mean_, mean_))
                vjoin(v)
                vchain(v, v.tensor_sub(var_, msq, var_))
                vjoin(v)
                vchain(v, v.tensor_scalar(var_, var_, 1e-5, None, ALU.add))
                vjoin(v)
                v.reciprocal(var_, var_).then_inc(sv, 1)
                v.wait_ge(ss, ss_p1 + 3 * i + 2)
                v.scalar_tensor_tensor(nb, mean_, -1.0, rstd, ALU.mult, ALU.mult).then_inc(sv, 1)
                if i >= 1:
                    for q4 in range(NQ4):
                        _emit_evac(v, i - 1, q4)
            for q4 in range(NQ4):
                _emit_evac(v, T8 - 1, q4)
        SV += T8 * PCH

        import math as _math
        RSD = 1.0 / _math.sqrt(D)

        @blk.scalar
        def _(s):
            for i in range(T8):
                o = (i % 2) * 5
                msq_, var_, rstd, nb = (stats[:, o + j:o + j + 1] for j in range(1, 5))
                s.wait_ge(sx, sx_p1 + 16 * min(i + 2, T8))
                if i >= 2:
                    s.wait_ge(sv, sv_p1 + 2 * (i - 2) + max(i - 3, 0) * NQ4 + 2)
                    s.wait_ge(st, st_p1 + (i - 1) * NQ4)
                s.activation(xn_tm[:, (i % 2) * D:(i % 2 + 1) * D],
                             xst[:, (i % 2) * D:(i % 2 + 1) * D], AF.Square,
                             scale=RSD, accum_out=msq_).then_inc(ss, 1)
                s.wait_ge(sv, sv_p1 + 2 * i + max(i - 1, 0) * NQ4 + 1)
                s.activation(rstd, var_, AF.Sqrt).then_inc(ss, 1)
                s.wait_ge(sv, sv_p1 + 2 * i + max(i - 1, 0) * NQ4 + 2)
                if i >= 2:
                    s.wait_ge(st, st_p1 + (i - 1) * NQ4)
                s.activation(xn_tm[:, (i % 2) * D:(i % 2 + 1) * D],
                             xst[:, (i % 2) * D:(i % 2 + 1) * D], AF.Identity,
                             bias=nb, scale=rstd).then_inc(ss, 1)
        SS += 3 * T8

        @blk.tensor
        def _(t):
            t.wait_ge(sw, 16)
            t.wait_ge(sm0, 1)
            for i in range(T8):
                t.wait_ge(ss, ss_p1 + 3 * i + 3)
                xb = xn_tm[:, (i % 2) * D:(i % 2 + 1) * D]
                for q4 in range(NQ4):
                    k = i * NQ4 + q4
                    pst = ps_tp[k % 2]
                    if k >= 2:
                        t.wait_ge(sv, _evac_val(k - 2))
                    ins = None
                    for j in range(4):
                        dt = q4 * 4 + j
                        ins = t.transpose(pst[:, j * 128:(j + 1) * 128],
                                          xb[:, dt * 128:(dt + 1) * 128], ident_s[:])
                    ins.then_inc(st, 1)
        ST += T8 * NQ4
        sv_xnT = SV

        c1p.close()

        c1b = c1.enter_context(contextlib.ExitStack())
        wc_sb = sb("wc_sb", [128, 2 * L], BF16, c1b)
        kv_stage = sb("kv_stage", [128, LT * T], BF16, c1b)
        ps_kv = [psm(f"ps_kv{i}", [128, 512], F32, c1b) for i in range(LT * TC)]

        sw_kv, st_kv, ss_kv = SW, ST, SS

        @blk.gpsimd
        def _(gp):
            gp.wait_ge(sv, sv_xnT)
            for dt in range(DT):
                if dt >= 2:
                    gp.wait_ge(st, st_kv + (dt - 1) * LT * TC)
                gp.dma_start(out=wc_sb[:, (dt % 2) * L:(dt % 2) * L + L],
                             in_=wc[dt * 128:(dt + 1) * 128, :]).then_inc(sw, 16)
        SW += 16 * DT

        @blk.tensor
        def _(t):
            t.wait_ge(sv, sv_xnT)
            for dt in range(DT):
                t.wait_ge(sw, sw_kv + 16 * min(dt + 2, DT))
                for lt in range(LT):
                    for tc in range(TC):
                        t.matmul(ps_kv[lt * TC + tc][:],
                                 wc_sb[:, (dt % 2) * L + lt * 128:(dt % 2) * L + (lt + 1) * 128],
                                 xnT[:, dt * T + tc * 512:dt * T + (tc + 1) * 512],
                                 start=(dt == 0), stop=(dt == DT - 1)).then_inc(st, 1)
        ST += DT * LT * TC

        @blk.scalar
        def _(s):
            s.wait_ge(st, ST)
            for lt in range(LT):
                for tc in range(TC):
                    s.copy(kv_stage[:, lt * T + tc * 512:lt * T + (tc + 1) * 512],
                           ps_kv[lt * TC + tc][:]).then_inc(ss, 1)
        SS += LT * TC

        sw_cc = SW

        @blk.gpsimd
        def _(gp):
            gp.wait_ge(ss, SS)
            for lt in range(LT):
                gp.dma_start(out=cc_in.ap()[lt], in_=kv_stage[:, lt * T:(lt + 1) * T]).then_inc(sw, 16)
            gp.wait_ge(sw, sw_cc + 16 * LT)
            gp.collective_compute(
                "AllGather", ALU.bypass,
                replica_groups=[[0, 1, 2, 3], [4, 5, 6, 7]],
                ins=[cc_in.ap().opt()],
                outs=[cc_out.ap().opt()],
            ).then_inc(scc, 1)
        SW += 16 * LT
        SCC = 1

        c1b.close()

        c2a = c1.enter_context(contextlib.ExitStack())
        wq_sb = sb("wq_sb", [128, 2 * DT * 256], BF16, c2a)
        wsm_sb_fwd = [sb("wsm_sb", [128, DT * SM], BF16, c2a)]
        ropeA = sb("ropeA", [128, 2 * 512], F32, c2a)
        ropeB = sb("ropeB", [128, 2 * 512], F32, c2a)
        ps_q = [psm(f"ps_q{i}", [128, 512], F32, c2a) for i in range(4)]

        NHP = H // 2
        NU = H * TC
        sw_q, st_q, ss_q, sv_q = SW, ST, SS, SV

        @blk.gpsimd
        def _(gp):
            gp.wait_ge(st, st_q)
            gp.wait_ge(ss, ss_q)
            gp.dma_start(out=wsm_sb_fwd[0][:], in_=wsm).then_inc(sw, 16)
            for hp in range(NHP):
                if hp >= 2:
                    gp.wait_ge(st, st_q + (hp - 1) * 2 * TC)
                gp.dma_start(
                    out=wq_sb[:, (hp % 2) * DT * 256:(hp % 2 + 1) * DT * 256],
                    in_=wq[hp],
                ).then_inc(sw, 16)
        SW += 16 * (NHP + 1)

        @blk.tensor
        def _(t):
            t.wait_ge(ss, ss_q)
            for hp in range(NHP):
                t.wait_ge(sw, sw_q + 16 + 16 * min(hp + 2, NHP))
                for h2 in range(2):
                    for tc in range(TC):
                        u = (hp * 2 + h2) * TC + tc
                        pq = ps_q[u % 4]
                        if u >= 4:
                            t.wait_ge(sv, sv_q + (u - 4) + 1)
                        ins = None
                        for dt in range(DT):
                            base = (hp % 2) * DT * 256 + dt * 256 + h2 * 128
                            ins = t.matmul(pq[:], wq_sb[:, base:base + 128],
                                           xnT[:, dt * T + tc * 512:dt * T + (tc + 1) * 512],
                                           start=(dt == 0), stop=(dt == DT - 1))
                        ins.then_inc(st, 1)
        ST += NU

        @blk.vector
        def _(v):
            v.wait_ge(sv, sv_q)
            for u in range(NU):
                h, tc = u // TC, u % TC
                pq = ps_q[u % 4]
                ra = ropeA[:, (u % 2) * 512:(u % 2 + 1) * 512]
                rb = ropeB[:, (u % 2) * 512:(u % 2 + 1) * 512]
                v.wait_ge(st, st_q + u + 1)
                if u >= 2:
                    v.wait_ge(sv, sv_q + u - 1)
                vjoin(v)
                vchain(v, v.tensor_mul(ra, pq[:], Tcq[:, tc * 512:(tc + 1) * 512]))
                vchain(v, v.tensor_mul(rb[0:64, :], pq[64:128, :], Tsq[0:64, tc * 512:(tc + 1) * 512]))
                vchain(v, v.tensor_mul(rb[64:128, :], pq[0:64, :], Tsq[64:128, tc * 512:(tc + 1) * 512]))
                vjoin(v)
                v.tensor_add(QA[:, h * T + tc * 512:h * T + (tc + 1) * 512], ra, rb).then_inc(sv, 1)
        SV += NU

        wsm_sb = wsm_sb_fwd[0]
        fsc = sb("fsc", [128, 32], F32, c2a)
        hw_tm = sb("hw_tm", [128, 128], BF16, c2a)
        ps_sm = psm("ps_sm", [128, SM], F32, c2a)
        ps_hwt = psm("ps_hwt", [128, 128], BF16, c2a)

        sw_sm, st_sm, ss_sm, sv_sm = SW, ST, SS, SV

        @blk.tensor
        def _(t):
            t.wait_ge(sw, sw_sm)
            for i in range(T8):
                if i >= 1:
                    t.wait_ge(sv, sv_sm + i)
                ins = None
                for dt in range(DT):
                    ins = t.matmul(ps_sm[:], xnT[:, dt * T + i * 128:dt * T + i * 128 + 128],
                                   wsm_sb[:, dt * SM:(dt + 1) * SM],
                                   start=(dt == 0), stop=(dt == DT - 1))
                ins.then_inc(st, 1)
                t.wait_ge(sv, sv_sm + i + 1)
                t.transpose(ps_hwt[:], hw_tm[:], ident_s[:]).then_inc(st, 1)
        ST += 2 * T8

        @blk.scalar
        def _(s):
            for i in range(T8):
                s.wait_ge(st, st_sm + 2 * i + 1)
                s.activation(fsc[:, 0:8], ps_sm[:, 0:8], AF.Exp, scale=1.0).then_inc(ss, 1)
                s.activation(fsc[:, 8:9], ps_sm[:, 8:9], AF.Sigmoid,
                             bias=bthr_s[:, 0:1], scale=1.0).then_inc(ss, 1)
                s.wait_ge(st, st_sm + 2 * i + 2)
                s.copy(hwT[:, i * 128:(i + 1) * 128], ps_hwt[:]).then_inc(ss, 1)
        SS += 3 * T8

        @blk.vector
        def _(v):
            for i in range(T8):
                v.wait_ge(ss, ss_sm + 3 * i + 2)
                if i >= 1:
                    v.wait_ge(st, st_sm + 2 * i)
                gsum, thr, iz, winv = fsc[:, 9:10], fsc[:, 10:11], fsc[:, 11:12], fsc[:, 12:13]
                gate = fsc[:, 16:24]
                ops = [
                    lambda: v.reduce_sum(gsum, fsc[:, 0:8], axis=AX.X),
                    lambda: v.reciprocal(gsum, gsum),
                    lambda: v.tensor_scalar(gate, fsc[:, 0:8], gsum[:, 0:1], None, ALU.mult),
                    lambda: v.tensor_scalar(thr, fsc[:, 8:9], 0.125, None, ALU.mult),
                    lambda: v.tensor_scalar(gate, gate, thr[:, 0:1], None, ALU.subtract),
                    lambda: v.tensor_relu(gate, gate),
                    lambda: v.reduce_sum(gsum, gate, axis=AX.X),
                    lambda: v.tensor_scalar(iz, gsum, 0.0, None, ALU.is_le),
                    lambda: v.tensor_add(gsum, gsum, iz),
                    lambda: v.reciprocal(winv, gsum),
                    lambda: v.tensor_scalar(gate, gate, 2.0, None, ALU.mult),
                    lambda: v.tensor_scalar(gate, gate, winv[:, 0:1], None, ALU.mult),
                ]
                for op in ops:
                    vjoin(v)
                    vchain(v, op())
                vjoin(v)
                ins = None
                for e in range(8):
                    ins = v.tensor_scalar(hw_tm[:, e * 16:(e + 1) * 16],
                                          ps_sm[:, 9 + e * 16:9 + (e + 1) * 16],
                                          gate[:, e:e + 1], None, ALU.mult)
                ins.then_inc(sv, 1)
        SV += T8

        c2a.close()

        c2b = c1.enter_context(contextlib.ExitStack())
        wbt_sb = sb("wbt_sb", [128, 2 * DT * DOC], BF16, c2b)
        bflat_sb = sb("bflat_sb", [128, D], BF16, c2b)
        bbase_s = sb("bbase_s", [1, D], BF16, c2b)
        fstage = sb("fstage", [128, 2 * DOC], BF16, c2b)
        ps_b = [psm(f"ps_b{i}", [128, DOC], F32, c2b) for i in range(2)]

        sw_b, st_b, ss_b, sx_b, sv_b = SW, ST, SS, SX, SV

        @blk.gpsimd
        def _(gp):
            gp.wait_ge(st, st_b)
            gp.wait_ge(sv, sv_b)
            gp.wait_ge(ss, ss_b)
            gp.dma_start(out=bflat_sb[:], in_=bflat).then_inc(sw, 16)
            gp.dma_start(out=bbase_s[:], in_=bbase).then_inc(sw, 16)
            for dc in range(NDC):
                if dc >= 2:
                    gp.wait_ge(st, st_b + (dc - 1) * T8)
                gp.dma_start(
                    out=wbt_sb[:, (dc % 2) * DT * DOC:(dc % 2 + 1) * DT * DOC],
                    in_=wbt[dc],
                ).then_inc(sw, 16)
        SW += 16 * (2 + NDC)

        @blk.tensor
        def _(t):
            t.wait_ge(ss, ss_b)
            for dc in range(NDC):
                t.wait_ge(sw, sw_b + 32 + 16 * min(dc + 2, NDC))
                for i in range(T8):
                    k = dc * T8 + i
                    pb = ps_b[k % 2]
                    if k >= 2:
                        t.wait_ge(ss, ss_b + k - 1)
                    for dt in range(DT):
                        o = (dc % 2) * DT * DOC + dt * DOC
                        t.matmul(pb[:], xnT[:, dt * T + i * 128:dt * T + i * 128 + 128],
                                 wbt_sb[:, o:o + DOC],
                                 start=(dt == 0), stop=False)
                    t.matmul(pb[:], hwT[:, i * 128:(i + 1) * 128],
                             bflat_sb[:, dc * DOC:(dc + 1) * DOC], start=False, stop=False)
                    t.matmul(pb[:], ones_col[:], bbase_s[0:1, dc * DOC:(dc + 1) * DOC],
                             start=False, stop=True).then_inc(st, 1)
        ST += NDC * T8

        @blk.scalar
        def _(s):
            for k in range(NDC * T8):
                s.wait_ge(st, st_b + k + 1)
                if k >= 2:
                    s.wait_ge([sf0, sf1][k % 2], 16 * (k // 2))
                s.copy(fstage[:, (k % 2) * DOC:(k % 2 + 1) * DOC], ps_b[k % 2][:]).then_inc(ss, 1)
        SS += NDC * T8

        @blk.sync
        def _(sy):
            for dc in range(NDC):
                for i in range(T8):
                    k = dc * T8 + i
                    sy.wait_ge(ss, ss_b + k + 1)
                    sy.dma_start(out=ffn_hbm.ap()[i * 128:(i + 1) * 128, dc * DOC:(dc + 1) * DOC],
                                 in_=fstage[:, (k % 2) * DOC:(k % 2 + 1) * DOC]).then_inc([sf0, sf1][k % 2], 16)
        NT2B = NDC * T8

        c2b.close()
        c1.close()

        c3 = es.enter_context(contextlib.ExitStack())
        kvlT = sb("kvlT", [128, LT * S], BF16, c3)
        wkv_sb = sb("wkv_sb", [128, 2 * 2 * LT * 128], BF16, c3)
        KT_sb = sb("KT_sb", [128, 2 * S], BF16, c3)
        V_sb = sb("V_sb", [128, 2 * S], BF16, c3)
        expS = sb("expS", [128, 4 * 512], BF16, c3)
        l_part = sb("l_part", [128, 1024], F32, c3)
        kvlbar = sb("kvlbar", [128, LT], BF16, c3)
        kvlbar_f = sb("kvlbar_f", [128, LT], F32, c3)
        kbar_f = sb("kbar_f", [128, 1], F32, c3)
        sums_s = sb("sums_s", [1, 512], F32, c3)
        krA = sb("krA", [128, 512], F32, c3)
        krB = sb("krB", [128, 512], F32, c3)
        rbc = sb("rbc", [128, 2 * 512], F32, c3)
        ps_k = [psm("ps_k0", [128, 512], F32, c3)]
        ps_v = psm("ps_v", [128, 512], F32, c3)
        ps_sp = [psm(f"ps_sp{i}", [128, 1024], F32, c3) for i in range(2)]
        ps_pv0 = psm("ps_pv0", [128, 512], F32, c3)
        ps_m = psm("ps_m", [128, 512], F32, c3)

        sx_kvl, sv_p3, ss_p3, sw_p3 = SX, SV, SS, SW
        st_p2 = ST

        @blk.sync
        def _(sy):
            sy.wait_ge(scc, SCC)
            sy.wait_ge(st, st_p2)
            sy.wait_ge(sv, sv_p3)
            sy.wait_ge(ss, ss_p3)
            for r in range(G):
                for lt in range(LT):
                    sy.dma_start(out=kvlT[:, lt * S + r * T:lt * S + (r + 1) * T],
                                 in_=cc_out.ap()[r, lt]).then_inc(sx, 16)
        SX += 16 * G * LT
        sx_kv_done = SX

        @blk.vector
        def _(v):
            v.wait_ge(sx, sx_kv_done)
            v.wait_ge(sf0, 16 * ((NT2B + 1) // 2))
            v.wait_ge(sf1, 16 * (NT2B // 2))
            for lt in range(LT):
                vchain(v, v.reduce_sum(kvlbar_f[:, lt:lt + 1], kvlT[:, lt * S:(lt + 1) * S], axis=AX.X))
            vjoin(v)
            vchain(v, v.tensor_scalar(kvlbar_f[:], kvlbar_f[:], 1.0 / S, None, ALU.mult))
            vjoin(v)
            v.tensor_copy(kvlbar[:], kvlbar_f[:]).then_inc(sv, 1)
        SV += 1
        sv_kvb = SV

        @blk.gpsimd
        def _(gp):
            gp.wait_ge(st, st_p2)
            gp.wait_ge(ss, ss_p3)
            gp.wait_ge(sf0, 16 * ((NT2B + 1) // 2))
            gp.wait_ge(sf1, 16 * (NT2B // 2))
            for h in range(H):
                if h >= 2:
                    gp.wait_ge(stk, (h - 1) * PH3)
                base = (h % 2) * 2 * LT * 128
                gp.dma_start(out=wkv_sb[:, base:base + LT * 128], in_=wk[h]).then_inc(sw, 16)
                gp.dma_start(out=wkv_sb[:, base + LT * 128:base + 2 * LT * 128],
                             in_=wv[h]).then_inc(sw, 16)
        SW += 32 * H

        VPH = KC + TC
        SPH = 1 + NVU

        sched = [("kbar", 0)]
        for h in range(H + 1):
            build = []
            if h < H:
                build = [("K", h, kc) for kc in range(KC)] + \
                        [("V", h, vq) for vq in range(NVU)]
            attn = []
            if h >= 1:
                for qc in range(TC):
                    for kt in range(KT):
                        attn.append(("S", h - 1, qc, kt))
                        if kt >= 4:
                            attn.append(("PV", h - 1, qc, kt - 4))
                    for kt in range(KT - 4, KT):
                        attn.append(("PV", h - 1, qc, kt))
                    attn.append(("tail", h - 1, qc))
            out_u, bi = [], 0
            for n_, a_ in enumerate(attn):
                out_u.append(a_)
                if n_ % 8 == 3 and bi < len(build):
                    out_u.append(build[bi]); bi += 1
            out_u.extend(build[bi:])
            if not attn:
                out_u = list(build)
            if h + 1 < H:
                out_u.append(("kbar", h + 1))
            sched.extend(out_u)

        @blk.tensor
        def _(t):
            t.wait_ge(sv, sv_kvb)
            t.wait_ge(ss, ss_p3)
            for unit in sched:
                kind = unit[0]
                if kind == "kbar":
                    h = unit[1]
                    base = (h % 2) * 2 * LT * 128
                    t.wait_ge(sw, sw_p3 + 32 * min(h + 2, H))
                    if h >= 1:
                        t.wait_ge(ss, ss_p3 + (h - 1) * SPH + 1)
                    if h >= 2:
                        t.wait_ge(sl, (h - 1) * TC * (KT // 2 + 1))
                    ins = None
                    for lt in range(LT):
                        ins = t.matmul(ps_m[:, 0:1], wkv_sb[:, base + lt * 128:base + (lt + 1) * 128],
                                       kvlbar[:, lt:lt + 1], start=(lt == 0), stop=(lt == LT - 1))
                    ins.then_inc(stk, 1)
                elif kind == "K":
                    h, kc = unit[1], unit[2]
                    base = (h % 2) * 2 * LT * 128
                    g = h * KC + kc
                    pk = ps_k[0]
                    if g >= 1:
                        t.wait_ge(svk, g)
                    ins = None
                    for lt in range(LT):
                        ins = t.matmul(pk[:], wkv_sb[:, base + lt * 128:base + (lt + 1) * 128],
                                       kvlT[:, lt * S + kc * 512:lt * S + (kc + 1) * 512],
                                       start=(lt == 0), stop=(lt == LT - 1))
                    ins.then_inc(stk, 1)
                elif kind == "V":
                    h, vq = unit[1], unit[2]
                    base = (h % 2) * 2 * LT * 128
                    gv = h * NVU + vq
                    if gv >= 1:
                        t.wait_ge(ss, ss_p3 + ((gv - 1) // NVU) * SPH + 1 + ((gv - 1) % NVU) + 1)
                    ins = None
                    for j in range(4):
                        tt = vq * 4 + j
                        for lt in range(LT):
                            ins = t.matmul(ps_v[:, j * 128:(j + 1) * 128],
                                           kvlT[:, lt * S + tt * 128:lt * S + (tt + 1) * 128],
                                           wkv_sb[:, base + LT * 128 + lt * 128:base + LT * 128 + (lt + 1) * 128],
                                           start=(lt == 0), stop=(lt == LT - 1))
                    ins.then_inc(stk, 1)
                elif kind == "S":
                    h, qc, kt = unit[1], unit[2], unit[3]
                    hb = h % 2
                    u = h * TC + qc
                    pss = ps_sp[(kt // 2) % 2][:, (kt % 2) * 512:(kt % 2 + 1) * 512]
                    if kt >= 4 and kt % 2 == 0:
                        t.wait_ge(se, u * (KT // 2) + kt // 2 - 1)
                    if kt % 4 == 0:
                        t.wait_ge(svk, h * KC + (kt // 4) + 1)
                    ins = t.matmul(pss, KT_sb[:, hb * S + kt * 128:hb * S + (kt + 1) * 128],
                                   QA[:, h * T + qc * 512:h * T + (qc + 1) * 512],
                                   start=True, stop=True)
                    if kt % 2 == 1:
                        ins.then_inc(sts, 1)
                elif kind == "PV":
                    h, qc, kt = unit[1], unit[2], unit[3]
                    hb = h % 2
                    u = h * TC + qc
                    if kt == 0 and u >= 1:
                        t.wait_ge(sv, sv_p3 + 1 + u)
                    if kt % 2 == 0:
                        t.wait_ge(se, u * (KT // 2) + kt // 2 + 1)
                    if kt % 4 == 0:
                        t.wait_ge(ss, ss_p3 + h * SPH + 1 + (kt // 4) + 1)
                    ins = t.matmul(ps_pv0[:], V_sb[:, hb * S + kt * 128:hb * S + (kt + 1) * 128],
                                   expS[:, (kt % 4) * 512:(kt % 4 + 1) * 512],
                                   start=(kt == 0), stop=(kt == KT - 1))
                    if kt % 2 == 1:
                        ins.then_inc(stp, 1)
                else:
                    h, qc = unit[1], unit[2]
                    u = h * TC + qc
                    t.wait_ge(sl, u * (KT // 2 + 1) + KT // 2)
                    if h + 1 <= H - 1:
                        t.wait_ge(ss, ss_p3 + (h + 1) * SPH + 1)
                    if u >= 1:
                        t.wait_ge(sl, u * (KT // 2 + 1))
                    t.matmul(ps_m[0:1, :], ones128f[:], l_part[:, 0:512], start=True, stop=False)
                    t.matmul(ps_m[0:1, :], ones128f[:], l_part[:, 512:1024], start=False, stop=True).then_inc(stp, 1)
                    t.wait_ge(sb2, u + 1)
                    t.matmul(ps_m[:, 0:512], ones1f[:], sums_s[:], start=True, stop=True).then_inc(stp, 1)

        @blk.scalar
        def _(s):
            s.wait_ge(sf0, 16 * ((NT2B + 1) // 2))
            s.wait_ge(sf1, 16 * (NT2B // 2))
            for unit in sched:
                kind = unit[0]
                if kind == "kbar":
                    h = unit[1]
                    s.wait_ge(stk, h * PH3 + 1)
                    if h >= 1:
                        s.wait_ge(svk, h * KC)
                    s.copy(kbar_f[:], ps_m[:, 0:1]).then_inc(ss, 1)
                elif kind == "V":
                    h, vq = unit[1], unit[2]
                    hb = h % 2
                    s.wait_ge(stk, h * PH3 + 1 + KC + vq + 1)
                    if h >= 2:
                        s.wait_ge(stp, (h - 1) * TC * (KT // 2 + 2))
                    s.copy(V_sb[:, hb * S + vq * 512:hb * S + (vq + 1) * 512], ps_v[:]).then_inc(ss, 1)
                elif kind == "S":
                    h, qc, kt = unit[1], unit[2], unit[3]
                    if kt % 2 == 0:
                        continue
                    u = h * TC + qc
                    j = kt // 2
                    s.wait_ge(sts, u * (KT // 2) + j + 1)
                    if j >= 2:
                        s.wait_ge(stp, u * (KT // 2 + 2) + j - 1)
                        s.wait_ge(sl, u * (KT // 2 + 1) + j - 1)
                    s.activation(expS[:, ((j % 2) * 1024):((j % 2) * 1024 + 1024)],
                                 ps_sp[(j % 2)][:], AF.Exp, scale=1.0).then_inc(se, 1)
                elif kind == "tail":
                    h, qc = unit[1], unit[2]
                    u = h * TC + qc
                    s.wait_ge(stp, u * (KT // 2 + 2) + KT // 2 + 1)
                    if u >= 1:
                        s.wait_ge(stp, (u - 1) * (KT // 2 + 2) + KT // 2 + 2)
                    s.copy(sums_s[:], ps_m[0:1, :]).then_inc(sb2, 1)

        @blk.vector
        def _(v):
            v.wait_ge(sf0, 16 * ((NT2B + 1) // 2))
            v.wait_ge(sf1, 16 * (NT2B // 2))
            for unit in sched:
                kind = unit[0]
                if kind == "K":
                    h, kc = unit[1], unit[2]
                    hb = h % 2
                    g = h * KC + kc
                    pk = ps_k[0]
                    v.wait_ge(stk, h * PH3 + 1 + kc + 1)
                    if g > 0:
                        v.wait_ge(svk, g)
                    if kc == 0:
                        v.wait_ge(ss, ss_p3 + h * SPH + 1)
                        if h >= 2:
                            v.wait_ge(sts, (h - 1) * TC * (KT // 2))
                    vjoin(v)
                    vchain(v, v.tensor_mul(krA[:], pk[:], Tck[:, kc * 512:(kc + 1) * 512]))
                    vchain(v, v.tensor_mul(krB[0:64, :], pk[64:128, :], Tsk[0:64, kc * 512:(kc + 1) * 512]))
                    vchain(v, v.tensor_mul(krB[64:128, :], pk[0:64, :], Tsk[64:128, kc * 512:(kc + 1) * 512]))
                    vjoin(v)
                    v.scalar_tensor_tensor(KT_sb[:, hb * S + kc * 512:hb * S + (kc + 1) * 512],
                                           krA[:], kbar_f[:, 0:1], krB[:],
                                           ALU.subtract, ALU.add).then_inc(svk, 1)
                elif kind == "S":
                    h, qc, kt = unit[1], unit[2], unit[3]
                    if kt % 2 == 0:
                        continue
                    u = h * TC + qc
                    j = kt // 2
                    v.wait_ge(se, u * (KT // 2) + j + 1)
                    exs = expS[:, (j % 2) * 1024:(j % 2) * 1024 + 1024]
                    if j == 0:
                        if u >= 1:
                            v.wait_ge(stp, (u - 1) * (KT // 2 + 2) + KT // 2 + 1)
                        v.tensor_scalar(l_part[:], exs, 1.0, None, ALU.mult).then_inc(sl, 1)
                    else:
                        v.wait_ge(sl, u * (KT // 2 + 1) + j)
                        v.tensor_add(l_part[:], l_part[:], exs).then_inc(sl, 1)
                elif kind == "tail":
                    h, qc = unit[1], unit[2]
                    u = h * TC + qc
                    v.wait_ge(stp, u * (KT // 2 + 2) + KT // 2 + 2)
                    if u >= 2:
                        v.wait_ge(sv, sv_p3 + 1 + (u - 2) + 1)
                    v.reciprocal(rbc[:, (u % 2) * 512:(u % 2 + 1) * 512],
                                 ps_m[:, 0:512]).then_inc(sl, 1)
                    v.wait_ge(sl, u * (KT // 2 + 1) + KT // 2 + 1)
                    v.tensor_mul(QA[:, h * T + qc * 512:h * T + (qc + 1) * 512],
                                 ps_pv0[:], rbc[:, (u % 2) * 512:(u % 2 + 1) * 512]).then_inc(sv, 1)
        SV += H * TC
        sv_attn = SV

        c3.close()

        c4 = es.enter_context(contextlib.ExitStack())
        wo_sb = sb("wo_sb", [128, 2 * DT * DOC], BF16, c4)
        xres = sb("xres", [128, 2 * DOC], F32, c4)
        fres = sb("fres", [128, 2 * DOC], BF16, c4)
        ostage = sb("ostage", [128, 2 * DOC], F32, c4)
        ps_o = [psm(f"ps_o{i}", [128, DOC], F32, c4) for i in range(2)]

        sv_p4, sw_p4, st_p4, sx_p4 = SV, SW, ST, SX

        @blk.gpsimd
        def _(gp):
            gp.wait_ge(sv, sv_attn)
            gp.wait_ge(svk, H * KC)
            for dc in range(NDC):
                if dc >= 2:
                    gp.wait_ge(st, st_p4 + (dc - 1) * T8)
                gp.dma_start(
                    out=wo_sb[:, (dc % 2) * DT * DOC:(dc % 2 + 1) * DT * DOC],
                    in_=wo[dc],
                ).then_inc(sw, 16)
        SW += 16 * NDC

        @blk.sync
        def _(sy):
            sy.wait_ge(sv, sv_attn)
            for dc in range(NDC):
                for i in range(T8):
                    k = dc * T8 + i
                    if k >= 2:
                        sy.wait_ge(sv, sv_p4 + k - 1)
                    sy.dma_start(out=xres[:, (k % 2) * DOC:(k % 2 + 1) * DOC],
                                 in_=x_own[i * 128:(i + 1) * 128, dc * DOC:(dc + 1) * DOC]).then_inc(sx, 16)
                    sy.dma_start(out=fres[:, (k % 2) * DOC:(k % 2 + 1) * DOC],
                                 in_=ffn_hbm.ap()[i * 128:(i + 1) * 128, dc * DOC:(dc + 1) * DOC]).then_inc(sx, 16)
        SX += 32 * NDC * T8

        @blk.tensor
        def _(t):
            t.wait_ge(sv, sv_attn)
            for dc in range(NDC):
                t.wait_ge(sw, sw_p4 + 16 * min(dc + 2, NDC))
                for i in range(T8):
                    k = dc * T8 + i
                    po = ps_o[k % 2]
                    if k >= 2:
                        t.wait_ge(sv, sv_p4 + k - 1)
                    ins = None
                    for h in range(DT):
                        o = (dc % 2) * DT * DOC + h * DOC
                        ins = t.matmul(po[:], QA[:, h * T + i * 128:h * T + i * 128 + 128],
                                       wo_sb[:, o:o + DOC], start=(h == 0), stop=(h == DT - 1))
                    ins.then_inc(st, 1)
        ST += NDC * T8

        @blk.vector
        def _(v):
            for k in range(NDC * T8):
                v.wait_ge(st, st_p4 + k + 1)
                v.wait_ge(sx, sx_p4 + 32 * min(k + 2, NDC * T8))
                if k >= 2:
                    v.wait_ge([so, so1][k % 2], 16 * (k // 2))
                osl = ostage[:, (k % 2) * DOC:(k % 2 + 1) * DOC]
                vjoin(v)
                vchain(v, v.tensor_add(osl, ps_o[k % 2][:], xres[:, (k % 2) * DOC:(k % 2 + 1) * DOC]))
                vjoin(v)
                v.tensor_add(osl, osl, fres[:, (k % 2) * DOC:(k % 2 + 1) * DOC]).then_inc(sv, 1)
        SV += NDC * T8

        @blk.scalar
        def _(s):
            for dc in range(NDC):
                for i in range(T8):
                    k = dc * T8 + i
                    s.wait_ge(sv, sv_p4 + k + 1)
                    s.dma_start(out=out[i * 128:(i + 1) * 128, dc * DOC:(dc + 1) * DOC],
                                in_=ostage[:, (k % 2) * DOC:(k % 2 + 1) * DOC]).then_inc([so, so1][k % 2], 16)
            s.wait_ge(so, 16 * ((NDC * T8 + 1) // 2))
            s.wait_ge(so1, 16 * ((NDC * T8) // 2))

        c4.close()

    return nc


def _host_layouts(inputs, cfg: CFG):
    D, T, S, ER = cfg.D, cfg.T, cfg.S, cfg.ER
    f32 = np.float32
    W_gate = np.asarray(inputs["L0_ffn_W_gate"])
    W_thr = np.asarray(inputs["L0_ffn_W_thr"])
    A_w = np.asarray(inputs["L0_ffn_A_w"])
    B_w = np.asarray(inputs["L0_ffn_B_w"])
    DT, LT, H, NDC, DOC, SM = cfg.DT, cfg.LT, cfg.H, cfg.NDC, cfg.DOC, cfg.SM

    def tile_din(w, np_, cw):
        return np.ascontiguousarray(
            np.asarray(w, f32).reshape(DT, 128, np_, cw).transpose(2, 1, 0, 3).reshape(np_, 128, DT * cw))

    def tile_lat(w, nh):
        return np.ascontiguousarray(
            np.asarray(w, f32).reshape(LT, 128, nh, 128).transpose(2, 1, 0, 3).reshape(nh, 128, LT * 128))

    wsm_flat = np.concatenate(
        [W_gate.T, W_thr[:, None], np.transpose(A_w, (2, 0, 1)).reshape(D, ER)], axis=1)
    common = {
        "wq": tile_din(inputs["L0_attn_Wq"], H // 2, 256),
        "wc": np.ascontiguousarray(inputs["L0_attn_Wc"], f32),
        "wk": tile_lat(inputs["L0_attn_Wk"], H),
        "wv": tile_lat(inputs["L0_attn_Wv"], H),
        "wo": tile_din(inputs["L0_attn_Wo"], NDC, DOC),
        "wbt": tile_din(np.asarray(inputs["L0_ffn_W_base"]).T, NDC, DOC),
        "wsm": np.ascontiguousarray(
            np.asarray(wsm_flat, f32).reshape(DT, 128, SM).transpose(1, 0, 2).reshape(128, DT * SM)),
        "bflat": np.ascontiguousarray(np.transpose(B_w, (0, 2, 1)).reshape(ER, D), f32),
        "bbase": np.ascontiguousarray(np.asarray(inputs["L0_ffn_b_base"], f32)[None, :]),
        "bthr": np.ascontiguousarray(np.broadcast_to(
            np.asarray(inputs["L0_ffn_b_thr"], f32).reshape(1, 1), (128, 1))),
        "posk": np.ascontiguousarray(np.broadcast_to(np.arange(S, dtype=f32)[None, :], (128, S))),
        "dind": np.ascontiguousarray((np.arange(128, dtype=f32) % 64)[:, None]),
        "ident": np.eye(128, dtype=f32),
    }
    x = np.asarray(inputs["L0_attn_hidden_states"], f32)
    in_maps = []
    for c in range(8):
        b, qq = c // 4, c % 4
        m = dict(common)
        m["x_own"] = np.ascontiguousarray(x[b, qq * T:(qq + 1) * T, :])
        m["posq"] = np.ascontiguousarray(np.broadcast_to(
            np.arange(qq * T, (qq + 1) * T, dtype=f32)[None, :], (128, T)))
        in_maps.append(m)
    return in_maps


def kernel(**inputs):
    cfg = CFG()
    nc = build_program(cfg)
    in_maps = _host_layouts(inputs, cfg)
    res = run_bass_kernel_spmd(nc, in_maps, core_ids=list(range(8)))
    out = np.empty((2, 4 * cfg.T, cfg.D), dtype=np.float32)
    for c in range(8):
        b, qq = c // 4, c % 4
        out[b, qq * cfg.T:(qq + 1) * cfg.T, :] = res.results[c]["out"]
    return out
